# version 09
# baseline (speedup 1.0000x reference)
import math
from contextlib import ExitStack

import numpy as np
import ml_dtypes

import concourse.bass as bass
import concourse.mybir as mybir
from concourse.bass_utils import run_bass_kernel_spmd

F32 = mybir.dt.float32
BF16 = mybir.dt.bfloat16
I32 = mybir.dt.int32
AF = mybir.ActivationFunctionType
ALU = mybir.AluOpType

S_TOK = 4096
D = 2048
NT = S_TOK // 128
DK = D // 128

D_INNER = 4096
NHEADS = 64
HEADDIM = 64
D_STATE = 128
D_IN = 8640
D_IN_PAD = 8704
INTER = 8192

MAGIC = float(1.5 * 2 ** 23)
TWO_PI = float(2 * np.pi)


def hadamard(m):
    H = np.array([[1.0]], dtype=np.float64)
    while H.shape[0] < m:
        H = np.block([[H, H], [H, -H]])
    return H


class Sem:
    def __init__(self, ctx, name):
        self.h = ctx.stack.enter_context(ctx.nc.semaphore(name))
        self.n = 0

    def inc(self, instr, k=1):
        instr.then_inc(self.h, k)
        self.n += k
        return self.n

    def wait(self, eng, val=None):
        eng.wait_ge(self.h, self.n if val is None else val)


class Ctx:
    def __init__(self, nc, stack):
        self.nc = nc
        self.stack = stack

    def sem(self, name):
        return Sem(self, name)

    def sb(self, name, shape, dtype, **kw):
        return self.stack.enter_context(self.nc.sbuf_tensor(name, shape, dtype, **kw))

    def ps(self, name, shape, dtype=F32):
        return self.stack.enter_context(self.nc.psum_tensor(name, shape, dtype))


def host_consts():
    c = {}
    c["ident"] = np.eye(128).astype(ml_dtypes.bfloat16)
    H128 = hadamard(128)
    c["h128_2048"] = (H128 / (math.sqrt(2048.0) * 64.0)).astype(ml_dtypes.bfloat16)
    c["h128_8192"] = (H128 / (math.sqrt(8192.0) * 64.0)).astype(ml_dtypes.bfloat16)
    H16 = hadamard(16)
    H64 = hadamard(64)
    c["i8h16"] = np.kron(np.eye(8), H16).astype(ml_dtypes.bfloat16)
    c["i2h64"] = np.kron(np.eye(2), H64).astype(ml_dtypes.bfloat16)
    c["h16i8"] = np.kron(H16, np.eye(8)).astype(ml_dtypes.bfloat16)
    c["h64i2"] = np.kron(H64, np.eye(2)).astype(ml_dtypes.bfloat16)
    c["utri_f32"] = np.triu(np.ones((128, 128))).astype(np.float32)
    c["ones_col_f32"] = np.ones((128, 1), dtype=np.float32)
    c["ones_col_bf"] = np.ones((128, 1), dtype=ml_dtypes.bfloat16)
    c["ones_row_bf"] = np.ones((1, 128), dtype=ml_dtypes.bfloat16)
    c["ones_row_f32"] = np.ones((1, 128), dtype=np.float32)
    c["slow_bf"] = (np.tril(np.ones((128, 128)), -1).T > 0).astype(ml_dtypes.bfloat16)
    c["slower_bf"] = np.tril(np.ones((128, 128)), -1).astype(ml_dtypes.bfloat16)
    c["triu_bf"] = np.triu(np.ones((128, 128))).astype(ml_dtypes.bfloat16)
    c["trius_bf"] = np.triu(np.ones((128, 128)), 1).astype(ml_dtypes.bfloat16)
    qs = np.zeros((128, 4 * 512), dtype=np.float64)
    for o in range(4):
        s_ = np.arange(128)[:, None] + 128 * o
        q_ = np.arange(512)[None, :]
        qs[:, o * 512:(o + 1) * 512] = (s_ <= q_)
    c["qsmask"] = qs.astype(ml_dtypes.bfloat16)
    sc = [MAGIC, -MAGIC, TWO_PI, -TWO_PI, np.pi, np.pi / 2, 1.0, -1.0, 0.5, -0.5,
          1.0 / TWO_PI, 1e-5, 1.0 / 64.0, 64.0, 0.25]
    c["csts"] = np.tile(np.array([sc], dtype=np.float32), (128, 1))
    return c


CST = {name: i for i, name in enumerate(
    ["MAGIC", "NMAGIC", "TWO_PI", "NTWO_PI", "PI", "HALF_PI", "ONE", "NONE",
     "HALF", "NHALF", "INV_TWO_PI", "EPS", "INV64", "C64", "QUARTER"])}


def declare_const_drams(nc):
    drams = {}
    for name, arr in host_consts().items():
        dt = BF16 if arr.dtype == ml_dtypes.bfloat16 else F32
        drams[name] = nc.dram_tensor("c_" + name, list(arr.shape), dt, kind="ExternalInput").ap()
    return drams


_CONST_SEQ = [0]


def load_consts_scoped(stack, nc, g, lsem, cdr, keys):
    arrs = host_consts()
    t = {}
    _CONST_SEQ[0] += 1
    for name in keys:
        arr = arrs[name]
        dt = BF16 if arr.dtype == ml_dtypes.bfloat16 else F32
        tile = stack.enter_context(nc.sbuf_tensor(f"sb{_CONST_SEQ[0]}_{name}", list(arr.shape), dt))
        lsem.inc(g.dma_start(out=tile[:], in_=cdr[name]), 16)
        t[name] = tile
    return t


def load_consts(ctx, g, lsem, keys=None):
    nc = ctx.nc
    cdr = declare_const_drams(nc)
    if keys is None:
        keys = list(host_consts().keys())
    return load_consts_scoped(ctx.stack, nc, g, lsem, cdr, keys)


def cst(consts, name):
    i = CST[name]
    return consts["csts"][:, i:i + 1]


def phase1(ctx, io, consts, eng, entry=None):
    nc, t, v, s, g, sy = ctx.nc, eng["t"], eng["v"], eng["s"], eng["g"], eng["sy"]
    with ExitStack() as ph:
        E = ph.enter_context
        if entry:
            for e in (t, v, s, g, sy):
                for key, (sem, val) in entry.items():
                    e.wait_ge(sem.h, val)
        xT = E(nc.sbuf_tensor("p1_xT", [128, DK * S_TOK], BF16, is_transpose_dest=True))
        uch = [E(nc.sbuf_tensor(f"p1_uch{i}", [128, D], BF16)) for i in range(2)]
        wnat = [E(nc.sbuf_tensor(f"p1_wnat{i}", [128, 4 * D], BF16)) for i in range(1)]
        wT = [E(nc.sbuf_tensor(f"p1_wT{i}", [128, 16 * 512], BF16, is_transpose_dest=True)) for i in range(2)]
        ot = [E(nc.sbuf_tensor(f"p1_ot{i}", [128, 512], BF16)) for i in range(8)]
        dlt_t = [E(nc.sbuf_tensor(f"p1_dlt{i}", [128, 192], F32)) for i in range(2)]
        mps = [E(nc.psum_tensor(f"p1_mp{i}", [128, 512], F32)) for i in range(4)]

        ld = ctx.sem("p1_ld")
        sq = ctx.sem("p1_sq")
        tev = ctx.sem("p1_tev")
        mmd = ctx.sem("p1_mmd")
        gev = ctx.sem("p1_gev")
        st = ctx.sem("p1_st")

        ident = consts["ident"]

        uch_busy = [0] * 2
        mps_busy = [0] * 4
        ot_busy = [0] * 8
        dlt_busy = [0] * 2
        wnat_busy = [0] * 1
        xT_view = lambda kt, ti: xT[:, kt * S_TOK + ti * 128: kt * S_TOK + (ti + 1) * 128]

        def do_transpose(src_ap, dst_ap, pi):
            sq.inc(sy.dma_start_transpose(dst_ap, src_ap), 16)

        for i in range(NT):
            b = uch[i % 2]
            if uch_busy[i % 2]:
                g.wait_ge(sq.h, uch_busy[i % 2])
            ld.inc(g.dma_start(out=b[:], in_=io["u"][i * 128:(i + 1) * 128, :]), 16)
            sy.wait_ge(ld.h, ld.n)
            for kt in range(DK):
                do_transpose(b[:, kt * 128:(kt + 1) * 128], xT_view(kt, i), (i * DK + kt) % 4)
            uch_busy[i % 2] = sq.n

        n_oc = D_IN_PAD // 512
        for oc in range(n_oc):
            par = 0
            wn = wnat[par]
            wt = wT[par]
            rows0 = oc * 512
            rows1 = min(rows0 + 512, D_IN)
            nrows = rows1 - rows0
            nfull = nrows // 128
            if wnat_busy[par]:
                g.wait_ge(sq.h, wnat_busy[par])
            ld.inc(g.dma_start(
                out=wn[:, 0:nfull * D].rearrange("p (a d) -> p a d", a=nfull),
                in_=io["w_in"][rows0:rows0 + nfull * 128, :].rearrange("(a p) d -> p a d", p=128)), 16)
            if nrows < 512:
                rem = nrows - nfull * 128
                ld.inc(g.dma_start(out=wn[0:rem, nfull * D:(nfull + 1) * D],
                                   in_=io["w_in"][rows0 + nfull * 128:rows1, :]), 16)
                v.wait_ge(ld.h, ld.n)
                tev.inc(v.memset(wn[64:128, nfull * D:(nfull + 1) * D], 0.0), 1)
            sy.wait_ge(ld.h, ld.n)
            if nrows < 512:
                sy.wait_ge(tev.h, tev.n)
            for a in range(4):
                for kt in range(DK):
                    do_transpose(wn[:, a * D + kt * 128: a * D + (kt + 1) * 128],
                                 wt[:, kt * 512 + a * 128: kt * 512 + (a + 1) * 128],
                                 (a * DK + kt) % 4)
            wnat_busy[par] = sq.n
            t.wait_ge(sq.h, sq.n)
            for ti in range(NT):
                pi = ti % 4
                mp = mps[pi]
                if mps_busy[pi]:
                    t.wait_ge(gev.h, mps_busy[pi])
                ins = None
                for kt in range(DK):
                    ins = t.matmul(mp[:], xT_view(kt, ti), wt[:, kt * 512:(kt + 1) * 512],
                                   start=(kt == 0), stop=(kt == DK - 1))
                mmd.inc(ins, 1)
                o = ot[ti % 8]
                v.wait_ge(mmd.h, mmd.n)
                if ot_busy[ti % 8]:
                    v.wait_ge(st.h, ot_busy[ti % 8])
                gev.inc(v.tensor_copy(o[:], mp[:]), 1)
                if oc == 16:
                    dl = dlt_t[ti % 2]
                    if dlt_busy[ti % 2]:
                        v.wait_ge(st.h, dlt_busy[ti % 2])
                    gev.inc(v.tensor_copy(dl[:], mp[:, 256:448]), 1)
                    mps_busy[pi] = gev.n
                    g.wait_ge(gev.h, gev.n)
                    st.inc(g.dma_start(out=io["zx"][ti * 128:(ti + 1) * 128, oc * 512:(oc + 1) * 512], in_=o[:]), 16)
                    ot_busy[ti % 8] = st.n
                    st.inc(g.dma_start(out=io["dlt"][ti * 128:(ti + 1) * 128, :], in_=dl[:]), 16)
                    dlt_busy[ti % 2] = st.n
                else:
                    mps_busy[pi] = gev.n
                    g.wait_ge(gev.h, gev.n)
                    st.inc(g.dma_start(out=io["zx"][ti * 128:(ti + 1) * 128, oc * 512:(oc + 1) * 512], in_=o[:]), 16)
                    ot_busy[ti % 8] = st.n
        g.wait_ge(st.h, st.n)
        sy.wait_ge(st.h, st.n)
        return {"gev": (gev, gev.n), "sq": (sq, sq.n), "mmd": (mmd, mmd.n), "st": (st, st.n)}


def build_p1(core_inputs):
    nc = bass.Bass("TRN2", target_bir_lowering=False)
    u = nc.dram_tensor("u", [S_TOK, D], F32, kind="ExternalInput").ap()
    w_in = nc.dram_tensor("w_in", [D_IN, D], F32, kind="ExternalInput").ap()
    zx = nc.dram_tensor("zx", [S_TOK, D_IN_PAD], BF16, kind="ExternalOutput").ap()
    dlt = nc.dram_tensor("dlt", [S_TOK, 192], F32, kind="ExternalOutput").ap()
    with ExitStack() as stack:
        ctx = Ctx(nc, stack)
        eng = {"t": nc.tensor, "v": nc.vector, "s": nc.scalar, "g": nc.gpsimd, "sy": nc.sync}
        ld0 = ctx.sem("c_ld")
        consts = load_consts(ctx, nc.gpsimd, ld0, keys=("ident", "csts"))
        nc.tensor.wait_ge(ld0.h, ld0.n)
        nc.vector.wait_ge(ld0.h, ld0.n)
        phase1(ctx, {"u": u, "w_in": w_in, "zx": zx, "dlt": dlt}, consts, eng)
    return nc


def rep_free(ap2d, reps, inner):
    return ap2d.unsqueeze(1).broadcast_to([ap2d.shape[0], reps, inner])


def rep_inner(ap2d, h, p2):
    return ap2d.unsqueeze(2).broadcast_to([ap2d.shape[0], h, p2])


def emit_rstd(s, consts, vcol, out, scale, tmp):
    s.activation(tmp, vcol, AF.Ln, bias=cst(consts, "EPS"), scale=scale)
    s.mul(tmp, tmp, -0.5)
    return s.activation(out, tmp, AF.Exp)


def emit_rsqrt(v, col, tmp, tmp2):
    MAGICI = 0x5F3759DF
    ci = tmp.bitcast(I32)
    v.tensor_scalar(ci, col.bitcast(I32), 1, None, ALU.arith_shift_right)
    v.tensor_scalar(ci, ci, -1, MAGICI, ALU.mult, ALU.add)
    for _ in range(3):
        v.tensor_tensor(tmp2, tmp, tmp, ALU.mult)
        v.tensor_tensor(tmp2, tmp2, col, ALU.mult)
        v.tensor_scalar(tmp2, tmp2, -0.5, 1.5, ALU.mult, ALU.add)
        v.tensor_tensor(tmp, tmp, tmp2, ALU.mult)
    v.tensor_copy(col, tmp)


def phase2(ctx, io, consts, eng, upto=None, dump_c0=False, entry=None):
    nc, t, v, s, g, sy = ctx.nc, eng["t"], eng["v"], eng["s"], eng["g"], eng["sy"]
    NH = NHEADS
    HG = 8
    ident = consts["ident"]
    with ExitStack() as ph:
        E = ph.enter_context
        if entry:
            for e in (t, v, s, g, sy):
                for key, (sem, val) in entry.items():
                    e.wait_ge(sem.h, val)
        ld = ctx.sem("p2_ld")
        tT = ctx.sem("p2_tT")
        vv = ctx.sem("p2_vv")
        ss = ctx.sem("p2_ss")
        sq = ctx.sem("p2_sq")
        st = ctx.sem("p2_st")

        GF = [128, 32 * 64]
        A_f = E(nc.sbuf_tensor("p2_A", GF, F32))
        A_bf = E(nc.sbuf_tensor("p2_Abf", GF, BF16))
        g_f = E(nc.sbuf_tensor("p2_g", GF, F32))
        g1_f = E(nc.sbuf_tensor("p2_g1", GF, F32))
        cosg = E(nc.sbuf_tensor("p2_cos", GF, BF16))
        sing = E(nc.sbuf_tensor("p2_sin", GF, BF16))
        D_bc = E(nc.sbuf_tensor("p2_Dbc", [128, 64], F32))
        wB_bc = E(nc.sbuf_tensor("p2_wBbc", [128, 128], BF16))
        wC_bc = E(nc.sbuf_tensor("p2_wCbc", [128, 128], BF16))
        pp = ExitStack()
        PE = pp.enter_context
        g2p_f = PE(nc.sbuf_tensor("p2_g2p", GF, F32))
        dtr = PE(nc.sbuf_tensor("p2_dtr", GF, F32))
        lamr = PE(nc.sbuf_tensor("p2_lamr", GF, F32))
        theta = PE(nc.sbuf_tensor("p2_theta", GF, F32))
        wk1 = PE(nc.sbuf_tensor("p2_wk1", GF, F32))
        wk2 = PE(nc.sbuf_tensor("p2_wk2", GF, F32))
        wk3 = PE(nc.sbuf_tensor("p2_wk3", GF, F32))
        rows = PE(nc.sbuf_tensor("p2_rows", [1, 512], F32))
        brow = PE(nc.sbuf_tensor("p2_brow", [1, 256], F32))
        dtb_bc = PE(nc.sbuf_tensor("p2_dtbbc", [128, 64], F32))
        nea_bc = PE(nc.sbuf_tensor("p2_neabc", [128, 64], F32))
        tsums = PE(nc.sbuf_tensor("p2_tsums", [1, 32 * 64], F32))
        carr = PE(nc.sbuf_tensor("p2_carr", [1, 33 * 64], F32))

        ps_a = E(nc.psum_tensor("p2_psa", [128, 64], F32))
        ps_b = E(nc.psum_tensor("p2_psb", [128, 128], F32))

        grid = lambda buf, i: buf[:, i * 64:(i + 1) * 64]

        ld.inc(g.dma_start(out=dtr[:].rearrange("p (i c) -> p i c", c=64),
                           in_=io["dlt"][:, 0:64].rearrange("(i p) c -> p i c", p=128)), 16)
        ld.inc(g.dma_start(out=lamr[:].rearrange("p (i c) -> p i c", c=64),
                           in_=io["dlt"][:, 64:128].rearrange("(i p) c -> p i c", p=128)), 16)
        ld.inc(g.dma_start(out=theta[:].rearrange("p (i c) -> p i c", c=64),
                           in_=io["dlt"][:, 128:192].rearrange("(i p) c -> p i c", p=128)), 16)
        ld.inc(g.dma_start(out=rows[:, 0:64], in_=io["alog"]), 16)
        ld.inc(g.dma_start(out=rows[:, 64:128], in_=io["dtb"]), 16)
        ld.inc(g.dma_start(out=rows[:, 128:192], in_=io["dcoef"]), 16)
        ld.inc(g.dma_start(out=brow[:, 0:128], in_=io["bnw"]), 16)
        ld.inc(g.dma_start(out=brow[:, 128:256], in_=io["cnw"]), 16)

        s.wait_ge(ld.h, ld.n)
        v.wait_ge(ld.h, ld.n)
        t.wait_ge(ld.h, ld.n)
        ss.inc(s.activation(rows[:, 192:256], rows[:, 0:64], AF.Exp), 1)
        v.wait_ge(ss.h, ss.n)
        vv.inc(v.tensor_scalar(rows[:, 256:320], rows[:, 192:256], -1.0, None, ALU.mult), 1)
        t.wait_ge(vv.h, vv.n)
        tT.inc(t.matmul(ps_a[:], consts["ones_row_f32"][:], rows[:, 64:128], start=True, stop=True), 1)
        v.wait_ge(tT.h, tT.n)
        vv.inc(v.tensor_copy(dtb_bc[:], ps_a[:]), 1)
        t.wait_ge(vv.h, vv.n)
        tT.inc(t.matmul(ps_a[:], consts["ones_row_f32"][:], rows[:, 256:320], start=True, stop=True), 1)
        v.wait_ge(tT.h, tT.n)
        vv.inc(v.tensor_copy(nea_bc[:], ps_a[:]), 1)
        t.wait_ge(vv.h, vv.n)
        tT.inc(t.matmul(ps_a[:], consts["ones_row_f32"][:], rows[:, 128:192], start=True, stop=True), 1)
        v.wait_ge(tT.h, tT.n)
        vv.inc(v.tensor_copy(D_bc[:], ps_a[:]), 1)
        tT.inc(t.matmul(ps_b[:], consts["ones_row_f32"][:], brow[:, 0:128], start=True, stop=True), 1)
        v.wait_ge(tT.h, tT.n)
        vv.inc(v.tensor_copy(wB_bc[:], ps_b[:]), 1)
        t.wait_ge(vv.h, vv.n)
        tT.inc(t.matmul(ps_b[:], consts["ones_row_f32"][:], brow[:, 128:256], start=True, stop=True), 1)
        v.wait_ge(tT.h, tT.n)
        vv.inc(v.tensor_copy(wC_bc[:], ps_b[:]), 1)

        NTI = 32
        G3 = lambda buf: buf[:].rearrange("p (i c) -> p i c", c=64)
        dtbrep = rep_free(dtb_bc[:], NTI, 64)
        nearep = rep_free(nea_bc[:], NTI, 64)
        v.tensor_tensor(G3(wk1), G3(dtr), dtbrep, ALU.add)
        vv.inc(v.scalar_tensor_tensor(wk2[:], wk1[:], -1.0, wk1[:], ALU.mult, ALU.max), 1)
        s.wait_ge(vv.h, vv.n)
        ss.inc(s.activation(wk2[:], wk2[:], AF.Exp, scale=-1.0), 1)
        ss.inc(s.activation(wk2[:], wk2[:], AF.Ln, bias=1.0), 1)
        v.wait_ge(ss.h, ss.n)
        v.tensor_scalar(wk3[:], wk1[:], 0.0, None, ALU.max)
        v.tensor_tensor(dtr[:], wk3[:], wk2[:], ALU.add)
        v.tensor_tensor(G3(A_f), G3(dtr), nearep, ALU.mult)
        vv.inc(v.tensor_copy(A_bf[:], A_f[:]), 1)
        if upto == "pre00":
            G3c = lambda buf: buf[:].rearrange("p (i c) -> p i c", c=64)
            g.wait_ge(ss.h, ss.n)
            g.wait_ge(vv.h, vv.n)
            for k, buf in enumerate((dtr, wk1, wk2, A_f)):
                st.inc(g.dma_start(out=io["dbg"][:, k * 64:(k + 1) * 64].rearrange("(i p) c -> p i c", p=128),
                                   in_=G3c(buf)), 16)
            g.wait_ge(st.h, st.n)
            sy.wait_ge(st.h, st.n)
            pp.close()
            return

        s.wait_ge(vv.h, vv.n)
        ss.inc(s.activation(lamr[:], lamr[:], AF.Sigmoid), 1)
        v.wait_ge(ss.h, ss.n)
        v.tensor_tensor(wk1[:], dtr[:], lamr[:], ALU.mult)
        v.tensor_copy(g1_f[:], wk1[:])
        vv.inc(v.tensor_tensor(wk2[:], dtr[:], wk1[:], ALU.subtract), 1)
        g.wait_ge(vv.h, vv.n)
        ld.inc(g.dma_start(out=io["adram"][:].rearrange("(i p) c -> p i c", p=128), in_=G3(A_f)), 16)
        g.wait_ge(ld.h, ld.n)
        ld.inc(g.dma_start(out=io["ashdram"][0:S_TOK - 1, :], in_=io["adram"][1:S_TOK, :]), 16)
        ld.inc(g.dma_start(out=io["ashdram"][S_TOK - 1:S_TOK, :], in_=io["adram"][S_TOK - 1:S_TOK, :]), 16)
        g.wait_ge(ld.h, ld.n)
        ld.inc(g.dma_start(out=G3(wk3), in_=io["ashdram"][:].rearrange("(i p) c -> p i c", p=128)), 16)
        s.wait_ge(ld.h, ld.n)
        ss.inc(s.activation(wk3[:], wk3[:], AF.Exp, scale=-1.0), 1)
        v.wait_ge(ss.h, ss.n)
        v.tensor_tensor(g2p_f[:], wk2[:], wk3[:], ALU.mult)
        vv.inc(v.tensor_tensor(g_f[:], wk1[:], g2p_f[:], ALU.add), 1)

        if upto == "pre0":
            G3c = lambda buf: buf[:].rearrange("p (i c) -> p i c", c=64)
            g.wait_ge(ss.h, ss.n)
            g.wait_ge(vv.h, vv.n)
            for k, buf in enumerate((A_f, g_f, g2p_f)):
                st.inc(g.dma_start(out=io["dbg"][:, k * 64:(k + 1) * 64].rearrange("(i p) c -> p i c", p=128),
                                   in_=G3c(buf)), 16)
            g.wait_ge(st.h, st.n)
            sy.wait_ge(st.h, st.n)
            pp.close()
            return

        for i in range(NTI):
            t.wait_ge(vv.h, vv.n)
            tT.inc(t.matmul(ps_a[0:1, 0:64], consts["ones_col_f32"][:], grid(theta, i),
                            start=True, stop=True), 1)
            v.wait_ge(tT.h, tT.n)
            vv.inc(v.tensor_copy(tsums[:, i * 64:(i + 1) * 64], ps_a[0:1, 0:64]), 1)
        v.tensor_copy(carr[:, 64:128], tsums[:, 0:64])
        last_carr = None
        for i in range(1, NTI):
            last_carr = v.tensor_tensor(carr[:, (i + 1) * 64:(i + 2) * 64], carr[:, i * 64:(i + 1) * 64],
                                        tsums[:, i * 64:(i + 1) * 64], ALU.add)
        vv.inc(last_carr, 1)
        if upto == "ang":
            g.wait_ge(vv.h, vv.n)
            st.inc(g.dma_start(out=io["dbg2"][0:1, 0:2048], in_=tsums[:]), 16)
            st.inc(g.dma_start(out=io["dbg2"][0:1, 2048:2048 + 33 * 64], in_=carr[:]), 16)
            g.wait_ge(st.h, st.n)
            sy.wait_ge(st.h, st.n)
            pp.close()
            return

        t.wait_ge(vv.h, vv.n)
        for i in range(NTI):
            t.wait_ge(vv.h, vv.n)
            if i == 0:
                tT.inc(t.matmul(ps_a[:], consts["utri_f32"][:], grid(theta, i),
                                start=True, stop=True), 1)
            else:
                t.matmul(ps_a[:], consts["utri_f32"][:], grid(theta, i), start=True, stop=False,
                         skip_group_check=True)
                tT.inc(t.matmul(ps_a[:], consts["ones_row_f32"][:], carr[:, i * 64:(i + 1) * 64],
                                start=False, stop=True, skip_group_check=True), 1)
            v.wait_ge(tT.h, tT.n)
            vv.inc(v.tensor_copy(grid(wk1, i), ps_a[:]), 1)
        INV2PI = 1.0 / TWO_PI
        v.tensor_scalar(wk2[:], wk1[:], INV2PI, MAGIC, ALU.mult, ALU.add)
        v.tensor_scalar(wk2[:], wk2[:], -MAGIC, None, ALU.add)
        v.scalar_tensor_tensor(wk3[:], wk2[:], -TWO_PI, wk1[:], ALU.mult, ALU.add)
        v.tensor_scalar(wk2[:], wk1[:], INV2PI, 0.25, ALU.mult, ALU.add)
        v.tensor_scalar(wk2[:], wk2[:], MAGIC, None, ALU.add)
        v.tensor_scalar(wk2[:], wk2[:], -MAGIC, None, ALU.add)
        v.scalar_tensor_tensor(wk2[:], wk2[:], -TWO_PI, wk1[:], ALU.mult, ALU.add)
        vv.inc(v.tensor_scalar(wk2[:], wk2[:], float(np.pi / 2), None, ALU.add), 1)
        s.wait_ge(vv.h, vv.n)
        ss.inc(s.activation(sing[:], wk3[:], AF.Sin), 1)
        ss.inc(s.activation(cosg[:], wk2[:], AF.Sin), 1)
        v.wait_ge(ss.h, ss.n)
        t.wait_ge(ss.h, ss.n)

        if upto == "pre":
            G3b = lambda buf: buf[:].rearrange("p (i c) -> p i c", c=64)
            g.wait_ge(ss.h, ss.n)
            g.wait_ge(vv.h, vv.n)
            for k, buf in enumerate((wk1, wk3, wk2, cosg, sing)):
                st.inc(g.dma_start(out=io["dbg"][:, k * 64:(k + 1) * 64].rearrange("(i p) c -> p i c", p=128),
                                   in_=G3b(buf)), 16)
            g.wait_ge(st.h, st.n)
            sy.wait_ge(st.h, st.n)
            pp.close()
            return

        pp.close()
        g.wait_ge(ss.h, ss.n)
        g.wait_ge(vv.h, vv.n)
        sy.wait_ge(ss.h, ss.n)
        NP = 2
        zxch = [E(nc.sbuf_tensor(f"p2_zx{i}", [128, 4608], BF16)) for i in range(NP)]
        mbufA = [E(nc.sbuf_tensor(f"p2_mA{i}", [128, 8192], BF16)) for i in range(NP)]
        mbufB = [E(nc.sbuf_tensor("p2_mB", [128, 8192], BF16))] * NP
        xg = [E(nc.sbuf_tensor(f"p2_xg{i}", [128, D_INNER], BF16)) for i in range(NP)]
        xw = [E(nc.sbuf_tensor(f"p2_xw{i}", [128, D_INNER], BF16)) for i in range(NP)]
        cxb = [E(nc.sbuf_tensor(f"p2_cx{i}", [128, D_INNER], BF16)) for i in range(NP)]
        ybuf = E(nc.sbuf_tensor("p2_y", [128, D_INNER], BF16))
        Brot = [E(nc.sbuf_tensor(f"p2_Brot{i}", [128, 128], BF16)) for i in range(NP)]
        Crot = [E(nc.sbuf_tensor(f"p2_Crot{i}", [128, 128], BF16)) for i in range(NP)]
        BT = [E(nc.sbuf_tensor(f"p2_BT{i}", [128, 128], BF16, is_transpose_dest=True)) for i in range(NP)]
        CT = [E(nc.sbuf_tensor(f"p2_CT{i}", [128, 128], BF16, is_transpose_dest=True)) for i in range(NP)]
        Ssb = [E(nc.sbuf_tensor(f"p2_Ssb{i}", [128, 128], BF16)) for i in range(NP)]
        state_f = E(nc.sbuf_tensor("p2_stf", [128, D_INNER], F32))
        state_b = E(nc.sbuf_tensor("p2_stb", [128, D_INNER], BF16))
        acs = [E(nc.sbuf_tensor(f"p2_acs{i}", [128, 64], F32)) for i in range(NP)]
        negA = [E(nc.sbuf_tensor(f"p2_negA{i}", [128, 64], F32)) for i in range(NP)]
        wexpE = [E(nc.sbuf_tensor(f"p2_wexp{i}", [128, 64], F32)) for i in range(NP)]
        expacs = [E(nc.sbuf_tensor(f"p2_eacs{i}", [128, 64], F32)) for i in range(NP)]
        gamb = [E(nc.sbuf_tensor(f"p2_gamb{i}", [128, 64], F32)) for i in range(NP)]
        ccoef = [E(nc.sbuf_tensor(f"p2_cco{i}", [128, 64], F32)) for i in range(NP)]
        sqt = [E(nc.sbuf_tensor("p2_sqt", [128, 128], F32))] * NP
        rcol = E(nc.sbuf_tensor("p2_rcol", [128, 16], F32))
        asum = [E(nc.sbuf_tensor(f"p2_asum{i}", [1, 64], F32)) for i in range(NP)]
        gamr = [E(nc.sbuf_tensor(f"p2_gamr{i}", [1, 64], F32)) for i in range(NP)]
        wkT = [E(nc.sbuf_tensor(f"p2_wkT{i}", [128, 512], F32)) for i in range(2)]

        ps_m = [E(nc.psum_tensor(f"p2_psm{i}", [128, 512], F32)) for i in range(2)]
        ps_S = E(nc.psum_tensor("p2_psS", [128, 128], F32))
        ps_yd = E(nc.psum_tensor("p2_psyd", [128, 512], F32))
        ps_yo = E(nc.psum_tensor("p2_psyo", [128, 512], F32))
        ps_st = E(nc.psum_tensor("p2_psst", [128, 512], F32))

        zx_busy = [0] * NP
        psm_busy = [0, 0]
        ydyo_busy = 0
        pst_busy = 0
        stbf_ready = vv.n
        psa_sbusy = 0
        psS_vbusy = 0
        mA_tbusy = [0] * NP
        mB_vbusy = [0] * NP
        bc_tbusy = [0] * NP
        prep_done = [0] * NP
        sprep_done = [0] * NP
        acs_vdone = [0] * NP
        ybuf_busy = 0
        c_sq = [0] * NP

        def emit_prep(c):
            par = c % NP
            zb = zxch[par]
            Asl = A_f[:, c * 64:(c + 1) * 64]
            Absl = A_bf[:, c * 64:(c + 1) * 64]
            gsl = g_f[:, c * 64:(c + 1) * 64]
            cossl = cosg[:, c * 64:(c + 1) * 64]
            sinsl = sing[:, c * 64:(c + 1) * 64]

            if zx_busy[par]:
                g.wait_ge(vv.h, zx_busy[par])
            ld.inc(g.dma_start(out=zb[:], in_=io["zx"][c * 128:(c + 1) * 128, 4096:8704]), 16)

            if mA_tbusy[par]:
                v.wait_ge(tT.h, mA_tbusy[par])
            vv.inc(v.tensor_tensor(
                mbufA[par][:].rearrange("p (h q) -> p h q", h=64),
                rep_inner(Absl, 64, 128),
                rep_free(consts["triu_bf"][:], 64, 128), ALU.mult), 1)
            rhsM_done = vv.n

            v.wait_ge(ld.h, ld.n)
            Braw = zb[:, 4096:4224]
            Craw = zb[:, 4224:4352]
            for (raw, rot, wbc, ci) in ((Braw, Brot[par], wB_bc, 0), (Craw, Crot[par], wC_bc, 1)):
                v.tensor_tensor(sqt[par][:], raw, raw, ALU.mult)
                vv.inc(v.tensor_reduce(rcol[:, ci * 4:ci * 4 + 1], sqt[par][:],
                                       mybir.AxisListType.X, ALU.add), 1)
                s.wait_ge(vv.h, vv.n)
                ss.inc(emit_rstd(s, consts, rcol[:, ci * 4:ci * 4 + 1],
                                 rcol[:, ci * 4 + 1:ci * 4 + 2], 1.0 / 128.0,
                                 rcol[:, ci * 4 + 2:ci * 4 + 3]), 1)
                v.wait_ge(ss.h, ss.n)
                v.scalar_tensor_tensor(sqt[par][:], raw, rcol[:, ci * 4 + 1:ci * 4 + 2], wbc[:],
                                       ALU.mult, ALU.mult)
                ev, od = sqt[par][:, 0:128:2], sqt[par][:, 1:128:2]
                v.tensor_tensor(rot[:, 0:128:2], cossl, ev, ALU.mult)
                v.tensor_tensor(wkT[0][:, 0:64], sinsl, od, ALU.mult)
                v.tensor_tensor(rot[:, 0:128:2], rot[:, 0:128:2], wkT[0][:, 0:64], ALU.subtract)
                v.tensor_tensor(rot[:, 1:128:2], sinsl, ev, ALU.mult)
                v.tensor_tensor(wkT[0][:, 0:64], cossl, od, ALU.mult)
                last_rope = v.tensor_tensor(rot[:, 1:128:2], rot[:, 1:128:2], wkT[0][:, 0:64], ALU.add)
            vv.inc(last_rope, 1)
            bc_done = vv.n
            v.tensor_tensor(sqt[par][:], Brot[par][:], Crot[par][:], ALU.mult)
            vv.inc(v.tensor_reduce(rcol[:, 8:9], sqt[par][:], mybir.AxisListType.X, ALU.add), 1)
            v.wait_ge(vv.h, vv.n)
            v.tensor_tensor(ccoef[par][:], g1_f[:, c * 64:(c + 1) * 64],
                            rcol[:, 8:9].broadcast_to([128, 64]), ALU.mult)
            v.tensor_tensor(ccoef[par][:], D_bc[:], ccoef[par][:], ALU.add)
            xsl = zb[:, 0:4096]
            x3 = xsl.rearrange("p (h q) -> p h q", h=64)
            v.tensor_tensor(cxb[par][:].rearrange("p (h q) -> p h q", h=64), x3,
                            rep_inner(ccoef[par][:], 64, 64), ALU.mult)
            vv.inc(v.tensor_tensor(xg[par][:].rearrange("p (h q) -> p h q", h=64), x3,
                                   rep_inner(gsl, 64, 64), ALU.mult), 1)
            zx_partial = vv.n

            sy.wait_ge(vv.h, bc_done)
            if bc_tbusy[par]:
                sy.wait_ge(tT.h, bc_tbusy[par])
            sq.inc(sy.dma_start_transpose(BT[par][:], Brot[par][:]), 16)
            sq.inc(sy.dma_start_transpose(CT[par][:], Crot[par][:]), 16)
            c_sq[par] = sq.n

            t.wait_ge(vv.h, rhsM_done)
            if mB_vbusy[par]:
                s.wait_ge(vv.h, mB_vbusy[par])
            for mt in range(16):
                pi = mt % 2
                if psm_busy[pi]:
                    t.wait_ge(ss.h, psm_busy[pi])
                tT.inc(t.matmul(ps_m[pi][:], consts["slower_bf"][:], mbufA[par][:, mt * 512:(mt + 1) * 512],
                                start=True, stop=True), 1)
                s.wait_ge(tT.h, tT.n)
                ss.inc(s.activation(mbufB[par][:, mt * 512:(mt + 1) * 512], ps_m[pi][:], AF.Exp), 1)
                psm_busy[pi] = ss.n
            mask_mm_done = tT.n

            v.tensor_scalar(negA[par][:], Asl, -1.0, None, ALU.mult)
            vv.inc(v.nop() if False else v.tensor_copy(rcol[:, 12:13], rcol[:, 12:13]), 1)
            t.wait_ge(vv.h, vv.n)
            if psa_sbusy:
                t.wait_ge(ss.h, psa_sbusy)
            tT.inc(t.matmul(ps_a[:], consts["utri_f32"][:], Asl, start=True, stop=True), 1)
            v.wait_ge(tT.h, tT.n)
            vv.inc(v.tensor_copy(acs[par][:], ps_a[:]), 1)
            tT.inc(t.matmul(ps_b[0:1, 0:64], consts["ones_col_f32"][:], Asl, start=True, stop=True), 1)
            v.wait_ge(tT.h, tT.n)
            vv.inc(v.tensor_copy(asum[par][:], ps_b[0:1, 0:64]), 1)
            t.wait_ge(vv.h, vv.n)
            t.matmul(ps_a[:], consts["utri_f32"][:], negA[par][:], start=True, stop=False,
                     skip_group_check=True)
            tT.inc(t.matmul(ps_a[:], consts["ones_row_f32"][:], asum[par][:], start=False, stop=True,
                            skip_group_check=True), 1)
            s.wait_ge(tT.h, tT.n)
            s.wait_ge(vv.h, vv.n)
            ss.inc(s.activation(wexpE[par][:], ps_a[:], AF.Exp), 1)
            psa_local = ss.n
            ss.inc(s.activation(expacs[par][:], acs[par][:], AF.Exp), 1)
            ss.inc(s.activation(gamr[par][:], asum[par][:], AF.Exp), 1)
            t.wait_ge(ss.h, ss.n)
            tT.inc(t.matmul(ps_b[:, 0:64], consts["ones_row_f32"][:], gamr[par][:],
                            start=True, stop=True), 1)
            v.wait_ge(tT.h, tT.n)
            vv.inc(v.tensor_copy(gamb[par][:], ps_b[:, 0:64]), 1)
            v.wait_ge(ss.h, ss.n)
            vv.inc(v.tensor_tensor(xw[par][:].rearrange("p (h q) -> p h q", h=64),
                                   xg[par][:].rearrange("p (h q) -> p h q", h=64),
                                   rep_inner(wexpE[par][:], 64, 64), ALU.mult), 1)
            v.wait_ge(ss.h, psm_busy[0])
            v.wait_ge(ss.h, psm_busy[1])
            v.wait_ge(tT.h, mask_mm_done)
            vv.inc(v.tensor_tensor(mbufA[par][:].rearrange("p (h q) -> p h q", h=64),
                                   mbufB[par][:].rearrange("p (h q) -> p h q", h=64),
                                   rep_free(Ssb[par][:], 64, 128), ALU.mult), 1)
            mB_vbusy[par] = vv.n
            zx_busy[par] = vv.n
            prep_done[par] = vv.n
            sprep_done[par] = ss.n
            return psa_local

        def emit_smm(c):
            par = c % NP
            t.wait_ge(sq.h, c_sq[par])
            if psS_vbusy:
                t.wait_ge(vv.h, psS_vbusy)
            tT.inc(t.matmul(ps_S[:], BT[par][:], CT[par][:], start=True, stop=True), 1)
            v.wait_ge(tT.h, tT.n)
            vv.inc(v.tensor_tensor(Ssb[par][:], ps_S[:], consts["trius_bf"][:], ALU.mult), 1)
            psS_vbusy = 0
            return vv.n

        for c in range(32):
            par = c % NP
            zb = zxch[par]
            Asl = A_f[:, c * 64:(c + 1) * 64]
            Absl = A_bf[:, c * 64:(c + 1) * 64]
            gsl = g_f[:, c * 64:(c + 1) * 64]
            cossl = cosg[:, c * 64:(c + 1) * 64]
            sinsl = sing[:, c * 64:(c + 1) * 64]

            if zx_busy[par]:
                g.wait_ge(vv.h, zx_busy[par])
            ld.inc(g.dma_start(out=zb[:], in_=io["zx"][c * 128:(c + 1) * 128, 4096:8704]), 16)

            if mA_tbusy[par]:
                v.wait_ge(tT.h, mA_tbusy[par])
            vv.inc(v.tensor_tensor(
                mbufA[par][:].rearrange("p (h q) -> p h q", h=64),
                rep_inner(Absl, 64, 128),
                rep_free(consts["triu_bf"][:], 64, 128), ALU.mult), 1)
            rhsM_done = vv.n

            v.wait_ge(ld.h, ld.n)
            Braw = zb[:, 4096:4224]
            Craw = zb[:, 4224:4352]
            last_rope = None
            for (raw, rot, wbc, ci) in ((Braw, Brot[par], wB_bc, 0), (Craw, Crot[par], wC_bc, 1)):
                if ci == 0 and bc_tbusy[par]:
                    v.wait_ge(tT.h, bc_tbusy[par])
                v.tensor_tensor(sqt[par][:], raw, raw, ALU.mult)
                vv.inc(v.tensor_reduce(rcol[:, ci * 4:ci * 4 + 1], sqt[par][:],
                                       mybir.AxisListType.X, ALU.add), 1)
                s.wait_ge(vv.h, vv.n)
                ss.inc(emit_rstd(s, consts, rcol[:, ci * 4:ci * 4 + 1],
                                 rcol[:, ci * 4 + 1:ci * 4 + 2], 1.0 / 128.0,
                                 rcol[:, ci * 4 + 2:ci * 4 + 3]), 1)
                v.wait_ge(ss.h, ss.n)
                v.scalar_tensor_tensor(sqt[par][:], raw, rcol[:, ci * 4 + 1:ci * 4 + 2], wbc[:],
                                       ALU.mult, ALU.mult)
                ev, od = sqt[par][:, 0:128:2], sqt[par][:, 1:128:2]
                v.tensor_tensor(rot[:, 0:128:2], cossl, ev, ALU.mult)
                v.tensor_tensor(wkT[0][:, 0:64], sinsl, od, ALU.mult)
                v.tensor_tensor(rot[:, 0:128:2], rot[:, 0:128:2], wkT[0][:, 0:64], ALU.subtract)
                v.tensor_tensor(rot[:, 1:128:2], sinsl, ev, ALU.mult)
                v.tensor_tensor(wkT[0][:, 0:64], cossl, od, ALU.mult)
                last_rope = v.tensor_tensor(rot[:, 1:128:2], rot[:, 1:128:2], wkT[0][:, 0:64], ALU.add)
            vv.inc(last_rope, 1)
            bc_done = vv.n
            v.tensor_tensor(sqt[par][:], Brot[par][:], Crot[par][:], ALU.mult)
            vv.inc(v.tensor_reduce(rcol[:, 8:9], sqt[par][:], mybir.AxisListType.X, ALU.add), 1)
            v.wait_ge(vv.h, vv.n)
            v.tensor_tensor(ccoef[par][:], g1_f[:, c * 64:(c + 1) * 64],
                            rcol[:, 8:9].broadcast_to([128, 64]), ALU.mult)
            v.tensor_tensor(ccoef[par][:], D_bc[:], ccoef[par][:], ALU.add)
            xsl = zb[:, 0:4096]
            x3v = xsl.rearrange("p (h q) -> p h q", h=64)
            v.tensor_tensor(cxb[par][:].rearrange("p (h q) -> p h q", h=64), x3v,
                            rep_inner(ccoef[par][:], 64, 64), ALU.mult)
            vv.inc(v.tensor_tensor(xg[par][:].rearrange("p (h q) -> p h q", h=64), x3v,
                                   rep_inner(gsl, 64, 64), ALU.mult), 1)

            sy.wait_ge(vv.h, bc_done)
            if bc_tbusy[par]:
                sy.wait_ge(tT.h, bc_tbusy[par])
            sq.inc(sy.dma_start_transpose(BT[par][:], Brot[par][:]), 16)
            sq.inc(sy.dma_start_transpose(CT[par][:], Crot[par][:]), 16)
            c_sq[par] = sq.n

            t.wait_ge(vv.h, rhsM_done)
            if mB_vbusy[par]:
                s.wait_ge(vv.h, mB_vbusy[par])
            for mt in range(16):
                pi = mt % 2
                if psm_busy[pi]:
                    t.wait_ge(ss.h, psm_busy[pi])
                tT.inc(t.matmul(ps_m[pi][:], consts["slower_bf"][:], mbufA[par][:, mt * 512:(mt + 1) * 512],
                                start=True, stop=True), 1)
                s.wait_ge(tT.h, tT.n)
                ss.inc(s.activation(mbufB[par][:, mt * 512:(mt + 1) * 512], ps_m[pi][:], AF.Exp), 1)
                psm_busy[pi] = ss.n
            mask_mm_done = tT.n

            t.wait_ge(sq.h, c_sq[par])
            if psS_vbusy:
                t.wait_ge(vv.h, psS_vbusy)
            tT.inc(t.matmul(ps_S[:], BT[par][:], CT[par][:], start=True, stop=True), 1)
            v.wait_ge(tT.h, tT.n)
            vv.inc(v.tensor_tensor(Ssb[par][:], ps_S[:], consts["trius_bf"][:], ALU.mult), 1)
            psS_vbusy = vv.n

            vv.inc(v.tensor_scalar(negA[par][:], Asl, -1.0, None, ALU.mult), 1)
            t.wait_ge(vv.h, vv.n)
            if psa_sbusy:
                t.wait_ge(ss.h, psa_sbusy)
            tT.inc(t.matmul(ps_a[:], consts["utri_f32"][:], Asl, start=True, stop=True), 1)
            v.wait_ge(tT.h, tT.n)
            vv.inc(v.tensor_copy(acs[par][:], ps_a[:]), 1)
            tT.inc(t.matmul(ps_b[0:1, 0:64], consts["ones_col_f32"][:], Asl, start=True, stop=True), 1)
            v.wait_ge(tT.h, tT.n)
            vv.inc(v.tensor_copy(asum[par][:], ps_b[0:1, 0:64]), 1)
            t.wait_ge(vv.h, vv.n)
            t.matmul(ps_a[:], consts["utri_f32"][:], negA[par][:], start=True, stop=False,
                     skip_group_check=True)
            tT.inc(t.matmul(ps_a[:], consts["ones_row_f32"][:], asum[par][:], start=False, stop=True,
                            skip_group_check=True), 1)
            s.wait_ge(tT.h, tT.n)
            s.wait_ge(vv.h, vv.n)
            ss.inc(s.activation(wexpE[par][:], ps_a[:], AF.Exp), 1)
            psa_sbusy = ss.n
            ss.inc(s.activation(expacs[par][:], acs[par][:], AF.Exp), 1)
            ss.inc(s.activation(gamr[par][:], asum[par][:], AF.Exp), 1)
            t.wait_ge(ss.h, ss.n)
            tT.inc(t.matmul(ps_b[:, 0:64], consts["ones_row_f32"][:], gamr[par][:],
                            start=True, stop=True), 1)
            v.wait_ge(tT.h, tT.n)
            vv.inc(v.tensor_copy(gamb[par][:], ps_b[:, 0:64]), 1)
            v.wait_ge(ss.h, ss.n)
            vv.inc(v.tensor_tensor(xw[par][:].rearrange("p (h q) -> p h q", h=64),
                                   xg[par][:].rearrange("p (h q) -> p h q", h=64),
                                   rep_inner(wexpE[par][:], 64, 64), ALU.mult), 1)
            v.wait_ge(ss.h, psm_busy[0])
            v.wait_ge(ss.h, psm_busy[1])
            vv.inc(v.tensor_tensor(mbufA[par][:].rearrange("p (h q) -> p h q", h=64),
                                   mbufB[par][:].rearrange("p (h q) -> p h q", h=64),
                                   rep_free(Ssb[par][:], 64, 128), ALU.mult), 1)
            mB_vbusy[par] = vv.n
            zx_busy[par] = vv.n
            mt_done = vv.n

            t.wait_ge(vv.h, mt_done)
            for hg in range(HG):
                if ydyo_busy:
                    t.wait_ge(vv.h, ydyo_busy)
                for h8 in range(8):
                    h = hg * 8 + h8
                    tT.inc(t.matmul(ps_yd[:, h8 * 64:(h8 + 1) * 64],
                                    mbufA[par][:, h * 128:(h + 1) * 128],
                                    xg[par][:, h * 64:(h + 1) * 64],
                                    start=(h8 == 0), stop=(h8 == 7), skip_group_check=True), 1)
                if pst_busy:
                    t.wait_ge(vv.h, pst_busy)
                for h8 in range(8):
                    h = hg * 8 + h8
                    tT.inc(t.matmul(ps_st[:, h8 * 64:(h8 + 1) * 64], Brot[par][:],
                                    xw[par][:, h * 64:(h + 1) * 64],
                                    start=(h8 == 0), stop=(h8 == 7), skip_group_check=True), 1)
                if c > 0:
                    if hg == 0:
                        t.wait_ge(vv.h, stbf_ready)
                    for h8 in range(8):
                        h = hg * 8 + h8
                        tT.inc(t.matmul(ps_yo[:, h8 * 64:(h8 + 1) * 64], CT[par][:],
                                        state_b[:, h * 64:(h + 1) * 64],
                                        start=(h8 == 0), stop=(h8 == 7), skip_group_check=True), 1)
                mm_done = tT.n
                v.wait_ge(tT.h, mm_done)
                if hg == 0 and ybuf_busy:
                    v.wait_ge(st.h, ybuf_busy)
                wt = wkT[hg % 2]
                if c > 0:
                    v.tensor_tensor(wt[:].rearrange("p (h q) -> p h q", h=8),
                                    ps_yo[:].rearrange("p (h q) -> p h q", h=8),
                                    rep_inner(expacs[par][:, hg * 8:(hg + 1) * 8], 8, 64), ALU.mult)
                    vv.inc(v.tensor_tensor(ybuf[:, hg * 512:(hg + 1) * 512], wt[:], ps_yd[:], ALU.add), 1)
                else:
                    vv.inc(v.tensor_copy(ybuf[:, hg * 512:(hg + 1) * 512], ps_yd[:]), 1)
                ydyo_busy = vv.n
                ssl = state_f[:, hg * 512:(hg + 1) * 512]
                if c > 0:
                    v.tensor_tensor(ssl.rearrange("p (h q) -> p h q", h=8), ssl.rearrange("p (h q) -> p h q", h=8),
                                    rep_inner(gamb[par][:, hg * 8:(hg + 1) * 8], 8, 64), ALU.mult)
                    v.tensor_tensor(ssl, ssl, ps_st[:], ALU.add)
                else:
                    v.tensor_copy(ssl, ps_st[:])
                vv.inc(v.tensor_copy(state_b[:, hg * 512:(hg + 1) * 512], ssl), 1)
                pst_busy = vv.n
            stbf_ready = vv.n
            mA_tbusy[par] = tT.n
            bc_tbusy[par] = tT.n
            vv.inc(v.tensor_tensor(ybuf[:], ybuf[:], cxb[par][:], ALU.add), 1)
            g.wait_ge(vv.h, vv.n)
            st.inc(g.dma_start(out=io["ydram"][c * 128:(c + 1) * 128, :], in_=ybuf[:]), 16)
            ybuf_busy = st.n
        g.wait_ge(st.h, st.n)
        return {"vv": (vv, vv.n), "ss": (ss, ss.n), "tT": (tT, tT.n), "st": (st, st.n)}


def phase2c(ctx, io, consts, eng, entry=None):
    nc, t, v, s, g, sy = ctx.nc, eng["t"], eng["v"], eng["s"], eng["g"], eng["sy"]
    ident = consts["ident"]
    with ExitStack() as ph:
        E = ph.enter_context
        ld = ctx.sem("p2c_ld")
        trd = ctx.sem("p2c_trd")
        tev = ctx.sem("p2c_tev")
        mmd = ctx.sem("p2c_mmd")
        gev = ctx.sem("p2c_gev")
        ssc = ctx.sem("p2c_ssc")
        st = ctx.sem("p2c_st")

        woT = E(nc.sbuf_tensor("p2c_woT", [128, 32 * D], BF16))
        wnat = E(nc.sbuf_tensor("p2c_wnat", [128, D_INNER], BF16))
        onw_g = E(nc.sbuf_tensor("p2c_onw", [128, 32], F32))
        ytile = [E(nc.sbuf_tensor(f"p2c_y{i}", [128, D_INNER], BF16)) for i in range(2)]
        ztile = [E(nc.sbuf_tensor("p2c_z", [128, D_INNER], BF16))]
        sq = E(nc.sbuf_tensor("p2c_sq", [128, D_INNER], BF16))
        vgrid = E(nc.sbuf_tensor("p2c_vg", [128, 40], F32))
        ylT = E(nc.sbuf_tensor("p2c_ylT", [128, 32 * 128], BF16))
        utile = E(nc.sbuf_tensor("p2c_u", [128, D], F32))
        xpre = E(nc.sbuf_tensor("p2c_xpre", [128, D], F32))
        x1t = E(nc.sbuf_tensor("p2c_x1", [128, D], F32))
        cols = E(nc.sbuf_tensor("p2c_cols", [128, 8], F32))
        tps = [E(nc.psum_tensor(f"p2c_tp{i}", [128, 128], BF16)) for i in range(4)]
        mps = [E(nc.psum_tensor(f"p2c_mp{i}", [128, 512], F32)) for i in range(4)]

        tps_busy = [0] * 4
        mps_busy = [0] * 4

        if entry:
            for e in (t, v, s, g, sy):
                for key, (sem, val) in entry.items():
                    e.wait_ge(sem.h, val)
        for kt in range(32):
            ld.inc(g.dma_start(out=onw_g[:, kt:kt + 1],
                               in_=io["onw"][0:1, kt * 128:(kt + 1) * 128].rearrange("o c -> c o")), 16)
        ybusy = [0, 0]
        zbusy = [0, 0]
        for ti in range(NT):
            par = ti % 2
            yt, zt = ytile[par], ztile[0]
            if ybusy[par]:
                g.wait_ge(st.h, ybusy[par])
            ld.inc(g.dma_start(out=yt[:], in_=io["ydram"][ti * 128:(ti + 1) * 128, :]), 16)
            if zbusy[0]:
                g.wait_ge(tev.h, zbusy[0])
            ld.inc(g.dma_start(out=zt[:], in_=io["zx"][ti * 128:(ti + 1) * 128, 0:D_INNER]), 16)
            s.wait_ge(ld.h, ld.n)
            ssc.inc(s.activation(zt[:], zt[:], AF.Silu), 1)
            v.wait_ge(ssc.h, ssc.n)
            tev.inc(v.tensor_tensor(yt[:], yt[:], zt[:], ALU.mult), 1)
            zbusy[0] = tev.n
            v.tensor_tensor(sq[:], yt[:], yt[:], ALU.mult)
            tev.inc(v.tensor_reduce(vgrid[:, ti:ti + 1], sq[:], mybir.AxisListType.X, ALU.add), 1)
            g.wait_ge(tev.h, tev.n)
            st.inc(g.dma_start(out=io["yzdram"][ti * 128:(ti + 1) * 128, :], in_=yt[:]), 16)
            ybusy[par] = st.n
        s.wait_ge(tev.h, tev.n)
        ssc.inc(emit_rstd(s, consts, vgrid[:, 0:32], vgrid[:, 0:32], 1.0 / float(D_INNER),
                          vgrid[:, 0:32]), 1)

        v.wait_ge(ld.h, ld.n)
        for oc in range(16):
            ld.inc(g.dma_start(out=wnat[:], in_=io["w_out"][oc * 128:(oc + 1) * 128, :]), 16)
            t.wait_ge(ld.h, ld.n)
            for kt in range(32):
                pi = kt % 4
                if tps_busy[pi]:
                    t.wait_ge(tev.h, tps_busy[pi])
                trd.inc(t.transpose(tps[pi][:], wnat[:, kt * 128:(kt + 1) * 128], ident[:]), 1)
                v.wait_ge(trd.h, trd.n)
                tev.inc(v.tensor_scalar(woT[:, kt * D + oc * 128: kt * D + (oc + 1) * 128],
                                        tps[pi][:], onw_g[:, kt:kt + 1], None, ALU.mult), 1)
                tps_busy[pi] = tev.n
            g.wait_ge(trd.h, trd.n)

        yz_busy = [0, 0]
        u_busy = 0
        x1_busy = 0
        for ti in range(NT):
            par = ti % 2
            yt = ytile[par]
            if yz_busy[par]:
                g.wait_ge(trd.h, yz_busy[par])
            ld.inc(g.dma_start(out=yt[:], in_=io["yzdram"][ti * 128:(ti + 1) * 128, :]), 16)
            if u_busy:
                g.wait_ge(gev.h, u_busy)
            ld.inc(g.dma_start(out=utile[:], in_=io["u"][ti * 128:(ti + 1) * 128, :]), 16)
            v.wait_ge(ld.h, ld.n)
            v.wait_ge(ssc.h, ssc.n)
            tev.inc(v.tensor_scalar(yt[:], yt[:], vgrid[:, ti:ti + 1], None, ALU.mult), 1)
            t.wait_ge(tev.h, tev.n)
            for kt in range(32):
                pi = kt % 4
                if tps_busy[pi]:
                    t.wait_ge(tev.h, tps_busy[pi])
                trd.inc(t.transpose(tps[pi][:], yt[:, kt * 128:(kt + 1) * 128], ident[:]), 1)
                v.wait_ge(trd.h, trd.n)
                tev.inc(v.tensor_copy(ylT[:, kt * 128:(kt + 1) * 128], tps[pi][:]), 1)
                tps_busy[pi] = tev.n
            yz_busy[par] = trd.n
            t.wait_ge(tev.h, tev.n)
            for oc in range(4):
                pi = oc % 4
                if mps_busy[pi]:
                    t.wait_ge(gev.h, mps_busy[pi])
                ins = None
                for kt in range(32):
                    ins = t.matmul(mps[pi][:], ylT[:, kt * 128:(kt + 1) * 128],
                                   woT[:, kt * D + oc * 512:(kt) * D + (oc + 1) * 512],
                                   start=(kt == 0), stop=(kt == 31))
                mmd.inc(ins, 1)
                v.wait_ge(mmd.h, mmd.n)
                gev.inc(v.tensor_tensor(xpre[:, oc * 512:(oc + 1) * 512], mps[pi][:],
                                        utile[:, oc * 512:(oc + 1) * 512], ALU.add), 1)
                mps_busy[pi] = gev.n
                u_busy = gev.n
            if x1_busy:
                v.wait_ge(st.h, x1_busy)
            tev.inc(v.tensor_reduce(cols[:, 0:1], xpre[:], mybir.AxisListType.X, ALU.add), 1)
            v.wait_ge(tev.h, tev.n)
            v.scalar_tensor_tensor(x1t[:], cols[:, 0:1].broadcast_to([128, D]), -1.0 / D,
                                   xpre[:], ALU.mult, ALU.add)
            v.tensor_tensor(xpre[:], x1t[:], x1t[:], ALU.mult)
            vv_ = tev
            vv_.inc(v.tensor_reduce(cols[:, 2:3], xpre[:], mybir.AxisListType.X, ALU.add), 1)
            s.wait_ge(vv_.h, vv_.n)
            ssc.inc(emit_rstd(s, consts, cols[:, 2:3], cols[:, 3:4], 1.0 / D, cols[:, 4:5]), 1)
            v.wait_ge(ssc.h, ssc.n)
            gev.inc(v.tensor_scalar(x1t[:], x1t[:], cols[:, 3:4], None, ALU.mult), 1)
            g.wait_ge(gev.h, gev.n)
            st.inc(g.dma_start(out=io["x1"][ti * 128:(ti + 1) * 128, :], in_=x1t[:]), 16)
            x1_busy = st.n
        g.wait_ge(st.h, st.n)
        sy.wait_ge(st.h, st.n)
        return {"tev": (tev, tev.n), "ssc": (ssc, ssc.n), "gev": (gev, gev.n), "st": (st, st.n)}


def build_p2(upto=None, dump_c0=False):
    nc = bass.Bass("TRN2", target_bir_lowering=False)
    io = {}
    io["zx"] = nc.dram_tensor("zx", [S_TOK, D_IN_PAD], BF16, kind="ExternalInput").ap()
    io["dlt"] = nc.dram_tensor("dlt", [S_TOK, 192], F32, kind="ExternalInput").ap()
    io["alog"] = nc.dram_tensor("alog", [1, 64], F32, kind="ExternalInput").ap()
    io["dtb"] = nc.dram_tensor("dtb", [1, 64], F32, kind="ExternalInput").ap()
    io["dcoef"] = nc.dram_tensor("dcoef", [1, 64], F32, kind="ExternalInput").ap()
    io["bnw"] = nc.dram_tensor("bnw", [1, 128], F32, kind="ExternalInput").ap()
    io["cnw"] = nc.dram_tensor("cnw", [1, 128], F32, kind="ExternalInput").ap()
    io["u"] = nc.dram_tensor("u", [S_TOK, D], F32, kind="ExternalInput").ap()
    io["w_out"] = nc.dram_tensor("w_out", [D, D_INNER], F32, kind="ExternalInput").ap()
    io["onw"] = nc.dram_tensor("onw", [1, D_INNER], F32, kind="ExternalInput").ap()
    io["x1"] = nc.dram_tensor("x1", [S_TOK, D], F32, kind="ExternalOutput").ap()
    io["dbg"] = nc.dram_tensor("dbg", [S_TOK, 320], F32, kind="ExternalOutput").ap()
    if dump_c0:
        io["dbgM"] = nc.dram_tensor("dbgM", [128, 8192], BF16, kind="ExternalOutput").ap()
        io["dbgE"] = nc.dram_tensor("dbgE", [128, 8192], BF16, kind="ExternalOutput").ap()
        io["dbgS"] = nc.dram_tensor("dbgS", [128, 128], BF16, kind="ExternalOutput").ap()
        io["dbgxg"] = nc.dram_tensor("dbgxg", [128, 4096], BF16, kind="ExternalOutput").ap()
        io["dbgyp"] = nc.dram_tensor("dbgyp", [128, 4096], BF16, kind="ExternalOutput").ap()
        io["dbgcx"] = nc.dram_tensor("dbgcx", [128, 4096], BF16, kind="ExternalOutput").ap()
    io["dbg2"] = nc.dram_tensor("dbg2", [1, 4224], F32, kind="ExternalOutput").ap()
    io["ydram"] = nc.dram_tensor("ydram", [S_TOK, D_INNER], BF16,
                                 kind=("ExternalOutput" if upto == "ssd" else "Internal")).ap()
    io["yzdram"] = nc.dram_tensor("yzdram", [S_TOK, D_INNER], BF16, kind="Internal").ap()
    io["adram"] = nc.dram_tensor("adram", [S_TOK, 64], F32, kind="Internal").ap()
    io["ashdram"] = nc.dram_tensor("ashdram", [S_TOK, 64], F32, kind="Internal").ap()
    with ExitStack() as stack:
        ctx = Ctx(nc, stack)
        eng = {"t": nc.tensor, "v": nc.vector, "s": nc.scalar, "g": nc.gpsimd, "sy": nc.sync}
        ld0 = ctx.sem("c_ld")
        consts = load_consts(ctx, nc.gpsimd, ld0, keys=("ident", "utri_f32", "ones_col_f32", "ones_row_f32", "triu_bf", "trius_bf", "slower_bf", "csts"))
        for e in (nc.tensor, nc.vector, nc.scalar, nc.sync):
            e.wait_ge(ld0.h, ld0.n)
        ph2 = phase2(ctx, io, consts, eng, upto=(None if upto == "ssd" else upto), dump_c0=dump_c0)
        if upto is None:
            phase2c(ctx, io, consts, eng, entry=ph2)
    return nc


def phase3(ctx, io, consts, eng, entry=None):
    nc, t, v, s, g, sy = ctx.nc, eng["t"], eng["v"], eng["s"], eng["g"], eng["sy"]
    ident = consts["ident"]
    with ExitStack() as ph:
        E = ph.enter_context
        ld = ctx.sem("p3_ld")
        tT = ctx.sem("p3_tT")
        vv = ctx.sem("p3_vv")
        ss = ctx.sem("p3_ss")
        st = ctx.sem("p3_st")

        if entry:
            for e in (t, v, s, g, sy):
                for key, (sem, val) in entry.items():
                    e.wait_ge(sem.h, val)

        sug = E(nc.sbuf_tensor("p3_sug", [128, 16], F32))
        sdp = E(nc.sbuf_tensor("p3_sdp", [128, 64], F32))
        wsu = E(nc.sbuf_tensor("p3_wsu", [128, 128], F32))
        wsd = E(nc.sbuf_tensor("p3_wsd", [128, 16], F32))
        for j in range(16):
            ld.inc(g.dma_start(out=sug[:, j:j + 1],
                               in_=io["su_ug"][0:1, j * 128:(j + 1) * 128].rearrange("o c -> c o")), 16)
        for j in range(64):
            ld.inc(g.dma_start(out=sdp[:, j:j + 1],
                               in_=io["su_dp"][0:1, j * 128:(j + 1) * 128].rearrange("o c -> c o")), 16)
        for icol in range(128):
            ld.inc(g.dma_start(out=wsu[:, icol:icol + 1],
                               in_=io["wsc_ug"][0:1, icol * 128:(icol + 1) * 128].rearrange("o c -> c o")), 16)
        for icol in range(16):
            ld.inc(g.dma_start(out=wsd[:, icol:icol + 1],
                               in_=io["wsc_dp"][0:1, icol * 128:(icol + 1) * 128].rearrange("o c -> c o")), 16)
        v.wait_ge(ld.h, ld.n)
        v.tensor_scalar(wsu[:], wsu[:], 64.0, None, ALU.mult)
        vv.inc(v.tensor_scalar(wsd[:], wsd[:], 64.0, None, ALU.mult), 1)

        def gather2048(dram, kt, cols):
            return dram[8 * kt:, cols].rearrange("(bl j r) c -> (bl j) r c", bl=8, j=16)[:, 0, :] \
                if False else dram.rearrange("(j blk bl) c -> (blk bl) j c", j=16, bl=8)[:, :, cols] \
                if False else None

        with ExitStack() as sA:
            EA = sA.enter_context
            Sjh = EA(nc.sbuf_tensor("p3_Sjh", [128, 16 * 128], BF16))
            xch = EA(nc.sbuf_tensor("p3_xch", [128, DK * S_TOK], BF16))
            xt = EA(nc.sbuf_tensor("p3_xt", [128, D], BF16))
            zev = [EA(nc.sbuf_tensor(f"p3_zev{i}", [128, 512], BF16)) for i in range(6)]
            tps = [EA(nc.psum_tensor(f"p3_tpA{i}", [128, 128], BF16)) for i in range(2)]
            mpsA = [EA(nc.psum_tensor(f"p3_mpA{i}", [128, 512], F32)) for i in range(2)]

            v.wait_ge(ld.h, ld.n)
            ins = None
            for j in range(16):
                ins = v.tensor_scalar(Sjh[:, j * 128:(j + 1) * 128], consts["h128_2048"][:],
                                      sug[:, j:j + 1], None, ALU.mult)
            vv.inc(ins, 1)
            t.wait_ge(vv.h, vv.n)

            tps_busy = [0, 0]
            mps_busy = [0, 0]
            zev_busy = [0] * 6
            xt_busy = 0
            xch_view = lambda kt, ti: xch[:, kt * S_TOK + ti * 128:(kt) * S_TOK + (ti + 1) * 128]

            for ti in range(NT):
                if xt_busy:
                    g.wait_ge(tT.h, xt_busy)
                ld.inc(g.dma_start(out=xt[:], in_=io["x1"][ti * 128:(ti + 1) * 128, :]), 16)
                t.wait_ge(ld.h, ld.n)
                for kt in range(DK):
                    pi = kt % 2
                    if tps_busy[pi]:
                        t.wait_ge(vv.h, tps_busy[pi])
                    tT.inc(t.transpose(tps[pi][:], xt[:, kt * 128:(kt + 1) * 128], ident[:]), 1)
                    v.wait_ge(tT.h, tT.n)
                    vv.inc(v.tensor_copy(xch_view(kt, ti), tps[pi][:]), 1)
                    tps_busy[pi] = vv.n
                xt_busy = tT.n
            t.wait_ge(vv.h, vv.n)
            for j in range(16):
                for tc in range(8):
                    pi = tc % 2
                    ze = (j * 8 + tc) % 6
                    if mps_busy[pi]:
                        t.wait_ge(vv.h, mps_busy[pi])
                    tT.inc(t.matmul(mpsA[pi][:], Sjh[:, j * 128:(j + 1) * 128],
                                    xch[:, j * S_TOK + tc * 512:(j) * S_TOK + (tc + 1) * 512],
                                    start=True, stop=True), 1)
                    v.wait_ge(tT.h, tT.n)
                    if zev_busy[ze]:
                        v.wait_ge(st.h, zev_busy[ze])
                    vv.inc(v.tensor_copy(zev[ze][:], mpsA[pi][:]), 1)
                    mps_busy[pi] = vv.n
                    g.wait_ge(vv.h, vv.n)
                    st.inc(g.dma_start(out=io["z2dram"][j * 128:(j + 1) * 128, tc * 512:(tc + 1) * 512],
                                       in_=zev[ze][:]), 16)
                    zev_busy[ze] = st.n

        with ExitStack() as sB:
            EB = sB.enter_context
            wnat = EB(nc.sbuf_tensor("p3_wnat", [128, 2 * D], BF16))
            wstg = EB(nc.sbuf_tensor("p3_wstg", [128, 16 * 256], BF16))
            gbuf = [EB(nc.sbuf_tensor(f"p3_gbuf{i}", [128, 512], BF16)) for i in range(4)]
            vev = [EB(nc.sbuf_tensor(f"p3_vev{i}", [128, 512], BF16)) for i in range(6)]
            tpsB = [EB(nc.psum_tensor(f"p3_tpB{i}", [128, 128], BF16)) for i in range(2)]
            mpsB = [EB(nc.psum_tensor(f"p3_mpB{i}", [128, 512], F32)) for i in range(2)]
            t.wait_ge(vv.h, vv.n)
            for (wdram, wdraft, vfin, scol_off) in ((io["w_up"], io["wdraft"], io["vupF"], 0),
                                                    (io["w_gate"], io["wdraft"], io["vgtF"], 64)):
                tps_busy = [0, 0]
                wnat_busy = 0
                wstg_st = 0
                for ic in range(32):
                    if wnat_busy:
                        g.wait_ge(tT.h, wnat_busy)
                    ld.inc(g.dma_start(
                        out=wnat[:].rearrange("p (a d) -> p a d", a=2),
                        in_=wdram[ic * 256:(ic + 1) * 256, :].rearrange("(a p) d -> p a d", p=128)), 16)
                    v.wait_ge(ld.h, ld.n)
                    ins = None
                    for a in range(2):
                        ins = v.tensor_scalar(wnat[:, a * D:(a + 1) * D], wnat[:, a * D:(a + 1) * D],
                                              wsu[:, scol_off + ic * 2 + a:scol_off + ic * 2 + a + 1],
                                              None, ALU.mult)
                    vv.inc(ins, 1)
                    t.wait_ge(vv.h, vv.n)
                    if wstg_st:
                        v.wait_ge(st.h, wstg_st)
                    for a in range(2):
                        for kt in range(DK):
                            pi = kt % 2
                            if tps_busy[pi]:
                                t.wait_ge(vv.h, tps_busy[pi])
                            tT.inc(t.transpose(tpsB[pi][:], wnat[:, a * D + kt * 128: a * D + (kt + 1) * 128],
                                               ident[:]), 1)
                            v.wait_ge(tT.h, tT.n)
                            vv.inc(v.tensor_copy(wstg[:, kt * 256 + a * 128: kt * 256 + (a + 1) * 128],
                                                 tpsB[pi][:]), 1)
                            tps_busy[pi] = vv.n
                    wnat_busy = tT.n
                    g.wait_ge(vv.h, vv.n)
                    st.inc(g.dma_start(
                        out=wdraft.rearrange("(kt p) c -> p kt c", p=128)[:, :, ic * 256:(ic + 1) * 256],
                        in_=wstg[:].rearrange("p (kt c) -> p kt c", c=256)), 16)
                    wstg_st = st.n
                g.wait_ge(st.h, st.n)
                gb_busy = [0] * 4
                mps_busy2 = [0, 0]
                vev_busy = [0] * 6
                for kt in range(DK):
                    for cchunk in range(16):
                        it = kt * 16 + cchunk
                        pi = it % 2
                        gb = it % 4
                        ve = it % 6
                        if gb_busy[gb]:
                            g.wait_ge(tT.h, gb_busy[gb])
                        for bl in range(8):
                            ld.inc(g.dma_start(
                                out=gbuf[gb][bl * 16:(bl + 1) * 16, :],
                                in_=wdraft[kt * 8 + bl::128, cchunk * 512:(cchunk + 1) * 512]), 16)
                        t.wait_ge(ld.h, ld.n)
                        if mps_busy2[pi]:
                            t.wait_ge(vv.h, mps_busy2[pi])
                        tT.inc(t.matmul(mpsB[pi][:], consts["i8h16"][:], gbuf[gb][:],
                                        start=True, stop=True), 1)
                        gb_busy[gb] = tT.n
                        v.wait_ge(tT.h, tT.n)
                        if vev_busy[ve]:
                            v.wait_ge(st.h, vev_busy[ve])
                        vv.inc(v.tensor_copy(vev[ve][:], mpsB[pi][:]), 1)
                        mps_busy2[pi] = vv.n
                        g.wait_ge(vv.h, vv.n)
                        st.inc(g.dma_start(out=vfin[kt * 128:(kt + 1) * 128,
                                                    cchunk * 512:(cchunk + 1) * 512],
                                           in_=vev[ve][:]), 16)
                        vev_busy[ve] = st.n

        with ExitStack() as sC:
            EC = sC.enter_context
            Sdh = EC(nc.sbuf_tensor("p3_Sdh", [128, 64 * 128], BF16))
            xpp = EC(nc.sbuf_tensor("p3_xpp", [128, 16 * S_TOK], BF16))
            Vu = [EC(nc.sbuf_tensor(f"p3_Vu{i}", [128, 16 * 128], BF16)) for i in range(2)]
            Vg = [EC(nc.sbuf_tensor(f"p3_Vg{i}", [128, 16 * 128], BF16)) for i in range(2)]
            sg2 = EC(nc.sbuf_tensor("p3_sg2", [128, 2 * 512], BF16))
            xtl = EC(nc.sbuf_tensor("p3_xtl", [128, 2 * 512], BF16))
            zev2 = [EC(nc.sbuf_tensor(f"p3_zev2{i}", [128, 512], BF16)) for i in range(6)]
            ps_g = [EC(nc.psum_tensor(f"p3_psg{i}", [128, 512], F32)) for i in range(2)]
            ps_u = [EC(nc.psum_tensor(f"p3_psu{i}", [128, 512], F32)) for i in range(2)]
            tpsC = [EC(nc.psum_tensor(f"p3_tpC{i}", [128, 128], BF16)) for i in range(2)]
            mpsC = [EC(nc.psum_tensor(f"p3_mpC{i}", [128, 512], F32)) for i in range(2)]
            t.wait_ge(vv.h, vv.n)
            v.wait_ge(ld.h, ld.n)
            ins = None
            for j in range(64):
                ins = v.tensor_scalar(Sdh[:, j * 128:(j + 1) * 128], consts["h128_8192"][:],
                                      sdp[:, j:j + 1], None, ALU.mult)
            vv.inc(ins, 1)
            t.wait_ge(vv.h, vv.n)
            xppv = lambda kt, tc: xpp[:, kt * S_TOK + tc * 512:(kt) * S_TOK + (tc + 1) * 512]
            for kt in range(DK):
                for bl in range(8):
                    ld.inc(g.dma_start(
                        out=xpp[bl * 16:(bl + 1) * 16, kt * S_TOK:(kt + 1) * S_TOK],
                        in_=io["z2dram"][kt * 8 + bl::128, :]), 16)
            Vu_busy = [0, 0]
            Vg_busy = [0, 0]
            psg_busy = [0, 0]
            psu_busy = [0, 0]
            sg_busy = [0, 0]
            tps_busy = [0, 0]
            mps_busy = [0, 0]
            zev_busy = [0] * 6
            pending = []
            xtl_busy = [0, 0]
            t.wait_ge(ld.h, ld.n)

            def flush_pass1():
                while pending:
                    (pjg, ptc, ppar, pvv) = pending.pop(0)
                    piz = ptc % 2
                    ze = (pjg * 8 + ptc) % 6
                    t.wait_ge(vv.h, pvv)
                    if mps_busy[piz]:
                        t.wait_ge(vv.h, mps_busy[piz])
                    tT.inc(t.matmul(mpsC[piz][:], Sdh[:, pjg * 128:(pjg + 1) * 128],
                                    xtl[:, ppar * 512:(ppar + 1) * 512],
                                    start=True, stop=True), 1)
                    xtl_busy[ppar] = tT.n
                    v.wait_ge(tT.h, tT.n)
                    if zev_busy[ze]:
                        v.wait_ge(st.h, zev_busy[ze])
                    vv.inc(v.tensor_copy(zev2[ze][:], mpsC[piz][:]), 1)
                    mps_busy[piz] = vv.n
                    g.wait_ge(vv.h, vv.n)
                    st.inc(g.dma_start(out=io["zdram"][pjg * 128:(pjg + 1) * 128,
                                                       ptc * 512:(ptc + 1) * 512],
                                       in_=zev2[ze][:]), 16)
                    zev_busy[ze] = st.n

            for jg in range(64):
                par = jg % 2
                if Vu_busy[par]:
                    g.wait_ge(tT.h, Vu_busy[par])
                ld.inc(g.dma_start(out=Vu[par][:].rearrange("p (kt m) -> p kt m", m=128),
                                   in_=io["vupF"].rearrange("(kt p) m -> p kt m", p=128)
                                   [:, :, jg * 128:(jg + 1) * 128]), 16)
                ld.inc(g.dma_start(out=Vg[par][:].rearrange("p (kt m) -> p kt m", m=128),
                                   in_=io["vgtF"].rearrange("(kt p) m -> p kt m", p=128)
                                   [:, :, jg * 128:(jg + 1) * 128]), 16)
                t.wait_ge(ld.h, ld.n)
                for tc in range(8):
                    pig = tc % 2
                    xpar = tc % 2
                    if psg_busy[pig]:
                        t.wait_ge(ss.h, psg_busy[pig])
                    ins = None
                    for kt in range(DK):
                        ins = t.matmul(ps_g[pig][:], Vg[par][:, kt * 128:(kt + 1) * 128],
                                       xppv(kt, tc), start=(kt == 0), stop=(kt == DK - 1))
                    tT.inc(ins, 1)
                    s.wait_ge(tT.h, tT.n)
                    if sg_busy[xpar]:
                        s.wait_ge(vv.h, sg_busy[xpar])
                    ss.inc(s.activation(sg2[:, xpar * 512:(xpar + 1) * 512], ps_g[pig][:], AF.Silu), 1)
                    psg_busy[pig] = ss.n
                    if psu_busy[pig]:
                        t.wait_ge(vv.h, psu_busy[pig])
                    ins = None
                    for kt in range(DK):
                        ins = t.matmul(ps_u[pig][:], Vu[par][:, kt * 128:(kt + 1) * 128],
                                       xppv(kt, tc), start=(kt == 0), stop=(kt == DK - 1))
                    tT.inc(ins, 1)
                    v.wait_ge(tT.h, tT.n)
                    v.wait_ge(ss.h, ss.n)
                    if xtl_busy[xpar]:
                        v.wait_ge(tT.h, xtl_busy[xpar])
                    vv.inc(v.tensor_tensor(xtl[:, xpar * 512:(xpar + 1) * 512],
                                           sg2[:, xpar * 512:(xpar + 1) * 512],
                                           ps_u[pig][:], ALU.mult), 1)
                    psu_busy[pig] = vv.n
                    sg_busy[xpar] = vv.n
                    pending.append((jg, tc, xpar, vv.n))
                    if len(pending) >= 2:
                        (pjg, ptc, ppar, pvv) = pending.pop(0)
                        piz = ptc % 2
                        ze = (pjg * 8 + ptc) % 6
                        t.wait_ge(vv.h, pvv)
                        if mps_busy[piz]:
                            t.wait_ge(vv.h, mps_busy[piz])
                        tT.inc(t.matmul(mpsC[piz][:], Sdh[:, pjg * 128:(pjg + 1) * 128],
                                        xtl[:, ppar * 512:(ppar + 1) * 512],
                                        start=True, stop=True), 1)
                        v.wait_ge(tT.h, tT.n)
                        if zev_busy[ze]:
                            v.wait_ge(st.h, zev_busy[ze])
                        vv.inc(v.tensor_copy(zev2[ze][:], mpsC[piz][:]), 1)
                        mps_busy[piz] = vv.n
                        xtl_busy[ppar] = tT.n
                        g.wait_ge(vv.h, vv.n)
                        st.inc(g.dma_start(out=io["zdram"][pjg * 128:(pjg + 1) * 128,
                                                           ptc * 512:(ptc + 1) * 512],
                                           in_=zev2[ze][:]), 16)
                        zev_busy[ze] = st.n
                flush_pass1()
                Vu_busy[par] = tT.n
                Vg_busy[par] = tT.n

        with ExitStack() as sD:
            ED = sD.enter_context
            wnat3 = ED(nc.sbuf_tensor("p3_wnat3", [128, INTER], BF16))
            vstg = ED(nc.sbuf_tensor("p3_vstg", [128, 64 * 128], BF16))
            gbuf3 = [ED(nc.sbuf_tensor(f"p3_gbuf3{i}", [128, 512], BF16)) for i in range(4)]
            vev3 = [ED(nc.sbuf_tensor(f"p3_vev3{i}", [128, 512], BF16)) for i in range(6)]
            tpsD = [ED(nc.psum_tensor(f"p3_tpD{i}", [128, 128], BF16)) for i in range(2)]
            mpsD = [ED(nc.psum_tensor(f"p3_mpD{i}", [128, 512], F32)) for i in range(2)]
            t.wait_ge(vv.h, vv.n)
            tps_busy = [0, 0]
            for ot in range(16):
                g.wait_ge(tT.h, tT.n)
                g.wait_ge(st.h, st.n)
                ld.inc(g.dma_start(out=wnat3[:], in_=io["w_down"][ot * 128:(ot + 1) * 128, :]), 16)
                v.wait_ge(ld.h, ld.n)
                vv.inc(v.tensor_scalar(wnat3[:], wnat3[:], wsd[:, ot:ot + 1], None, ALU.mult), 1)
                t.wait_ge(vv.h, vv.n)
                for kt in range(64):
                    pi = kt % 2
                    if tps_busy[pi]:
                        t.wait_ge(vv.h, tps_busy[pi])
                    tT.inc(t.transpose(tpsD[pi][:], wnat3[:, kt * 128:(kt + 1) * 128], ident[:]), 1)
                    v.wait_ge(tT.h, tT.n)
                    vv.inc(v.tensor_copy(vstg[:, kt * 128:(kt + 1) * 128], tpsD[pi][:]), 1)
                    tps_busy[pi] = vv.n
                g.wait_ge(vv.h, vv.n)
                st.inc(g.dma_start(
                    out=io["wdraft2"].rearrange("(kt p) o -> p kt o", p=128)[:, :, ot * 128:(ot + 1) * 128],
                    in_=vstg[:].rearrange("p (kt o) -> p kt o", o=128)), 16)
            g.wait_ge(st.h, st.n)
            gb_busy = [0] * 4
            mps_busy2 = [0, 0]
            vev_busy = [0] * 6
            for kt in range(64):
                for cchunk in range(4):
                    it = kt * 4 + cchunk
                    pi = it % 2
                    gb = it % 4
                    ve = it % 6
                    if gb_busy[gb]:
                        g.wait_ge(tT.h, gb_busy[gb])
                    for bl in range(2):
                        ld.inc(g.dma_start(
                            out=gbuf3[gb][bl * 64:(bl + 1) * 64, :],
                            in_=io["wdraft2"][kt * 2 + bl::128, cchunk * 512:(cchunk + 1) * 512]), 16)
                    t.wait_ge(ld.h, ld.n)
                    if mps_busy2[pi]:
                        t.wait_ge(vv.h, mps_busy2[pi])
                    tT.inc(t.matmul(mpsD[pi][:], consts["i2h64"][:], gbuf3[gb][:],
                                    start=True, stop=True), 1)
                    gb_busy[gb] = tT.n
                    v.wait_ge(tT.h, tT.n)
                    if vev_busy[ve]:
                        v.wait_ge(st.h, vev_busy[ve])
                    vv.inc(v.tensor_copy(vev3[ve][:], mpsD[pi][:]), 1)
                    mps_busy2[pi] = vv.n
                    g.wait_ge(vv.h, vv.n)
                    st.inc(g.dma_start(out=io["vdown"][kt * 128:(kt + 1) * 128,
                                                       cchunk * 512:(cchunk + 1) * 512],
                                       in_=vev3[ve][:]), 16)
                    vev_busy[ve] = st.n

        with ExitStack() as sE:
            EE = sE.enter_context
            xdd = EE(nc.sbuf_tensor("p3_xdd", [128, 64 * 512], BF16))
            voc = [EE(nc.sbuf_tensor(f"p3_voc{i}", [128, 64 * 256], BF16)) for i in range(2)]
            x1tl = EE(nc.sbuf_tensor("p3_x1tl", [128, 4 * D], BF16))
            xpr = EE(nc.sbuf_tensor("p3_xpr", [128, 4 * D], BF16))
            x2tl = EE(nc.sbuf_tensor("p3_x2t", [128, D], F32))
            cols = EE(nc.sbuf_tensor("p3_cols", [128, 8], F32))
            ps_d = [EE(nc.psum_tensor(f"p3_psd{i}", [128, 256], F32)) for i in range(4)]

            psd_busy = [0] * 4
            voc_busy = [0, 0]
            x1_busy = 0
            x2_busy = 0
            xdd_busy = 0

            for tb in range(8):
                if xdd_busy:
                    g.wait_ge(tT.h, xdd_busy)
                for kt in range(64):
                    for bl in range(2):
                        ld.inc(g.dma_start(
                            out=xdd[bl * 64:(bl + 1) * 64, kt * 512:(kt + 1) * 512],
                            in_=io["zdram"][kt * 2 + bl::128, tb * 512:(tb + 1) * 512]), 16)
                if x1_busy:
                    g.wait_ge(vv.h, x1_busy)
                ld.inc(g.dma_start(out=x1tl[:].rearrange("p (a d) -> p a d", a=4),
                                   in_=io["x1"][tb * 512:(tb + 1) * 512, :].rearrange("(a p) d -> p a d", p=128)), 16)
                t.wait_ge(ld.h, ld.n)
                for oc in range(8):
                    vpar = oc % 2
                    if voc_busy[vpar]:
                        g.wait_ge(tT.h, voc_busy[vpar])
                    ld.inc(g.dma_start(
                        out=voc[vpar][:].rearrange("p (k o) -> p k o", o=256),
                        in_=io["vdown"][:, oc * 256:(oc + 1) * 256].rearrange("(k p) o -> p k o", p=128)), 16)
                    t.wait_ge(ld.h, ld.n)
                    for ts_ in range(4):
                        pid = (oc * 4 + ts_) % 4
                        if psd_busy[pid]:
                            t.wait_ge(vv.h, psd_busy[pid])
                        ins = None
                        for kt in range(64):
                            ins = t.matmul(ps_d[pid][:],
                                           xdd[:, kt * 512 + ts_ * 128:(kt) * 512 + (ts_ + 1) * 128],
                                           voc[vpar][:, kt * 256:(kt + 1) * 256],
                                           start=(kt == 0), stop=(kt == 63))
                        tT.inc(ins, 1)
                        v.wait_ge(tT.h, tT.n)
                        v.wait_ge(ld.h, ld.n)
                        vv.inc(v.tensor_tensor(xpr[:, ts_ * D + oc * 256: ts_ * D + (oc + 1) * 256],
                                               ps_d[pid][:],
                                               x1tl[:, ts_ * D + oc * 256: ts_ * D + (oc + 1) * 256],
                                               ALU.add), 1)
                        psd_busy[pid] = vv.n
                    voc_busy[vpar] = tT.n
                x1_busy = vv.n
                xdd_busy = tT.n
                for ts_ in range(4):
                    ti = tb * 4 + ts_
                    if x2_busy:
                        v.wait_ge(st.h, x2_busy)
                    xp = xpr[:, ts_ * D:(ts_ + 1) * D]
                    vv.inc(v.tensor_reduce(cols[:, 0:1], xp, mybir.AxisListType.X, ALU.add), 1)
                    v.wait_ge(vv.h, vv.n)
                    v.scalar_tensor_tensor(x2tl[:], cols[:, 0:1].broadcast_to([128, D]), -1.0 / D,
                                           xp, ALU.mult, ALU.add)
                    v.tensor_tensor(xp, x2tl[:], x2tl[:], ALU.mult)
                    vv.inc(v.tensor_reduce(cols[:, 2:3], xp, mybir.AxisListType.X, ALU.add), 1)
                    s.wait_ge(vv.h, vv.n)
                    ss.inc(emit_rstd(s, consts, cols[:, 2:3], cols[:, 3:4], 1.0 / D, cols[:, 4:5]), 1)
                    v.wait_ge(ss.h, ss.n)
                    vv.inc(v.tensor_scalar(x2tl[:], x2tl[:], cols[:, 3:4], None, ALU.mult), 1)
                    g.wait_ge(vv.h, vv.n)
                    st.inc(g.dma_start(out=io["x2"][ti * 128:(ti + 1) * 128, :], in_=x2tl[:]), 16)
                    x2_busy = st.n
        g.wait_ge(st.h, st.n)
        sy.wait_ge(st.h, st.n)
        return {"vv": (vv, vv.n), "ss": (ss, ss.n), "tT": (tT, tT.n), "st": (st, st.n)}


def build_p3():
    nc = bass.Bass("TRN2", target_bir_lowering=False)
    io = {}
    io["x1"] = nc.dram_tensor("x1", [S_TOK, D], F32, kind="ExternalInput").ap()
    io["su_ug"] = nc.dram_tensor("su_ug", [1, D], F32, kind="ExternalInput").ap()
    io["su_dp"] = nc.dram_tensor("su_dp", [1, INTER], F32, kind="ExternalInput").ap()
    io["wsc_ug"] = nc.dram_tensor("wsc_ug", [1, 2 * INTER], F32, kind="ExternalInput").ap()
    io["wsc_dp"] = nc.dram_tensor("wsc_dp", [1, D], F32, kind="ExternalInput").ap()
    io["w_up"] = nc.dram_tensor("w_up", [INTER, D], F32, kind="ExternalInput").ap()
    io["w_gate"] = nc.dram_tensor("w_gate", [INTER, D], F32, kind="ExternalInput").ap()
    io["w_down"] = nc.dram_tensor("w_down", [D, INTER], F32, kind="ExternalInput").ap()
    io["x2"] = nc.dram_tensor("x2", [S_TOK, D], F32, kind="ExternalOutput").ap()
    io["zdram"] = nc.dram_tensor("zdram", [INTER, S_TOK], BF16, kind="Internal").ap()
    io["vdown"] = nc.dram_tensor("vdown", [INTER, D], BF16, kind="Internal").ap()
    io["z2dram"] = nc.dram_tensor("z2dram", [D, S_TOK], BF16, kind="Internal").ap()
    io["wdraft"] = nc.dram_tensor("wdraft", [D, INTER], BF16, kind="Internal").ap()
    io["wdraft2"] = nc.dram_tensor("wdraft2", [INTER, D], BF16, kind="Internal").ap()
    io["vupF"] = nc.dram_tensor("vupF", [D, INTER], BF16, kind="Internal").ap()
    io["vgtF"] = nc.dram_tensor("vgtF", [D, INTER], BF16, kind="Internal").ap()
    with ExitStack() as stack:
        ctx = Ctx(nc, stack)
        eng = {"t": nc.tensor, "v": nc.vector, "s": nc.scalar, "g": nc.gpsimd, "sy": nc.sync}
        ld0 = ctx.sem("c_ld")
        consts = load_consts(ctx, nc.gpsimd, ld0, keys=("ident", "h128_2048", "h128_8192", "i8h16", "i2h64", "csts"))
        for e in (nc.tensor, nc.vector, nc.scalar, nc.sync):
            e.wait_ge(ld0.h, ld0.n)
        phase3(ctx, io, consts, eng)
    return nc



def phase4(ctx, io, consts, eng, entry=None):
    nc, t, v, s, g, sy = ctx.nc, eng["t"], eng["v"], eng["s"], eng["g"], eng["sy"]
    ident = consts["ident"]
    NQC = 8
    with ExitStack() as ph:
        E = ph.enter_context
        ld = ctx.sem("p4_ld")
        tT = ctx.sem("p4_tT")
        vv = ctx.sem("p4_vv")
        ss = ctx.sem("p4_ss")
        st = ctx.sem("p4_st")
        if entry:
            for e in (t, v, s, g, sy):
                for key, (sem, val) in entry.items():
                    e.wait_ge(sem.h, val)

        with ExitStack() as sA:
            EA = sA.enter_context
            x2ch = EA(nc.sbuf_tensor("p4_x2ch", [128, DK * S_TOK], BF16))
            xt = EA(nc.sbuf_tensor("p4_xt", [128, D], BF16))
            wqm = [EA(nc.sbuf_tensor(f"p4_wqm{i}", [128, 16 * 128], BF16)) for i in range(2)]
            wvcb = [EA(nc.sbuf_tensor(f"p4_wvcb{i}", [128, 16 * 512], BF16)) for i in range(2)]
            ev = [EA(nc.sbuf_tensor(f"p4_ev{i}", [128, 512], BF16)) for i in range(6)]
            tps = [EA(nc.psum_tensor(f"p4_tp{i}", [128, 128], BF16)) for i in range(2)]
            mps = [EA(nc.psum_tensor(f"p4_mp{i}", [128, 512], F32)) for i in range(4)]

            tps_busy = [0, 0]
            mps_busy = [0] * 4
            ev_busy = [0] * 6
            xt_busy = 0
            xch_view = lambda kt, c0, w: x2ch[:, kt * S_TOK + c0: kt * S_TOK + c0 + w]

            for ti in range(NT):
                if xt_busy:
                    g.wait_ge(tT.h, xt_busy)
                ld.inc(g.dma_start(out=xt[:], in_=io["x2"][ti * 128:(ti + 1) * 128, :]), 16)
                t.wait_ge(ld.h, ld.n)
                for kt in range(DK):
                    pi = kt % 2
                    if tps_busy[pi]:
                        t.wait_ge(vv.h, tps_busy[pi])
                    tT.inc(t.transpose(tps[pi][:], xt[:, kt * 128:(kt + 1) * 128], ident[:]), 1)
                    v.wait_ge(tT.h, tT.n)
                    vv.inc(v.tensor_copy(xch_view(kt, ti * 128, 128), tps[pi][:]), 1)
                    tps_busy[pi] = vv.n
                xt_busy = tT.n
            t.wait_ge(vv.h, vv.n)
            for (wdram, outdram) in ((io["wq"], io["qT"]), (io["wk"], io["kT"])):
                wq_busy = [0, 0]
                for m in range(16):
                    par = m % 2
                    if wq_busy[par]:
                        g.wait_ge(tT.h, wq_busy[par])
                    ld.inc(g.dma_start(out=wqm[par][:].rearrange("p (k c) -> p k c", c=128),
                                       in_=wdram[:, m * 128:(m + 1) * 128]
                                       .rearrange("(k p) c -> p k c", p=128)), 16)
                    t.wait_ge(ld.h, ld.n)
                    for tc in range(NQC):
                        pi = tc % 4
                        if mps_busy[pi]:
                            t.wait_ge(vv.h, mps_busy[pi])
                        ins = None
                        for kt in range(DK):
                            ins = t.matmul(mps[pi][:], wqm[par][:, kt * 128:(kt + 1) * 128],
                                           xch_view(kt, tc * 512, 512),
                                           start=(kt == 0), stop=(kt == DK - 1))
                        tT.inc(ins, 1)
                        pe = tc % 6
                        v.wait_ge(tT.h, tT.n)
                        if ev_busy[pe]:
                            v.wait_ge(st.h, ev_busy[pe])
                        vv.inc(v.tensor_copy(ev[pe][:], mps[pi][:]), 1)
                        mps_busy[pi] = vv.n
                        g.wait_ge(vv.h, vv.n)
                        st.inc(g.dma_start(out=outdram[m * 128:(m + 1) * 128, tc * 512:(tc + 1) * 512],
                                           in_=ev[pe][:]), 16)
                        ev_busy[pe] = st.n
                    wq_busy[par] = tT.n
            wv_busy = [0, 0]
            for oc in range(4):
                par = oc % 2
                if wv_busy[par]:
                    g.wait_ge(tT.h, wv_busy[par])
                ld.inc(g.dma_start(out=wvcb[par][:].rearrange("p (k c) -> p k c", c=512),
                                   in_=io["wv"][:, oc * 512:(oc + 1) * 512]
                                   .rearrange("(k p) c -> p k c", p=128)), 16)
                t.wait_ge(ld.h, ld.n)
                for ti in range(NT):
                    pi = ti % 4
                    if mps_busy[pi]:
                        t.wait_ge(vv.h, mps_busy[pi])
                    ins = None
                    for kt in range(DK):
                        ins = t.matmul(mps[pi][:], xch_view(kt, ti * 128, 128),
                                       wvcb[par][:, kt * 512:(kt + 1) * 512],
                                       start=(kt == 0), stop=(kt == DK - 1))
                    tT.inc(ins, 1)
                    pe = ti % 6
                    v.wait_ge(tT.h, tT.n)
                    if ev_busy[pe]:
                        v.wait_ge(st.h, ev_busy[pe])
                    vv.inc(v.tensor_copy(ev[pe][:], mps[pi][:]), 1)
                    mps_busy[pi] = vv.n
                    g.wait_ge(vv.h, vv.n)
                    st.inc(g.dma_start(out=io["vtok"][ti * 128:(ti + 1) * 128, oc * 512:(oc + 1) * 512],
                                       in_=ev[pe][:]), 16)
                    ev_busy[pe] = st.n
                wv_busy[par] = tT.n

        with ExitStack() as sB:
            EB = sB.enter_context
            qTh = EB(nc.sbuf_tensor("p4_qTh", [128, S_TOK], BF16))
            kTh = EB(nc.sbuf_tensor("p4_kTh", [128, S_TOK], BF16))
            vh = EB(nc.sbuf_tensor("p4_vh", [128, 32 * 128], BF16))
            Ebuf = EB(nc.sbuf_tensor("p4_E", [128, 36 * 512], BF16))
            vsufT = EB(nc.sbuf_tensor("p4_vsufT", [128, S_TOK], BF16))
            bsums = EB(nc.sbuf_tensor("p4_bs", [1, 32 * 128], BF16))
            carr4 = EB(nc.sbuf_tensor("p4_carr", [1, 32 * 128], BF16))
            denr = EB(nc.sbuf_tensor("p4_denr", [1, 1024], F32))
            recb = EB(nc.sbuf_tensor("p4_recb", [1, 512], BF16))
            oev = [EB(nc.sbuf_tensor(f"p4_oev{i}", [128, 512], BF16)) for i in range(2)]
            wrk = EB(nc.sbuf_tensor("p4_wrk", [128, 512], F32))
            cnt_sb = EB(nc.sbuf_tensor("p4_cnt", [1, S_TOK], F32))
            ps_sc = [EB(nc.psum_tensor(f"p4_psc{i}", [128, 512], F32)) for i in range(2)]
            ps_av = EB(nc.psum_tensor("p4_psav", [128, 512], F32))
            ps_dn = EB(nc.psum_tensor("p4_psdn", [1, 512], F32))
            ps_vs = EB(nc.psum_tensor("p4_psvs", [128, 128], F32))
            ps_bc = EB(nc.psum_tensor("p4_psbc", [128, 512], F32))
            ps_b1 = EB(nc.psum_tensor("p4_psb1", [1, 128], F32))

            t.wait_ge(vv.h, vv.n)
            ld.inc(g.dma_start(out=cnt_sb[:], in_=io["cntrow"]), 16)
            qk_busy = 0
            Efree = [0] * 2
            E_busy = 0
            av_busy = 0
            vs_busy = 0
            bc_busy = 0
            oev_busy = [0, 0]
            dn_busy = 0
            b1_busy = 0

            for h in range(16):
                if qk_busy:
                    g.wait_ge(tT.h, qk_busy)
                ld.inc(g.dma_start(out=qTh[:], in_=io["qT"][h * 128:(h + 1) * 128, :]), 16)
                ld.inc(g.dma_start(out=kTh[:], in_=io["kT"][h * 128:(h + 1) * 128, :]), 16)
                ld.inc(g.dma_start(out=vh[:].rearrange("p (k c) -> p k c", c=128),
                                   in_=io["vtok"][:, h * 128:(h + 1) * 128]
                                   .rearrange("(k p) c -> p k c", p=128)), 16)
                t.wait_ge(ld.h, ld.n)
                for kb in range(NT):
                    if b1_busy:
                        t.wait_ge(vv.h, b1_busy)
                    tT.inc(t.matmul(ps_b1[:], consts["ones_col_bf"][:], vh[:, kb * 128:(kb + 1) * 128],
                                    start=True, stop=True), 1)
                    v.wait_ge(tT.h, tT.n)
                    vv.inc(v.tensor_copy(bsums[:, kb * 128:(kb + 1) * 128], ps_b1[:]), 1)
                    b1_busy = vv.n
                v.tensor_copy(carr4[:, 30 * 128:31 * 128], bsums[:, 31 * 128:32 * 128])
                ins = None
                for kb in range(29, -1, -1):
                    ins = v.tensor_tensor(carr4[:, kb * 128:(kb + 1) * 128],
                                          carr4[:, (kb + 1) * 128:(kb + 2) * 128],
                                          bsums[:, (kb + 1) * 128:(kb + 2) * 128], ALU.add)
                vv.inc(ins, 1)
                t.wait_ge(vv.h, vv.n)
                for kb in range(NT):
                    if vs_busy:
                        t.wait_ge(vv.h, vs_busy)
                    if kb < 31:
                        t.matmul(ps_vs[:], vh[:, kb * 128:(kb + 1) * 128], consts["slower_bf"][:],
                                 start=True, stop=False, skip_group_check=True)
                        ins = t.matmul(ps_vs[:], carr4[0:1, kb * 128:(kb + 1) * 128],
                                       consts["ones_row_bf"][:], start=False, stop=True,
                                       skip_group_check=True)
                    else:
                        ins = t.matmul(ps_vs[:], vh[:, kb * 128:(kb + 1) * 128], consts["slower_bf"][:],
                                       start=True, stop=True)
                    tT.inc(ins, 1)
                    v.wait_ge(tT.h, tT.n)
                    vv.inc(v.tensor_copy(vsufT[:, kb * 128:(kb + 1) * 128], ps_vs[:]), 1)
                    vs_busy = vv.n
                for qc in range(NQC):
                    nkb = 4 * (qc + 1)
                    if E_busy:
                        s.wait_ge(tT.h, E_busy)
                    for kb in range(nkb):
                        pi = kb % 2
                        if Efree[pi]:
                            t.wait_ge(ss.h, Efree[pi])
                        tT.inc(t.matmul(ps_sc[pi][:], kTh[:, kb * 128:(kb + 1) * 128],
                                        qTh[:, qc * 512:(qc + 1) * 512], start=True, stop=True), 1)
                        s.wait_ge(tT.h, tT.n)
                        ss.inc(s.activation(Ebuf[:, kb * 512:(kb + 1) * 512], ps_sc[pi][:],
                                            AF.Exp, scale=1.0 / 64.0), 1)
                        Efree[pi] = ss.n
                    v.wait_ge(ss.h, ss.n)
                    ins = None
                    for o in range(4):
                        kb = qc * 4 + o
                        ins = v.tensor_tensor(Ebuf[:, kb * 512:(kb + 1) * 512],
                                              Ebuf[:, kb * 512:(kb + 1) * 512],
                                              consts["qsmask"][:, o * 512:(o + 1) * 512], ALU.mult)
                    vv.inc(ins, 1)
                    maskdone = vv.n
                    t.wait_ge(vv.h, maskdone)
                    if av_busy:
                        t.wait_ge(vv.h, av_busy)
                    for kb in range(nkb):
                        tT.inc(t.matmul(ps_av[:], vh[:, kb * 128:(kb + 1) * 128],
                                        Ebuf[:, kb * 512:(kb + 1) * 512],
                                        start=(kb == 0), stop=(kb == nkb - 1)), 1)
                    if dn_busy:
                        t.wait_ge(vv.h, dn_busy)
                    for kb in range(nkb):
                        tT.inc(t.matmul(ps_dn[:], consts["ones_col_bf"][:],
                                        Ebuf[:, kb * 512:(kb + 1) * 512],
                                        start=(kb == 0), stop=(kb == nkb - 1)), 1)
                    E_busy = tT.n
                    v.wait_ge(tT.h, tT.n)
                    v.tensor_tensor(denr[:, 0:512], ps_dn[:], cnt_sb[:, qc * 512:(qc + 1) * 512], ALU.add)
                    dn_busy = vv.n + 2
                    vv.inc(v.reciprocal(denr[:, 512:1024], denr[:, 0:512]), 1)
                    v.wait_ge(vv.h, vv.n)
                    vv.inc(v.tensor_copy(recb[:], denr[:, 512:1024]), 1)
                    t.wait_ge(vv.h, vv.n)
                    if bc_busy:
                        t.wait_ge(vv.h, bc_busy)
                    tT.inc(t.matmul(ps_bc[:], consts["ones_row_bf"][:], recb[:],
                                    start=True, stop=True), 1)
                    v.wait_ge(tT.h, tT.n)
                    v.tensor_tensor(wrk[:], ps_av[:], vsufT[:, qc * 512:(qc + 1) * 512], ALU.add)
                    av_busy = vv.n + 1
                    pe = qc % 2
                    if oev_busy[pe]:
                        v.wait_ge(st.h, oev_busy[pe])
                    vv.inc(v.tensor_tensor(oev[pe][:], wrk[:], ps_bc[:], ALU.mult), 1)
                    bc_busy = vv.n
                    g.wait_ge(vv.h, vv.n)
                    st.inc(g.dma_start(out=io["attnT"][h * 128:(h + 1) * 128, qc * 512:(qc + 1) * 512],
                                       in_=oev[pe][:]), 16)
                    oev_busy[pe] = st.n
                qk_busy = tT.n

        with ExitStack() as sC:
            EC = sC.enter_context
            aT = EC(nc.sbuf_tensor("p4_aT", [128, 16 * 128], BF16))
            wof = EC(nc.sbuf_tensor("p4_wof", [128, 16 * D], BF16))
            x2tl = EC(nc.sbuf_tensor("p4_x2tl", [128, D], F32))
            xpr = EC(nc.sbuf_tensor("p4_xpr", [128, D], F32))
            x3tl = EC(nc.sbuf_tensor("p4_x3t", [128, D], F32))
            cols = EC(nc.sbuf_tensor("p4_cols", [128, 8], F32))
            mps4 = [EC(nc.psum_tensor(f"p4_mp4{i}", [128, 512], F32)) for i in range(4)]
            t.wait_ge(vv.h, vv.n)
            ld.inc(g.dma_start(out=wof[:].rearrange("p (k c) -> p k c", c=D),
                               in_=io["wo"].rearrange("(k p) c -> p k c", p=128)), 16)
            mps_busy = [0] * 4
            aT_busy = 0
            x2_busy = 0
            x3_busy = 0
            for ti in range(NT):
                if aT_busy:
                    g.wait_ge(tT.h, aT_busy)
                ld.inc(g.dma_start(out=aT[:].rearrange("p (k c) -> p k c", c=128),
                                   in_=io["attnT"][:, ti * 128:(ti + 1) * 128]
                                   .rearrange("(k p) c -> p k c", p=128)), 16)
                if x2_busy:
                    g.wait_ge(vv.h, x2_busy)
                ld.inc(g.dma_start(out=x2tl[:], in_=io["x2"][ti * 128:(ti + 1) * 128, :]), 16)
                t.wait_ge(ld.h, ld.n)
                for oc in range(4):
                    pi = oc % 4
                    if mps_busy[pi]:
                        t.wait_ge(vv.h, mps_busy[pi])
                    ins = None
                    for kt in range(DK):
                        ins = t.matmul(mps4[pi][:], aT[:, kt * 128:(kt + 1) * 128],
                                       wof[:, kt * D + oc * 512: kt * D + (oc + 1) * 512],
                                       start=(kt == 0), stop=(kt == DK - 1))
                    tT.inc(ins, 1)
                    v.wait_ge(tT.h, tT.n)
                    vv.inc(v.tensor_tensor(xpr[:, oc * 512:(oc + 1) * 512], mps4[pi][:],
                                           x2tl[:, oc * 512:(oc + 1) * 512], ALU.add), 1)
                    mps_busy[pi] = vv.n
                aT_busy = tT.n
                x2_busy = vv.n
                if x3_busy:
                    v.wait_ge(st.h, x3_busy)
                vv.inc(v.tensor_reduce(cols[:, 0:1], xpr[:], mybir.AxisListType.X, ALU.add), 1)
                v.wait_ge(vv.h, vv.n)
                v.scalar_tensor_tensor(x3tl[:], cols[:, 0:1].broadcast_to([128, D]), -1.0 / D,
                                       xpr[:], ALU.mult, ALU.add)
                v.tensor_tensor(xpr[:], x3tl[:], x3tl[:], ALU.mult)
                vv.inc(v.tensor_reduce(cols[:, 2:3], xpr[:], mybir.AxisListType.X, ALU.add), 1)
                s.wait_ge(vv.h, vv.n)
                ss.inc(emit_rstd(s, consts, cols[:, 2:3], cols[:, 3:4], 1.0 / D, cols[:, 4:5]), 1)
                v.wait_ge(ss.h, ss.n)
                vv.inc(v.tensor_scalar(x3tl[:], x3tl[:], cols[:, 3:4], None, ALU.mult), 1)
                g.wait_ge(vv.h, vv.n)
                st.inc(g.dma_start(out=io["x3"][ti * 128:(ti + 1) * 128, :], in_=x3tl[:]), 16)
                x3_busy = st.n
        g.wait_ge(st.h, st.n)
        sy.wait_ge(st.h, st.n)
        return {"vv": (vv, vv.n), "ss": (ss, ss.n), "tT": (tT, tT.n), "st": (st, st.n)}


def build_p4():
    nc = bass.Bass("TRN2", target_bir_lowering=False)
    io = {}
    io["x2"] = nc.dram_tensor("x2", [S_TOK, D], F32, kind="ExternalInput").ap()
    io["wq"] = nc.dram_tensor("wq", [D, D], F32, kind="ExternalInput").ap()
    io["wk"] = nc.dram_tensor("wk", [D, D], F32, kind="ExternalInput").ap()
    io["wv"] = nc.dram_tensor("wv", [D, D], F32, kind="ExternalInput").ap()
    io["wo"] = nc.dram_tensor("wo", [D, D], F32, kind="ExternalInput").ap()
    io["cntrow"] = nc.dram_tensor("cntrow", [1, S_TOK], F32, kind="ExternalInput").ap()
    io["x3"] = nc.dram_tensor("x3", [S_TOK, D], F32, kind="ExternalOutput").ap()
    io["qT"] = nc.dram_tensor("qT", [D, S_TOK], BF16, kind="Internal").ap()
    io["kT"] = nc.dram_tensor("kT", [D, S_TOK], BF16, kind="Internal").ap()
    io["vtok"] = nc.dram_tensor("vtok", [S_TOK, D], BF16, kind="Internal").ap()
    io["attnT"] = nc.dram_tensor("attnT", [D, S_TOK], BF16, kind="Internal").ap()
    with ExitStack() as stack:
        ctx = Ctx(nc, stack)
        eng = {"t": nc.tensor, "v": nc.vector, "s": nc.scalar, "g": nc.gpsimd, "sy": nc.sync}
        ld0 = ctx.sem("c_ld")
        consts = load_consts(ctx, nc.gpsimd, ld0, keys=("ident", "ones_col_bf", "ones_row_bf", "slower_bf", "qsmask", "csts"))
        for e in (nc.tensor, nc.vector, nc.scalar, nc.sync):
            e.wait_ge(ld0.h, ld0.n)
        phase4(ctx, io, consts, eng)
    return nc

def phase5(ctx, io, consts, eng, entry=None):
    nc, t, v, s, g, sy = ctx.nc, eng["t"], eng["v"], eng["s"], eng["g"], eng["sy"]
    ident = consts["ident"]
    with ExitStack() as ph:
        E = ph.enter_context
        ld = ctx.sem("p5_ld")
        tT = ctx.sem("p5_tT")
        vv = ctx.sem("p5_vv")
        ss = ctx.sem("p5_ss")
        st = ctx.sem("p5_st")
        if entry:
            for e in (t, v, s, g, sy):
                for key, (sem, val) in entry.items():
                    e.wait_ge(sem.h, val)

        with ExitStack() as sA:
            EA = sA.enter_context
            atb = EA(nc.sbuf_tensor("p5_atb", [16, D], BF16))
            bub = EA(nc.sbuf_tensor("p5_bub", [16, INTER], BF16))
            adb = EA(nc.sbuf_tensor("p5_adb", [16, INTER], BF16))
            bdb = EA(nc.sbuf_tensor("p5_bdb", [16, D], BF16))
            aload = EA(nc.sbuf_tensor("p5_aload", [128, 80 * 16], BF16))
            qi = EA(nc.sbuf_tensor("p5_qi", [128, INTER], I32))
            dq = EA(nc.sbuf_tensor("p5_dq", [128, 4096], F32))
            dqb = EA(nc.sbuf_tensor("p5_dqb", [128, INTER], BF16))
            ssb = EA(nc.sbuf_tensor("p5_ssb", [128, 128], F32))
            stg = EA(nc.sbuf_tensor("p5_stg", [128, 64 * 128], BF16))
            tps = [EA(nc.psum_tensor(f"p5_tp{i}", [128, 128], BF16)) for i in range(2)]
            abp = [EA(nc.psum_tensor(f"p5_ab{i}", [128, 128], F32)) for i in range(2)]

            ld.inc(g.dma_start(out=aload[:, 0:16 * 16].rearrange("p (k r) -> p k r", r=16),
                               in_=io["alup"].rearrange("(k p) r -> p k r", p=128)), 16)
            ld.inc(g.dma_start(out=bub[:], in_=io["blup"]), 16)
            ld.inc(g.dma_start(out=aload[:, 256:256 + 64 * 16].rearrange("p (k r) -> p k r", r=16),
                               in_=io["aldn"].rearrange("(k p) r -> p k r", p=128)), 16)
            ld.inc(g.dma_start(out=bdb[:], in_=io["bldn"]), 16)
            t.wait_ge(ld.h, ld.n)
            tps_busy = [0, 0]
            for kt in range(16):
                pi = kt % 2
                if tps_busy[pi]:
                    t.wait_ge(vv.h, tps_busy[pi])
                tT.inc(t.transpose(tps[pi][0:16, :], aload[:, kt * 16:(kt + 1) * 16], ident[:]), 1)
                v.wait_ge(tT.h, tT.n)
                vv.inc(v.tensor_copy(atb[:, kt * 128:(kt + 1) * 128], tps[pi][0:16, :]), 1)
                tps_busy[pi] = vv.n
            for kt in range(64):
                pi = kt % 2
                if tps_busy[pi]:
                    t.wait_ge(vv.h, tps_busy[pi])
                tT.inc(t.transpose(tps[pi][0:16, :], aload[:, 256 + kt * 16:256 + (kt + 1) * 16],
                                   ident[:]), 1)
                v.wait_ge(tT.h, tT.n)
                vv.inc(v.tensor_copy(adb[:, kt * 128:(kt + 1) * 128], tps[pi][0:16, :]), 1)
                tps_busy[pi] = vv.n

            ab_busy = [0, 0]
            stg_st = 0
            qi_busy = 0
            dqb_busy = 0
            for ht in range(64):
                if qi_busy:
                    g.wait_ge(vv.h, qi_busy)
                ld.inc(g.dma_start(out=qi[:, 0:D], in_=io["wupq"][ht * 128:(ht + 1) * 128, :]), 16)
                ld.inc(g.dma_start(out=ssb[:, 0:32], in_=io["wups"][ht * 128:(ht + 1) * 128, :]), 16)
                v.wait_ge(ld.h, ld.n)
                if dqb_busy:
                    v.wait_ge(tT.h, dqb_busy)
                v.tensor_scalar(dq[:, 0:D], qi[:, 0:D], -7.5, None, ALU.add)
                vv.inc(v.tensor_tensor(dqb[:, 0:D].rearrange("p (a c) -> p a c", c=64),
                                       dq[:, 0:D].rearrange("p (a c) -> p a c", c=64),
                                       rep_inner(ssb[:, 0:32], 32, 64), ALU.mult), 1)
                qi_busy = vv.n
                t.wait_ge(vv.h, vv.n)
                if stg_st:
                    v.wait_ge(st.h, stg_st)
                for dt in range(16):
                    pi = dt % 2
                    if tps_busy[pi]:
                        t.wait_ge(vv.h, tps_busy[pi])
                    tT.inc(t.transpose(tps[pi][:], dqb[:, dt * 128:(dt + 1) * 128], ident[:]), 1)
                    v.wait_ge(tT.h, tT.n)
                    vv.inc(v.tensor_copy(stg[:, dt * 128:(dt + 1) * 128], tps[pi][:]), 1)
                    tps_busy[pi] = vv.n
                    if ab_busy[pi]:
                        t.wait_ge(vv.h, ab_busy[pi])
                    tT.inc(t.matmul(abp[pi][:], atb[:, dt * 128:(dt + 1) * 128],
                                    bub[:, ht * 128:(ht + 1) * 128], start=True, stop=True), 1)
                    v.wait_ge(tT.h, tT.n)
                    v.wait_ge(vv.h, vv.n)
                    vv.inc(v.tensor_tensor(stg[:, dt * 128:(dt + 1) * 128], abp[pi][:],
                                           stg[:, dt * 128:(dt + 1) * 128], ALU.add), 1)
                    ab_busy[pi] = vv.n
                dqb_busy = tT.n
                g.wait_ge(vv.h, vv.n)
                st.inc(g.dma_start(
                    out=io["weffu"].rearrange("(kt p) c -> p kt c", p=128)[:, 0:16, ht * 128:(ht + 1) * 128],
                    in_=stg[:, 0:16 * 128].rearrange("p (kt c) -> p kt c", c=128)), 16)
                stg_st = st.n

            for dt in range(16):
                for hh in range(2):
                    ld.inc(g.dma_start(out=qi[:, hh * 4096:(hh + 1) * 4096],
                                       in_=io["wdnq"][dt * 128:(dt + 1) * 128,
                                                      hh * 4096:(hh + 1) * 4096]), 16)
                ld.inc(g.dma_start(out=ssb[:], in_=io["wdns"][dt * 128:(dt + 1) * 128, :]), 16)
                v.wait_ge(ld.h, ld.n)
                if stg_st:
                    v.wait_ge(st.h, stg_st)
                if dqb_busy:
                    v.wait_ge(tT.h, dqb_busy)
                for hh in range(2):
                    v.tensor_scalar(dq[:], qi[:, hh * 4096:(hh + 1) * 4096], -7.5, None, ALU.add)
                    vv.inc(v.tensor_tensor(
                        dqb[:, hh * 4096:(hh + 1) * 4096].rearrange("p (a c) -> p a c", c=64),
                        dq[:].rearrange("p (a c) -> p a c", c=64),
                        rep_inner(ssb[:, hh * 64:(hh + 1) * 64], 64, 64), ALU.mult), 1)
                t.wait_ge(vv.h, vv.n)
                for ht in range(64):
                    pi = ht % 2
                    if tps_busy[pi]:
                        t.wait_ge(vv.h, tps_busy[pi])
                    tT.inc(t.transpose(tps[pi][:], dqb[:, ht * 128:(ht + 1) * 128], ident[:]), 1)
                    v.wait_ge(tT.h, tT.n)
                    vv.inc(v.tensor_copy(stg[:, ht * 128:(ht + 1) * 128], tps[pi][:]), 1)
                    tps_busy[pi] = vv.n
                    if ab_busy[pi]:
                        t.wait_ge(vv.h, ab_busy[pi])
                    tT.inc(t.matmul(abp[pi][:], adb[:, ht * 128:(ht + 1) * 128],
                                    bdb[:, dt * 128:(dt + 1) * 128], start=True, stop=True), 1)
                    v.wait_ge(tT.h, tT.n)
                    v.wait_ge(vv.h, vv.n)
                    vv.inc(v.tensor_tensor(stg[:, ht * 128:(ht + 1) * 128], abp[pi][:],
                                           stg[:, ht * 128:(ht + 1) * 128], ALU.add), 1)
                    ab_busy[pi] = vv.n
                dqb_busy = tT.n
                g.wait_ge(vv.h, vv.n)
                st.inc(g.dma_start(
                    out=io["weffd"].rearrange("(kt p) c -> p kt c", p=128)[:, :, dt * 128:(dt + 1) * 128],
                    in_=stg[:].rearrange("p (kt c) -> p kt c", c=128)), 16)
                stg_st = st.n

        with ExitStack() as sB:
            EB = sB.enter_context
            x3ch = EB(nc.sbuf_tensor("p5_x3ch", [128, DK * S_TOK], BF16))
            xt = EB(nc.sbuf_tensor("p5_xt", [128, D], BF16))
            bupg = EB(nc.sbuf_tensor("p5_bupg", [128, 64], F32))
            wut = [EB(nc.sbuf_tensor(f"p5_wut{i}", [128, 16 * 128], BF16)) for i in range(2)]
            ev = [EB(nc.sbuf_tensor(f"p5_ev{i}", [128, 512], BF16)) for i in range(6)]
            tpsB = [EB(nc.psum_tensor(f"p5_tpB{i}", [128, 128], BF16)) for i in range(2)]
            mpsB = [EB(nc.psum_tensor(f"p5_mpB{i}", [128, 512], F32)) for i in range(4)]
            t.wait_ge(vv.h, vv.n)
            for icol in range(64):
                ld.inc(g.dma_start(out=bupg[:, icol:icol + 1],
                                   in_=io["bup"][0:1, icol * 128:(icol + 1) * 128]
                                   .rearrange("o c -> c o")), 16)
            tps_busy = [0, 0]
            mps_busy = [0] * 4
            ev_busy = [0] * 6
            xt_busy = 0
            for ti in range(NT):
                if xt_busy:
                    g.wait_ge(tT.h, xt_busy)
                ld.inc(g.dma_start(out=xt[:], in_=io["x3"][ti * 128:(ti + 1) * 128, :]), 16)
                t.wait_ge(ld.h, ld.n)
                for kt in range(DK):
                    pi = kt % 2
                    if tps_busy[pi]:
                        t.wait_ge(vv.h, tps_busy[pi])
                    tT.inc(t.transpose(tpsB[pi][:], xt[:, kt * 128:(kt + 1) * 128], ident[:]), 1)
                    v.wait_ge(tT.h, tT.n)
                    vv.inc(v.tensor_copy(x3ch[:, kt * S_TOK + ti * 128: kt * S_TOK + (ti + 1) * 128],
                                         tpsB[pi][:]), 1)
                    tps_busy[pi] = vv.n
                xt_busy = tT.n
            t.wait_ge(vv.h, vv.n)
            wu_busy = [0, 0]
            for ht in range(64):
                par = ht % 2
                if wu_busy[par]:
                    g.wait_ge(tT.h, wu_busy[par])
                ld.inc(g.dma_start(out=wut[par][:].rearrange("p (k c) -> p k c", c=128),
                                   in_=io["weffu"].rearrange("(k p) c -> p k c", p=128)
                                   [:, :, ht * 128:(ht + 1) * 128]), 16)
                t.wait_ge(ld.h, ld.n)
                for tc in range(8):
                    pi = tc % 4
                    if mps_busy[pi]:
                        t.wait_ge(ss.h, mps_busy[pi])
                    ins = None
                    for kt in range(DK):
                        ins = t.matmul(mpsB[pi][:], wut[par][:, kt * 128:(kt + 1) * 128],
                                       x3ch[:, kt * S_TOK + tc * 512:(kt) * S_TOK + (tc + 1) * 512],
                                       start=(kt == 0), stop=(kt == DK - 1))
                    tT.inc(ins, 1)
                    s.wait_ge(tT.h, tT.n)
                    pe = tc % 6
                    if ev_busy[pe]:
                        s.wait_ge(st.h, ev_busy[pe])
                    ss.inc(s.activation(ev[pe][:], mpsB[pi][:], AF.Relu, bias=bupg[:, ht:ht + 1]), 1)
                    mps_busy[pi] = ss.n
                    g.wait_ge(ss.h, ss.n)
                    st.inc(g.dma_start(out=io["x4dram"][ht * 128:(ht + 1) * 128, tc * 512:(tc + 1) * 512],
                                       in_=ev[pe][:]), 16)
                    ev_busy[pe] = st.n
                wu_busy[par] = tT.n

        with ExitStack() as sC:
            EC = sC.enter_context
            wdc = EC(nc.sbuf_tensor("p5_wdc", [128, 64 * 512], BF16))
            x4s = [EC(nc.sbuf_tensor(f"p5_x4s{i}", [128, 64 * 128], BF16)) for i in range(2)]
            bdr = EC(nc.sbuf_tensor("p5_bdr", [1, D], BF16))
            x3s = EC(nc.sbuf_tensor("p5_x3s", [128, 512], F32))
            evc = [EC(nc.sbuf_tensor(f"p5_evc{i}", [128, 512], F32)) for i in range(6)]
            mpsC = [EC(nc.psum_tensor(f"p5_mpC{i}", [128, 512], F32)) for i in range(4)]
            t.wait_ge(vv.h, vv.n)
            v.wait_ge(ld.h, ld.n)
            ld.inc(g.dma_start(out=bdr[:], in_=io["bdn"]), 16)
            v.wait_ge(ld.h, ld.n)
            mps_busy = [0] * 4
            x4_busy = [0, 0]
            ev_busy = [0] * 6
            x3_busy = 0
            for oc in range(4):
                ld.inc(g.dma_start(out=wdc[:].rearrange("p (k c) -> p k c", c=512),
                                   in_=io["weffd"].rearrange("(k p) c -> p k c", p=128)
                                   [:, :, oc * 512:(oc + 1) * 512]), 16)
                t.wait_ge(ld.h, ld.n)
                for ti in range(NT):
                    par = ti % 2
                    if x4_busy[par]:
                        g.wait_ge(tT.h, x4_busy[par])
                    ld.inc(g.dma_start(out=x4s[par][:].rearrange("p (k c) -> p k c", c=128),
                                       in_=io["x4dram"].rearrange("(k p) c -> p k c", p=128)
                                       [:, :, ti * 128:(ti + 1) * 128]), 16)
                    if x3_busy:
                        g.wait_ge(vv.h, x3_busy)
                    ld.inc(g.dma_start(out=x3s[:], in_=io["x3"][ti * 128:(ti + 1) * 128,
                                                                oc * 512:(oc + 1) * 512]), 16)
                    t.wait_ge(ld.h, ld.n)
                    pi = ti % 4
                    if mps_busy[pi]:
                        t.wait_ge(vv.h, mps_busy[pi])
                    for kt in range(64):
                        t.matmul(mpsC[pi][:], x4s[par][:, kt * 128:(kt + 1) * 128],
                                 wdc[:, kt * 512:(kt + 1) * 512],
                                 start=(kt == 0), stop=False, skip_group_check=True)
                    ins = t.matmul(mpsC[pi][:], consts["ones_row_bf"][:],
                                   bdr[:, oc * 512:(oc + 1) * 512],
                                   start=False, stop=True, skip_group_check=True)
                    tT.inc(ins, 1)
                    x4_busy[par] = tT.n
                    v.wait_ge(tT.h, tT.n)
                    pe = ti % 6
                    if ev_busy[pe]:
                        v.wait_ge(st.h, ev_busy[pe])
                    vv.inc(v.tensor_tensor(evc[pe][:], mpsC[pi][:], x3s[:], ALU.add), 1)
                    mps_busy[pi] = vv.n
                    x3_busy = vv.n
                    g.wait_ge(vv.h, vv.n)
                    st.inc(g.dma_start(out=io["x5pre"][ti * 128:(ti + 1) * 128,
                                                       oc * 512:(oc + 1) * 512],
                                       in_=evc[pe][:]), 16)
                    ev_busy[pe] = st.n
            g.wait_ge(st.h, st.n)
            xpr = EC(nc.sbuf_tensor("p5_xpr", [128, D], F32))
            xo = EC(nc.sbuf_tensor("p5_xo", [128, D], F32))
            cols = EC(nc.sbuf_tensor("p5_cols", [128, 8], F32))
            xo_busy = 0
            xp_busy = 0
            for ti in range(NT):
                if xp_busy:
                    g.wait_ge(vv.h, xp_busy)
                ld.inc(g.dma_start(out=xpr[:], in_=io["x5pre"][ti * 128:(ti + 1) * 128, :]), 16)
                v.wait_ge(ld.h, ld.n)
                if xo_busy:
                    v.wait_ge(st.h, xo_busy)
                vv.inc(v.tensor_reduce(cols[:, 0:1], xpr[:], mybir.AxisListType.X, ALU.add), 1)
                v.wait_ge(vv.h, vv.n)
                v.scalar_tensor_tensor(xo[:], cols[:, 0:1].broadcast_to([128, D]), -1.0 / D,
                                       xpr[:], ALU.mult, ALU.add)
                v.tensor_tensor(xpr[:], xo[:], xo[:], ALU.mult)
                vv.inc(v.tensor_reduce(cols[:, 2:3], xpr[:], mybir.AxisListType.X, ALU.add), 1)
                xp_busy = vv.n
                s.wait_ge(vv.h, vv.n)
                ss.inc(emit_rstd(s, consts, cols[:, 2:3], cols[:, 3:4], 1.0 / D, cols[:, 4:5]), 1)
                v.wait_ge(ss.h, ss.n)
                vv.inc(v.tensor_scalar(xo[:], xo[:], cols[:, 3:4], None, ALU.mult), 1)
                g.wait_ge(vv.h, vv.n)
                st.inc(g.dma_start(out=io["out"][ti * 128:(ti + 1) * 128, :], in_=xo[:]), 16)
                xo_busy = st.n
        g.wait_ge(st.h, st.n)
        sy.wait_ge(st.h, st.n)
        return {"vv": (vv, vv.n), "ss": (ss, ss.n), "tT": (tT, tT.n), "st": (st, st.n)}


def build_p5():
    nc = bass.Bass("TRN2", target_bir_lowering=False)
    io = {}
    io["x3"] = nc.dram_tensor("x3", [S_TOK, D], F32, kind="ExternalInput").ap()
    io["wupq"] = nc.dram_tensor("wupq", [INTER, D], I32, kind="ExternalInput").ap()
    io["wups"] = nc.dram_tensor("wups", [INTER, 32], F32, kind="ExternalInput").ap()
    io["bup"] = nc.dram_tensor("bup", [1, INTER], F32, kind="ExternalInput").ap()
    io["alup"] = nc.dram_tensor("alup", [D, 16], F32, kind="ExternalInput").ap()
    io["blup"] = nc.dram_tensor("blup", [16, INTER], F32, kind="ExternalInput").ap()
    io["wdnq"] = nc.dram_tensor("wdnq", [D, INTER], I32, kind="ExternalInput").ap()
    io["wdns"] = nc.dram_tensor("wdns", [D, 128], F32, kind="ExternalInput").ap()
    io["bdn"] = nc.dram_tensor("bdn", [1, D], F32, kind="ExternalInput").ap()
    io["aldn"] = nc.dram_tensor("aldn", [INTER, 16], F32, kind="ExternalInput").ap()
    io["bldn"] = nc.dram_tensor("bldn", [16, D], F32, kind="ExternalInput").ap()
    io["out"] = nc.dram_tensor("out", [S_TOK, D], F32, kind="ExternalOutput").ap()
    io["weffu"] = nc.dram_tensor("weffu", [D, INTER], BF16, kind="Internal").ap()
    io["weffd"] = nc.dram_tensor("weffd", [INTER, D], BF16, kind="Internal").ap()
    io["x4dram"] = nc.dram_tensor("x4dram", [INTER, S_TOK], BF16, kind="Internal").ap()
    io["x5pre"] = nc.dram_tensor("x5pre", [S_TOK, D], F32, kind="Internal").ap()
    with ExitStack() as stack:
        ctx = Ctx(nc, stack)
        eng = {"t": nc.tensor, "v": nc.vector, "s": nc.scalar, "g": nc.gpsimd, "sy": nc.sync}
        ld0 = ctx.sem("c_ld")
        consts = load_consts(ctx, nc.gpsimd, ld0, keys=("ident", "ones_row_bf", "csts"))
        for e in (nc.tensor, nc.vector, nc.scalar, nc.sync):
            e.wait_ge(ld0.h, ld0.n)
        phase5(ctx, io, consts, eng)
    return nc


def kernel_multilaunch(**inputs):
    n = 8
    call = host_consts()
    def csel(keys):
        return {("c_" + k): call[k] for k in keys}
    c1 = csel(("ident", "csts"))
    c2 = csel(("ident", "utri_f32", "ones_col_f32", "ones_row_f32", "triu_bf", "trius_bf", "slower_bf", "csts"))
    c3 = csel(("ident", "h128_2048", "h128_8192", "i8h16", "i2h64", "csts"))
    c4 = csel(("ident", "ones_col_bf", "ones_row_bf", "slower_bf", "qsmask", "csts"))
    c5 = csel(("ident", "ones_row_bf", "csts"))
    u_full = np.ascontiguousarray(inputs["L0_attn_u"])
    cnt = (4095.0 - np.arange(S_TOK, dtype=np.float32)).reshape(1, S_TOK)

    nc1 = build_p1(None)
    maps = [dict(u=u_full[i], w_in=inputs["L0_attn_W_in"], **c1) for i in range(n)]
    r1 = run_bass_kernel_spmd(nc1, maps, core_ids=list(range(n))).results

    nc2 = build_p2()
    maps = [dict(zx=r1[i]["zx"], dlt=r1[i]["dlt"],
                 alog=inputs["L0_attn_A_log"].reshape(1, 64),
                 dtb=inputs["L0_attn_dt_bias"].reshape(1, 64),
                 dcoef=inputs["L0_attn_D"].reshape(1, 64),
                 bnw=inputs["L0_attn_B_norm_w"].reshape(1, 128),
                 cnw=inputs["L0_attn_C_norm_w"].reshape(1, 128),
                 u=u_full[i], w_out=inputs["L0_attn_W_out"],
                 onw=inputs["L0_attn_out_norm_w"].reshape(1, D_INNER), **c2)
            for i in range(n)]
    r2 = run_bass_kernel_spmd(nc2, maps, core_ids=list(range(n))).results

    nc3 = build_p3()
    maps = [dict(x1=r2[i]["x1"],
                 su_ug=inputs["L0_ffn_SU_ug"].reshape(1, D),
                 su_dp=inputs["L0_ffn_SU_dp"].reshape(1, INTER),
                 wsc_ug=inputs["L0_ffn_Wscale_ug"].reshape(1, 2 * INTER),
                 wsc_dp=inputs["L0_ffn_Wscale_dp"].reshape(1, D),
                 w_up=inputs["L0_ffn_W_up"], w_gate=inputs["L0_ffn_W_gate"],
                 w_down=inputs["L0_ffn_W_down"], **c3) for i in range(n)]
    r3 = run_bass_kernel_spmd(nc3, maps, core_ids=list(range(n))).results

    nc4 = build_p4()
    maps = [dict(x2=r3[i]["x2"], wq=inputs["L1_attn_Wq"], wk=inputs["L1_attn_Wk"],
                 wv=inputs["L1_attn_Wv"], wo=inputs["L1_attn_Wo"], cntrow=cnt, **c4)
            for i in range(n)]
    r4 = run_bass_kernel_spmd(nc4, maps, core_ids=list(range(n))).results

    nc5 = build_p5()
    maps = [dict(x3=r4[i]["x3"],
                 wupq=inputs["L1_ffn_w_up_q"], wups=inputs["L1_ffn_w_up_scale"],
                 bup=inputs["L1_ffn_b_up"].reshape(1, INTER),
                 alup=inputs["L1_ffn_w_up_lora_a"], blup=inputs["L1_ffn_w_up_lora_b"],
                 wdnq=inputs["L1_ffn_w_down_q"], wdns=inputs["L1_ffn_w_down_scale"],
                 bdn=inputs["L1_ffn_b_down"].reshape(1, D),
                 aldn=inputs["L1_ffn_w_down_lora_a"], bldn=inputs["L1_ffn_w_down_lora_b"],
                 **c5) for i in range(n)]
    r5 = run_bass_kernel_spmd(nc5, maps, core_ids=list(range(n))).results

    return np.stack([r5[i]["out"] for i in range(n)], axis=0)


P1_KEYS = ("ident", "csts")
P2_KEYS = ("ident", "utri_f32", "ones_col_f32", "ones_row_f32", "triu_bf", "trius_bf",
           "slower_bf", "csts")
P3_KEYS = ("ident", "h128_2048", "h128_8192", "i8h16", "i2h64", "csts")
P4_KEYS = ("ident", "ones_col_bf", "ones_row_bf", "slower_bf", "qsmask", "csts")
P5_KEYS = ("ident", "ones_row_bf", "csts")


def build_fused():
    nc = bass.Bass("TRN2", target_bir_lowering=False)
    io = {}
    De = lambda name, shape, dt, kind: nc.dram_tensor(name, shape, dt, kind=kind).ap()
    io["u"] = De("u", [S_TOK, D], F32, "ExternalInput")
    io["w_in"] = De("w_in", [D_IN, D], F32, "ExternalInput")
    io["alog"] = De("alog", [1, 64], F32, "ExternalInput")
    io["dtb"] = De("dtb", [1, 64], F32, "ExternalInput")
    io["dcoef"] = De("dcoef", [1, 64], F32, "ExternalInput")
    io["bnw"] = De("bnw", [1, 128], F32, "ExternalInput")
    io["cnw"] = De("cnw", [1, 128], F32, "ExternalInput")
    io["w_out"] = De("w_out", [D, D_INNER], F32, "ExternalInput")
    io["onw"] = De("onw", [1, D_INNER], F32, "ExternalInput")
    io["su_ug"] = De("su_ug", [1, D], F32, "ExternalInput")
    io["su_dp"] = De("su_dp", [1, INTER], F32, "ExternalInput")
    io["wsc_ug"] = De("wsc_ug", [1, 2 * INTER], F32, "ExternalInput")
    io["wsc_dp"] = De("wsc_dp", [1, D], F32, "ExternalInput")
    io["w_up"] = De("w_up", [INTER, D], F32, "ExternalInput")
    io["w_gate"] = De("w_gate", [INTER, D], F32, "ExternalInput")
    io["w_down"] = De("w_down", [D, INTER], F32, "ExternalInput")
    io["wq"] = De("wq", [D, D], F32, "ExternalInput")
    io["wk"] = De("wk", [D, D], F32, "ExternalInput")
    io["wv"] = De("wv", [D, D], F32, "ExternalInput")
    io["wo"] = De("wo", [D, D], F32, "ExternalInput")
    io["cntrow"] = De("cntrow", [1, S_TOK], F32, "ExternalInput")
    io["wupq"] = De("wupq", [INTER, D], I32, "ExternalInput")
    io["wups"] = De("wups", [INTER, 32], F32, "ExternalInput")
    io["bup"] = De("bup", [1, INTER], F32, "ExternalInput")
    io["alup"] = De("alup", [D, 16], F32, "ExternalInput")
    io["blup"] = De("blup", [16, INTER], F32, "ExternalInput")
    io["wdnq"] = De("wdnq", [D, INTER], I32, "ExternalInput")
    io["wdns"] = De("wdns", [D, 128], F32, "ExternalInput")
    io["bdn"] = De("bdn", [1, D], F32, "ExternalInput")
    io["aldn"] = De("aldn", [INTER, 16], F32, "ExternalInput")
    io["bldn"] = De("bldn", [16, D], F32, "ExternalInput")
    io["out"] = De("out", [S_TOK, D], F32, "ExternalOutput")
    io["zx"] = De("zx", [S_TOK, D_IN_PAD], BF16, "Internal")
    io["dlt"] = De("dlt", [S_TOK, 192], F32, "Internal")
    io["dbg"] = De("dbg", [S_TOK, 320], F32, "Internal")
    io["dbg2"] = De("dbg2", [1, 4224], F32, "Internal")
    io["ydbg"] = None
    io["ydram"] = De("ydram", [S_TOK, D_INNER], BF16, "Internal")
    io["yzdram"] = De("yzdram", [S_TOK, D_INNER], BF16, "Internal")
    io["adram"] = De("adram", [S_TOK, 64], F32, "Internal")
    io["ashdram"] = De("ashdram", [S_TOK, 64], F32, "Internal")
    io["x1"] = De("x1", [S_TOK, D], F32, "Internal")
    io["zdram"] = De("zdram", [INTER, S_TOK], BF16, "Internal")
    io["vdown"] = De("vdown", [INTER, D], BF16, "Internal")
    io["z2dram"] = De("z2dram", [D, S_TOK], BF16, "Internal")
    io["wdraft"] = De("wdraft", [D, INTER], BF16, "Internal")
    io["wdraft2"] = De("wdraft2", [INTER, D], BF16, "Internal")
    io["vupF"] = De("vupF", [D, INTER], BF16, "Internal")
    io["vgtF"] = De("vgtF", [D, INTER], BF16, "Internal")
    io["x2"] = De("x2", [S_TOK, D], F32, "Internal")
    io["qT"] = De("qT", [D, S_TOK], BF16, "Internal")
    io["kT"] = De("kT", [D, S_TOK], BF16, "Internal")
    io["vtok"] = De("vtok", [S_TOK, D], BF16, "Internal")
    io["attnT"] = De("attnT", [D, S_TOK], BF16, "Internal")
    io["x3"] = De("x3", [S_TOK, D], F32, "Internal")
    io["weffu"] = De("weffu", [D, INTER], BF16, "Internal")
    io["weffd"] = De("weffd", [INTER, D], BF16, "Internal")
    io["x4dram"] = De("x4dram", [INTER, S_TOK], BF16, "Internal")
    io["x5pre"] = De("x5pre", [S_TOK, D], F32, "Internal")
    cdr = declare_const_drams(nc)
    with ExitStack() as stack:
        ctx = Ctx(nc, stack)
        eng = {"t": nc.tensor, "v": nc.vector, "s": nc.scalar, "g": nc.gpsimd, "sy": nc.sync}

        def consts_for(keys, lsem):
            return load_consts_scoped(stack, nc, nc.gpsimd, lsem, cdr, keys)

        def run_phase(keys, fn, name, entry):
            cs = ExitStack()
            lds = ctx.sem("f_ld_" + name)
            cc = load_consts_scoped(cs, nc, nc.gpsimd, lds, cdr, keys)
            for e in (nc.tensor, nc.vector, nc.scalar, nc.sync):
                e.wait_ge(lds.h, lds.n)
            ex = fn(ctx, io, cc, eng, entry=entry)
            cs.close()
            return ex

        e1 = run_phase(P1_KEYS, phase1, "p1", None)
        cs2 = ExitStack()
        lds2 = ctx.sem("f_ld_p2")
        c2 = load_consts_scoped(cs2, nc, nc.gpsimd, lds2, cdr, P2_KEYS)
        for e in (nc.tensor, nc.vector, nc.scalar, nc.sync):
            e.wait_ge(lds2.h, lds2.n)
        e2 = phase2(ctx, io, c2, eng, entry=e1)
        e2c = phase2c(ctx, io, c2, eng, entry=e2)
        cs2.close()
        e3 = run_phase(P3_KEYS, phase3, "p3", e2c)
        e4 = run_phase(P4_KEYS, phase4, "p4", e3)
        run_phase(P5_KEYS, phase5, "p5", e4)
    return nc


def kernel(**inputs):
    n = 8
    call = host_consts()
    consts = {("c_" + k): v for k, v in call.items()}
    u_full = np.ascontiguousarray(inputs["L0_attn_u"])
    cnt = (4095.0 - np.arange(S_TOK, dtype=np.float32)).reshape(1, S_TOK)
    nc = build_fused()
    maps = [dict(
        u=u_full[i], w_in=inputs["L0_attn_W_in"],
        alog=inputs["L0_attn_A_log"].reshape(1, 64),
        dtb=inputs["L0_attn_dt_bias"].reshape(1, 64),
        dcoef=inputs["L0_attn_D"].reshape(1, 64),
        bnw=inputs["L0_attn_B_norm_w"].reshape(1, 128),
        cnw=inputs["L0_attn_C_norm_w"].reshape(1, 128),
        w_out=inputs["L0_attn_W_out"],
        onw=inputs["L0_attn_out_norm_w"].reshape(1, D_INNER),
        su_ug=inputs["L0_ffn_SU_ug"].reshape(1, D),
        su_dp=inputs["L0_ffn_SU_dp"].reshape(1, INTER),
        wsc_ug=inputs["L0_ffn_Wscale_ug"].reshape(1, 2 * INTER),
        wsc_dp=inputs["L0_ffn_Wscale_dp"].reshape(1, D),
        w_up=inputs["L0_ffn_W_up"], w_gate=inputs["L0_ffn_W_gate"],
        w_down=inputs["L0_ffn_W_down"],
        wq=inputs["L1_attn_Wq"], wk=inputs["L1_attn_Wk"],
        wv=inputs["L1_attn_Wv"], wo=inputs["L1_attn_Wo"], cntrow=cnt,
        wupq=inputs["L1_ffn_w_up_q"], wups=inputs["L1_ffn_w_up_scale"],
        bup=inputs["L1_ffn_b_up"].reshape(1, INTER),
        alup=inputs["L1_ffn_w_up_lora_a"], blup=inputs["L1_ffn_w_up_lora_b"],
        wdnq=inputs["L1_ffn_w_down_q"], wdns=inputs["L1_ffn_w_down_scale"],
        bdn=inputs["L1_ffn_b_down"].reshape(1, D),
        aldn=inputs["L1_ffn_w_down_lora_a"], bldn=inputs["L1_ffn_w_down_lora_b"],
        **consts) for i in range(n)]
    res = run_bass_kernel_spmd(nc, maps, core_ids=list(range(n)))
    return np.stack([res.results[i]["out"] for i in range(n)], axis=0)


# version 11
# speedup vs baseline: 1.2515x; 1.2515x over previous
import math
from contextlib import ExitStack

import numpy as np
import ml_dtypes

import concourse.bass as bass
import concourse.mybir as mybir
from concourse.bass_utils import run_bass_kernel_spmd

F32 = mybir.dt.float32
BF16 = mybir.dt.bfloat16
I32 = mybir.dt.int32
AF = mybir.ActivationFunctionType
ALU = mybir.AluOpType

S_TOK = 4096
D = 2048
NT = S_TOK // 128
DK = D // 128

D_INNER = 4096
NHEADS = 64
HEADDIM = 64
D_STATE = 128
D_IN = 8640
D_IN_PAD = 8704
INTER = 8192

MAGIC = float(1.5 * 2 ** 23)
TWO_PI = float(2 * np.pi)


def hadamard(m):
    H = np.array([[1.0]], dtype=np.float64)
    while H.shape[0] < m:
        H = np.block([[H, H], [H, -H]])
    return H


class Sem:
    def __init__(self, ctx, name):
        self.h = ctx.stack.enter_context(ctx.nc.semaphore(name))
        self.n = 0

    def inc(self, instr, k=1):
        instr.then_inc(self.h, k)
        self.n += k
        return self.n

    def wait(self, eng, val=None):
        eng.wait_ge(self.h, self.n if val is None else val)


class Ctx:
    def __init__(self, nc, stack):
        self.nc = nc
        self.stack = stack

    def sem(self, name):
        return Sem(self, name)

    def sb(self, name, shape, dtype, **kw):
        return self.stack.enter_context(self.nc.sbuf_tensor(name, shape, dtype, **kw))

    def ps(self, name, shape, dtype=F32):
        return self.stack.enter_context(self.nc.psum_tensor(name, shape, dtype))


def host_consts():
    c = {}
    c["ident"] = np.eye(128).astype(ml_dtypes.bfloat16)
    H128 = hadamard(128)
    c["h128_2048"] = (H128 / (math.sqrt(2048.0) * 64.0)).astype(ml_dtypes.bfloat16)
    c["h128_8192"] = (H128 / (math.sqrt(8192.0) * 64.0)).astype(ml_dtypes.bfloat16)
    H16 = hadamard(16)
    H64 = hadamard(64)
    c["i8h16"] = np.kron(np.eye(8), H16).astype(ml_dtypes.bfloat16)
    c["i2h64"] = np.kron(np.eye(2), H64).astype(ml_dtypes.bfloat16)
    c["h16i8"] = np.kron(H16, np.eye(8)).astype(ml_dtypes.bfloat16)
    c["h64i2"] = np.kron(H64, np.eye(2)).astype(ml_dtypes.bfloat16)
    c["utri_f32"] = np.triu(np.ones((128, 128))).astype(np.float32)
    c["ones_col_f32"] = np.ones((128, 1), dtype=np.float32)
    c["ones_col_bf"] = np.ones((128, 1), dtype=ml_dtypes.bfloat16)
    c["ones_row_bf"] = np.ones((1, 128), dtype=ml_dtypes.bfloat16)
    c["ones_row_f32"] = np.ones((1, 128), dtype=np.float32)
    c["slow_bf"] = (np.tril(np.ones((128, 128)), -1).T > 0).astype(ml_dtypes.bfloat16)
    c["slower_bf"] = np.tril(np.ones((128, 128)), -1).astype(ml_dtypes.bfloat16)
    c["triu_bf"] = np.triu(np.ones((128, 128))).astype(ml_dtypes.bfloat16)
    c["trius_bf"] = np.triu(np.ones((128, 128)), 1).astype(ml_dtypes.bfloat16)
    qs = np.zeros((128, 4 * 512), dtype=np.float64)
    for o in range(4):
        s_ = np.arange(128)[:, None] + 128 * o
        q_ = np.arange(512)[None, :]
        qs[:, o * 512:(o + 1) * 512] = (s_ <= q_)
    c["qsmask"] = qs.astype(ml_dtypes.bfloat16)
    sc = [MAGIC, -MAGIC, TWO_PI, -TWO_PI, np.pi, np.pi / 2, 1.0, -1.0, 0.5, -0.5,
          1.0 / TWO_PI, 1e-5, 1.0 / 64.0, 64.0, 0.25]
    c["csts"] = np.tile(np.array([sc], dtype=np.float32), (128, 1))
    return c


CST = {name: i for i, name in enumerate(
    ["MAGIC", "NMAGIC", "TWO_PI", "NTWO_PI", "PI", "HALF_PI", "ONE", "NONE",
     "HALF", "NHALF", "INV_TWO_PI", "EPS", "INV64", "C64", "QUARTER"])}


def declare_const_drams(nc):
    drams = {}
    for name, arr in host_consts().items():
        dt = BF16 if arr.dtype == ml_dtypes.bfloat16 else F32
        drams[name] = nc.dram_tensor("c_" + name, list(arr.shape), dt, kind="ExternalInput").ap()
    return drams


_CONST_SEQ = [0]


def load_consts_scoped(stack, nc, g, lsem, cdr, keys):
    arrs = host_consts()
    t = {}
    _CONST_SEQ[0] += 1
    for name in keys:
        arr = arrs[name]
        dt = BF16 if arr.dtype == ml_dtypes.bfloat16 else F32
        tile = stack.enter_context(nc.sbuf_tensor(f"sb{_CONST_SEQ[0]}_{name}", list(arr.shape), dt))
        lsem.inc(g.dma_start(out=tile[:], in_=cdr[name]), 16)
        t[name] = tile
    return t


def load_consts(ctx, g, lsem, keys=None):
    nc = ctx.nc
    cdr = declare_const_drams(nc)
    if keys is None:
        keys = list(host_consts().keys())
    return load_consts_scoped(ctx.stack, nc, g, lsem, cdr, keys)


def cst(consts, name):
    i = CST[name]
    return consts["csts"][:, i:i + 1]


def phase1(ctx, io, consts, eng, entry=None):
    nc, t, v, s, g, sy = ctx.nc, eng["t"], eng["v"], eng["s"], eng["g"], eng["sy"]
    with ExitStack() as ph:
        E = ph.enter_context
        if entry:
            for e in (t, v, s, g, sy):
                for key, (sem, val) in entry.items():
                    e.wait_ge(sem.h, val)
        xT = E(nc.sbuf_tensor("p1_xT", [128, DK * S_TOK], BF16))
        uch = [E(nc.sbuf_tensor(f"p1_uch{i}", [128, D], BF16)) for i in range(2)]
        wnat = [E(nc.sbuf_tensor(f"p1_wnat{i}", [128, 4 * D], BF16)) for i in range(1)]
        wT = [E(nc.sbuf_tensor(f"p1_wT{i}", [128, 16 * 512], BF16)) for i in range(2)]
        ot = [E(nc.sbuf_tensor(f"p1_ot{i}", [128, 512], BF16)) for i in range(4)]
        dlt_t = [E(nc.sbuf_tensor(f"p1_dlt{i}", [128, 192], F32)) for i in range(2)]
        tps = [E(nc.psum_tensor(f"p1_tp{i}", [128, 128], BF16)) for i in range(4)]
        mps = [E(nc.psum_tensor(f"p1_mp{i}", [128, 512], F32)) for i in range(4)]

        ld = ctx.sem("p1_ld")
        trd = ctx.sem("p1_trd")
        tev = ctx.sem("p1_tev")
        mmd = ctx.sem("p1_mmd")
        gev = ctx.sem("p1_gev")
        st = ctx.sem("p1_st")
        wfree = ctx.sem("p1_wfree")

        ident = consts["ident"]

        uch_busy = [0] * 2
        tps_busy = [0] * 4
        mps_busy = [0] * 4
        ot_busy = [0] * 4
        dlt_busy = [0] * 2
        wnat_busy = [0] * 1
        xT_view = lambda kt, ti: xT[:, kt * S_TOK + ti * 128: kt * S_TOK + (ti + 1) * 128]

        def do_transpose(src_ap, dst_ap, pi):
            if tps_busy[pi]:
                t.wait_ge(tev.h, tps_busy[pi])
            trd.inc(t.transpose(tps[pi][:], src_ap, ident[:]), 1)
            v.wait_ge(trd.h, trd.n)
            tev.inc(v.tensor_copy(dst_ap, tps[pi][:]), 1)
            tps_busy[pi] = tev.n

        for i in range(NT):
            b = uch[i % 2]
            if uch_busy[i % 2]:
                g.wait_ge(trd.h, uch_busy[i % 2])
            ld.inc(g.dma_start(out=b[:], in_=io["u"][i * 128:(i + 1) * 128, :]), 16)
            t.wait_ge(ld.h, ld.n)
            for kt in range(DK):
                do_transpose(b[:, kt * 128:(kt + 1) * 128], xT_view(kt, i), (i * DK + kt) % 4)
            uch_busy[i % 2] = trd.n

        n_oc = D_IN_PAD // 512
        for oc in range(n_oc):
            par = 0
            wn = wnat[par]
            wt = wT[par]
            rows0 = oc * 512
            rows1 = min(rows0 + 512, D_IN)
            nrows = rows1 - rows0
            nfull = nrows // 128
            if wnat_busy[par]:
                g.wait_ge(wfree.h, wnat_busy[par])
            ld.inc(g.dma_start(
                out=wn[:, 0:nfull * D].rearrange("p (a d) -> p a d", a=nfull),
                in_=io["w_in"][rows0:rows0 + nfull * 128, :].rearrange("(a p) d -> p a d", p=128)), 16)
            if nrows < 512:
                rem = nrows - nfull * 128
                ld.inc(g.dma_start(out=wn[0:rem, nfull * D:(nfull + 1) * D],
                                   in_=io["w_in"][rows0 + nfull * 128:rows1, :]), 16)
                v.wait_ge(ld.h, ld.n)
                tev.inc(v.memset(wn[64:128, nfull * D:(nfull + 1) * D], 0.0), 1)
            t.wait_ge(ld.h, ld.n)
            if nrows < 512:
                t.wait_ge(tev.h, tev.n)
            for a in range(4):
                for kt in range(DK):
                    do_transpose(wn[:, a * D + kt * 128: a * D + (kt + 1) * 128],
                                 wt[:, kt * 512 + a * 128: kt * 512 + (a + 1) * 128],
                                 (a * DK + kt) % 4)
            wfree.inc(t.nop(), 1)
            wnat_busy[par] = wfree.n
            t.wait_ge(tev.h, tev.n)
            for ti in range(NT):
                pi = ti % 4
                mp = mps[pi]
                if mps_busy[pi]:
                    t.wait_ge(gev.h, mps_busy[pi])
                ins = None
                for kt in range(DK):
                    ins = t.matmul(mp[:], xT_view(kt, ti), wt[:, kt * 512:(kt + 1) * 512],
                                   start=(kt == 0), stop=(kt == DK - 1))
                mmd.inc(ins, 1)
                o = ot[ti % 4]
                v.wait_ge(mmd.h, mmd.n)
                if ot_busy[ti % 4]:
                    v.wait_ge(st.h, ot_busy[ti % 4])
                gev.inc(v.tensor_copy(o[:], mp[:]), 1)
                if oc == 16:
                    dl = dlt_t[ti % 2]
                    if dlt_busy[ti % 2]:
                        v.wait_ge(st.h, dlt_busy[ti % 2])
                    gev.inc(v.tensor_copy(dl[:], mp[:, 256:448]), 1)
                    mps_busy[pi] = gev.n
                    g.wait_ge(gev.h, gev.n)
                    st.inc(g.dma_start(out=io["zx"][ti * 128:(ti + 1) * 128, oc * 512:(oc + 1) * 512], in_=o[:]), 16)
                    ot_busy[ti % 4] = st.n
                    st.inc(g.dma_start(out=io["dlt"][ti * 128:(ti + 1) * 128, :], in_=dl[:]), 16)
                    dlt_busy[ti % 2] = st.n
                else:
                    mps_busy[pi] = gev.n
                    g.wait_ge(gev.h, gev.n)
                    st.inc(g.dma_start(out=io["zx"][ti * 128:(ti + 1) * 128, oc * 512:(oc + 1) * 512], in_=o[:]), 16)
                    ot_busy[ti % 4] = st.n
        g.wait_ge(st.h, st.n)
        sy.wait_ge(st.h, st.n)
        return {"gev": (gev, gev.n), "tev": (tev, tev.n), "mmd": (mmd, mmd.n), "st": (st, st.n)}


def build_p1(core_inputs):
    nc = bass.Bass("TRN2", target_bir_lowering=False)
    u = nc.dram_tensor("u", [S_TOK, D], F32, kind="ExternalInput").ap()
    w_in = nc.dram_tensor("w_in", [D_IN, D], F32, kind="ExternalInput").ap()
    zx = nc.dram_tensor("zx", [S_TOK, D_IN_PAD], BF16, kind="ExternalOutput").ap()
    dlt = nc.dram_tensor("dlt", [S_TOK, 192], F32, kind="ExternalOutput").ap()
    with ExitStack() as stack:
        ctx = Ctx(nc, stack)
        eng = {"t": nc.tensor, "v": nc.vector, "s": nc.scalar, "g": nc.gpsimd, "sy": nc.sync}
        ld0 = ctx.sem("c_ld")
        consts = load_consts(ctx, nc.gpsimd, ld0, keys=("ident", "csts"))
        nc.tensor.wait_ge(ld0.h, ld0.n)
        nc.vector.wait_ge(ld0.h, ld0.n)
        phase1(ctx, {"u": u, "w_in": w_in, "zx": zx, "dlt": dlt}, consts, eng)
    return nc


def rep_free(ap2d, reps, inner):
    return ap2d.unsqueeze(1).broadcast_to([ap2d.shape[0], reps, inner])


def rep_inner(ap2d, h, p2):
    return ap2d.unsqueeze(2).broadcast_to([ap2d.shape[0], h, p2])


def emit_rstd(s, consts, vcol, out, scale, tmp):
    s.activation(tmp, vcol, AF.Ln, bias=cst(consts, "EPS"), scale=scale)
    s.mul(tmp, tmp, -0.5)
    return s.activation(out, tmp, AF.Exp)


def emit_rsqrt(v, col, tmp, tmp2):
    MAGICI = 0x5F3759DF
    ci = tmp.bitcast(I32)
    v.tensor_scalar(ci, col.bitcast(I32), 1, None, ALU.arith_shift_right)
    v.tensor_scalar(ci, ci, -1, MAGICI, ALU.mult, ALU.add)
    for _ in range(3):
        v.tensor_tensor(tmp2, tmp, tmp, ALU.mult)
        v.tensor_tensor(tmp2, tmp2, col, ALU.mult)
        v.tensor_scalar(tmp2, tmp2, -0.5, 1.5, ALU.mult, ALU.add)
        v.tensor_tensor(tmp, tmp, tmp2, ALU.mult)
    v.tensor_copy(col, tmp)


def phase2(ctx, io, consts, eng, upto=None, dump_c0=False, entry=None):
    nc, t, v, s, g, sy = ctx.nc, eng["t"], eng["v"], eng["s"], eng["g"], eng["sy"]
    NH = NHEADS
    HG = 8
    ident = consts["ident"]
    with ExitStack() as ph:
        E = ph.enter_context
        if entry:
            for e in (t, v, s, g, sy):
                for key, (sem, val) in entry.items():
                    e.wait_ge(sem.h, val)
        ld = ctx.sem("p2_ld")
        tT = ctx.sem("p2_tT")
        vv = ctx.sem("p2_vv")
        ss = ctx.sem("p2_ss")
        sq = ctx.sem("p2_sq")
        st = ctx.sem("p2_st")

        GF = [128, 32 * 64]
        A_f = E(nc.sbuf_tensor("p2_A", GF, F32))
        A_bf = E(nc.sbuf_tensor("p2_Abf", GF, BF16))
        g_f = E(nc.sbuf_tensor("p2_g", GF, F32))
        g1_f = E(nc.sbuf_tensor("p2_g1", GF, F32))
        g2p_f = E(nc.sbuf_tensor("p2_g2p", GF, F32))
        cosg = E(nc.sbuf_tensor("p2_cos", GF, F32))
        sing = E(nc.sbuf_tensor("p2_sin", GF, F32))
        D_bc = E(nc.sbuf_tensor("p2_Dbc", [128, 64], F32))
        wB_bc = E(nc.sbuf_tensor("p2_wBbc", [128, 128], BF16))
        wC_bc = E(nc.sbuf_tensor("p2_wCbc", [128, 128], BF16))
        pp = ExitStack()
        PE = pp.enter_context
        dtr = PE(nc.sbuf_tensor("p2_dtr", GF, F32))
        lamr = PE(nc.sbuf_tensor("p2_lamr", GF, F32))
        theta = PE(nc.sbuf_tensor("p2_theta", GF, F32))
        wk1 = PE(nc.sbuf_tensor("p2_wk1", GF, F32))
        wk2 = PE(nc.sbuf_tensor("p2_wk2", GF, F32))
        wk3 = PE(nc.sbuf_tensor("p2_wk3", GF, F32))
        rows = PE(nc.sbuf_tensor("p2_rows", [1, 512], F32))
        brow = PE(nc.sbuf_tensor("p2_brow", [1, 256], F32))
        dtb_bc = PE(nc.sbuf_tensor("p2_dtbbc", [128, 64], F32))
        nea_bc = PE(nc.sbuf_tensor("p2_neabc", [128, 64], F32))
        tsums = PE(nc.sbuf_tensor("p2_tsums", [1, 32 * 64], F32))
        carr = PE(nc.sbuf_tensor("p2_carr", [1, 33 * 64], F32))

        ps_a = E(nc.psum_tensor("p2_psa", [128, 64], F32))
        ps_b = E(nc.psum_tensor("p2_psb", [128, 128], F32))

        grid = lambda buf, i: buf[:, i * 64:(i + 1) * 64]

        ld.inc(g.dma_start(out=dtr[:].rearrange("p (i c) -> p i c", c=64),
                           in_=io["dlt"][:, 0:64].rearrange("(i p) c -> p i c", p=128)), 16)
        ld.inc(g.dma_start(out=lamr[:].rearrange("p (i c) -> p i c", c=64),
                           in_=io["dlt"][:, 64:128].rearrange("(i p) c -> p i c", p=128)), 16)
        ld.inc(g.dma_start(out=theta[:].rearrange("p (i c) -> p i c", c=64),
                           in_=io["dlt"][:, 128:192].rearrange("(i p) c -> p i c", p=128)), 16)
        ld.inc(g.dma_start(out=rows[:, 0:64], in_=io["alog"]), 16)
        ld.inc(g.dma_start(out=rows[:, 64:128], in_=io["dtb"]), 16)
        ld.inc(g.dma_start(out=rows[:, 128:192], in_=io["dcoef"]), 16)
        ld.inc(g.dma_start(out=brow[:, 0:128], in_=io["bnw"]), 16)
        ld.inc(g.dma_start(out=brow[:, 128:256], in_=io["cnw"]), 16)

        s.wait_ge(ld.h, ld.n)
        v.wait_ge(ld.h, ld.n)
        t.wait_ge(ld.h, ld.n)
        ss.inc(s.activation(rows[:, 192:256], rows[:, 0:64], AF.Exp), 1)
        v.wait_ge(ss.h, ss.n)
        vv.inc(v.tensor_scalar(rows[:, 256:320], rows[:, 192:256], -1.0, None, ALU.mult), 1)
        t.wait_ge(vv.h, vv.n)
        tT.inc(t.matmul(ps_a[:], consts["ones_row_f32"][:], rows[:, 64:128], start=True, stop=True), 1)
        v.wait_ge(tT.h, tT.n)
        vv.inc(v.tensor_copy(dtb_bc[:], ps_a[:]), 1)
        t.wait_ge(vv.h, vv.n)
        tT.inc(t.matmul(ps_a[:], consts["ones_row_f32"][:], rows[:, 256:320], start=True, stop=True), 1)
        v.wait_ge(tT.h, tT.n)
        vv.inc(v.tensor_copy(nea_bc[:], ps_a[:]), 1)
        t.wait_ge(vv.h, vv.n)
        tT.inc(t.matmul(ps_a[:], consts["ones_row_f32"][:], rows[:, 128:192], start=True, stop=True), 1)
        v.wait_ge(tT.h, tT.n)
        vv.inc(v.tensor_copy(D_bc[:], ps_a[:]), 1)
        tT.inc(t.matmul(ps_b[:], consts["ones_row_f32"][:], brow[:, 0:128], start=True, stop=True), 1)
        v.wait_ge(tT.h, tT.n)
        vv.inc(v.tensor_copy(wB_bc[:], ps_b[:]), 1)
        t.wait_ge(vv.h, vv.n)
        tT.inc(t.matmul(ps_b[:], consts["ones_row_f32"][:], brow[:, 128:256], start=True, stop=True), 1)
        v.wait_ge(tT.h, tT.n)
        vv.inc(v.tensor_copy(wC_bc[:], ps_b[:]), 1)

        NTI = 32
        G3 = lambda buf: buf[:].rearrange("p (i c) -> p i c", c=64)
        dtbrep = rep_free(dtb_bc[:], NTI, 64)
        nearep = rep_free(nea_bc[:], NTI, 64)
        v.tensor_tensor(G3(wk1), G3(dtr), dtbrep, ALU.add)
        vv.inc(v.scalar_tensor_tensor(wk2[:], wk1[:], -1.0, wk1[:], ALU.mult, ALU.max), 1)
        s.wait_ge(vv.h, vv.n)
        ss.inc(s.activation(wk2[:], wk2[:], AF.Exp, scale=-1.0), 1)
        ss.inc(s.activation(wk2[:], wk2[:], AF.Ln, bias=1.0), 1)
        v.wait_ge(ss.h, ss.n)
        v.tensor_scalar(wk3[:], wk1[:], 0.0, None, ALU.max)
        v.tensor_tensor(dtr[:], wk3[:], wk2[:], ALU.add)
        v.tensor_tensor(G3(A_f), G3(dtr), nearep, ALU.mult)
        vv.inc(v.tensor_copy(A_bf[:], A_f[:]), 1)
        if upto == "pre00":
            G3c = lambda buf: buf[:].rearrange("p (i c) -> p i c", c=64)
            g.wait_ge(ss.h, ss.n)
            g.wait_ge(vv.h, vv.n)
            for k, buf in enumerate((dtr, wk1, wk2, A_f)):
                st.inc(g.dma_start(out=io["dbg"][:, k * 64:(k + 1) * 64].rearrange("(i p) c -> p i c", p=128),
                                   in_=G3c(buf)), 16)
            g.wait_ge(st.h, st.n)
            sy.wait_ge(st.h, st.n)
            pp.close()
            return

        s.wait_ge(vv.h, vv.n)
        ss.inc(s.activation(lamr[:], lamr[:], AF.Sigmoid), 1)
        v.wait_ge(ss.h, ss.n)
        v.tensor_tensor(wk1[:], dtr[:], lamr[:], ALU.mult)
        v.tensor_copy(g1_f[:], wk1[:])
        vv.inc(v.tensor_tensor(wk2[:], dtr[:], wk1[:], ALU.subtract), 1)
        g.wait_ge(vv.h, vv.n)
        ld.inc(g.dma_start(out=io["adram"][:].rearrange("(i p) c -> p i c", p=128), in_=G3(A_f)), 16)
        g.wait_ge(ld.h, ld.n)
        ld.inc(g.dma_start(out=io["ashdram"][0:S_TOK - 1, :], in_=io["adram"][1:S_TOK, :]), 16)
        ld.inc(g.dma_start(out=io["ashdram"][S_TOK - 1:S_TOK, :], in_=io["adram"][S_TOK - 1:S_TOK, :]), 16)
        g.wait_ge(ld.h, ld.n)
        ld.inc(g.dma_start(out=G3(wk3), in_=io["ashdram"][:].rearrange("(i p) c -> p i c", p=128)), 16)
        s.wait_ge(ld.h, ld.n)
        ss.inc(s.activation(wk3[:], wk3[:], AF.Exp, scale=-1.0), 1)
        v.wait_ge(ss.h, ss.n)
        v.tensor_tensor(g2p_f[:], wk2[:], wk3[:], ALU.mult)
        vv.inc(v.tensor_tensor(g_f[:], wk1[:], g2p_f[:], ALU.add), 1)

        if upto == "pre0":
            G3c = lambda buf: buf[:].rearrange("p (i c) -> p i c", c=64)
            g.wait_ge(ss.h, ss.n)
            g.wait_ge(vv.h, vv.n)
            for k, buf in enumerate((A_f, g_f, g2p_f)):
                st.inc(g.dma_start(out=io["dbg"][:, k * 64:(k + 1) * 64].rearrange("(i p) c -> p i c", p=128),
                                   in_=G3c(buf)), 16)
            g.wait_ge(st.h, st.n)
            sy.wait_ge(st.h, st.n)
            pp.close()
            return

        for i in range(NTI):
            t.wait_ge(vv.h, vv.n)
            tT.inc(t.matmul(ps_a[0:1, 0:64], consts["ones_col_f32"][:], grid(theta, i),
                            start=True, stop=True), 1)
            v.wait_ge(tT.h, tT.n)
            vv.inc(v.tensor_copy(tsums[:, i * 64:(i + 1) * 64], ps_a[0:1, 0:64]), 1)
        v.tensor_copy(carr[:, 64:128], tsums[:, 0:64])
        last_carr = None
        for i in range(1, NTI):
            last_carr = v.tensor_tensor(carr[:, (i + 1) * 64:(i + 2) * 64], carr[:, i * 64:(i + 1) * 64],
                                        tsums[:, i * 64:(i + 1) * 64], ALU.add)
        vv.inc(last_carr, 1)
        if upto == "ang":
            g.wait_ge(vv.h, vv.n)
            st.inc(g.dma_start(out=io["dbg2"][0:1, 0:2048], in_=tsums[:]), 16)
            st.inc(g.dma_start(out=io["dbg2"][0:1, 2048:2048 + 33 * 64], in_=carr[:]), 16)
            g.wait_ge(st.h, st.n)
            sy.wait_ge(st.h, st.n)
            pp.close()
            return

        t.wait_ge(vv.h, vv.n)
        for i in range(NTI):
            t.wait_ge(vv.h, vv.n)
            if i == 0:
                tT.inc(t.matmul(ps_a[:], consts["utri_f32"][:], grid(theta, i),
                                start=True, stop=True), 1)
            else:
                t.matmul(ps_a[:], consts["utri_f32"][:], grid(theta, i), start=True, stop=False,
                         skip_group_check=True)
                tT.inc(t.matmul(ps_a[:], consts["ones_row_f32"][:], carr[:, i * 64:(i + 1) * 64],
                                start=False, stop=True, skip_group_check=True), 1)
            v.wait_ge(tT.h, tT.n)
            vv.inc(v.tensor_copy(grid(wk1, i), ps_a[:]), 1)
        INV2PI = 1.0 / TWO_PI
        v.tensor_scalar(wk2[:], wk1[:], INV2PI, MAGIC, ALU.mult, ALU.add)
        v.tensor_scalar(wk2[:], wk2[:], -MAGIC, None, ALU.add)
        v.scalar_tensor_tensor(wk3[:], wk2[:], -TWO_PI, wk1[:], ALU.mult, ALU.add)
        v.tensor_scalar(wk2[:], wk1[:], INV2PI, 0.25, ALU.mult, ALU.add)
        v.tensor_scalar(wk2[:], wk2[:], MAGIC, None, ALU.add)
        v.tensor_scalar(wk2[:], wk2[:], -MAGIC, None, ALU.add)
        v.scalar_tensor_tensor(wk2[:], wk2[:], -TWO_PI, wk1[:], ALU.mult, ALU.add)
        vv.inc(v.tensor_scalar(wk2[:], wk2[:], float(np.pi / 2), None, ALU.add), 1)
        s.wait_ge(vv.h, vv.n)
        ss.inc(s.activation(sing[:], wk3[:], AF.Sin), 1)
        ss.inc(s.activation(cosg[:], wk2[:], AF.Sin), 1)
        v.wait_ge(ss.h, ss.n)
        t.wait_ge(ss.h, ss.n)

        if upto == "pre":
            G3b = lambda buf: buf[:].rearrange("p (i c) -> p i c", c=64)
            g.wait_ge(ss.h, ss.n)
            g.wait_ge(vv.h, vv.n)
            for k, buf in enumerate((wk1, wk3, wk2, cosg, sing)):
                st.inc(g.dma_start(out=io["dbg"][:, k * 64:(k + 1) * 64].rearrange("(i p) c -> p i c", p=128),
                                   in_=G3b(buf)), 16)
            g.wait_ge(st.h, st.n)
            sy.wait_ge(st.h, st.n)
            pp.close()
            return

        pp.close()
        g.wait_ge(ss.h, ss.n)
        g.wait_ge(vv.h, vv.n)
        sy.wait_ge(ss.h, ss.n)
        zxch = [E(nc.sbuf_tensor(f"p2_zx{i}", [128, D_IN_PAD], BF16)) for i in range(2)]
        mbufA = E(nc.sbuf_tensor("p2_mA", [128, 8192], BF16))
        mbufB = E(nc.sbuf_tensor("p2_mB", [128, 8192], BF16))
        xg = E(nc.sbuf_tensor("p2_xg", [128, D_INNER], BF16))
        xw = E(nc.sbuf_tensor("p2_xw", [128, D_INNER], BF16))
        cxb = E(nc.sbuf_tensor("p2_cx", [128, D_INNER], BF16))
        ybuf = E(nc.sbuf_tensor("p2_y", [128, D_INNER], BF16))
        Brot = E(nc.sbuf_tensor("p2_Brot", [128, 128], BF16))
        Crot = E(nc.sbuf_tensor("p2_Crot", [128, 128], BF16))
        BT = E(nc.sbuf_tensor("p2_BT", [128, 128], BF16, is_transpose_dest=True))
        CT = E(nc.sbuf_tensor("p2_CT", [128, 128], BF16, is_transpose_dest=True))
        Ssb = E(nc.sbuf_tensor("p2_Ssb", [128, 128], BF16))
        state_f = E(nc.sbuf_tensor("p2_stf", [128, D_INNER], F32))
        state_b = E(nc.sbuf_tensor("p2_stb", [128, D_INNER], BF16))
        acs = E(nc.sbuf_tensor("p2_acs", [128, 64], F32))
        negA = E(nc.sbuf_tensor("p2_negA", [128, 64], F32))
        wexpE = E(nc.sbuf_tensor("p2_wexp", [128, 64], F32))
        expacs = E(nc.sbuf_tensor("p2_eacs", [128, 64], F32))
        gamb = E(nc.sbuf_tensor("p2_gamb", [128, 64], F32))
        ccoef = E(nc.sbuf_tensor("p2_cco", [128, 64], F32))
        sqt = E(nc.sbuf_tensor("p2_sqt", [128, 128], F32))
        rcol = E(nc.sbuf_tensor("p2_rcol", [128, 8], F32))
        asum = E(nc.sbuf_tensor("p2_asum", [1, 64], F32))
        gamr = E(nc.sbuf_tensor("p2_gamr", [1, 64], F32))
        wkT = [E(nc.sbuf_tensor(f"p2_wkT{i}", [128, 512], F32)) for i in range(2)]

        ps_m = [E(nc.psum_tensor(f"p2_psm{i}", [128, 512], F32)) for i in range(2)]
        ps_S = E(nc.psum_tensor("p2_psS", [128, 128], F32))
        ps_yd = E(nc.psum_tensor("p2_psyd", [128, 512], F32))
        ps_yo = E(nc.psum_tensor("p2_psyo", [128, 512], F32))
        ps_st = E(nc.psum_tensor("p2_psst", [128, 512], F32))


        zx_busy = [0, 0]
        psm_busy = [0, 0]
        ydyo_busy = 0
        pst_busy = 0
        stbf_ready = vv.n
        psa_sbusy = 0
        mA_free_at = 0
        prev_groups_t = 0
        ybuf_busy = 0

        for c in range(32):
            zb = zxch[c % 2]
            Asl = A_f[:, c * 64:(c + 1) * 64]
            Absl = A_bf[:, c * 64:(c + 1) * 64]
            gsl = g_f[:, c * 64:(c + 1) * 64]
            g2psl = g2p_f[:, c * 64:(c + 1) * 64]
            cossl = cosg[:, c * 64:(c + 1) * 64]
            sinsl = sing[:, c * 64:(c + 1) * 64]

            if zx_busy[c % 2]:
                g.wait_ge(vv.h, zx_busy[c % 2])
            ld.inc(g.dma_start(out=zb[:], in_=io["zx"][c * 128:(c + 1) * 128, :]), 16)

            if prev_groups_t:
                v.wait_ge(tT.h, prev_groups_t)
            vv.inc(v.tensor_tensor(
                mbufA[:].rearrange("p (h q) -> p h q", h=64),
                rep_inner(Absl, 64, 128),
                rep_free(consts["triu_bf"][:], 64, 128), ALU.mult), 1)
            rhsM_done = vv.n

            v.wait_ge(ld.h, ld.n)
            Braw = zb[:, 8192:8320]
            Craw = zb[:, 8320:8448]
            for (raw, rot, wbc, ci) in ((Braw, Brot, wB_bc, 0), (Craw, Crot, wC_bc, 1)):
                v.tensor_tensor(sqt[:], raw, raw, ALU.mult)
                vv.inc(v.tensor_reduce(rcol[:, ci * 4:ci * 4 + 1], sqt[:],
                                       mybir.AxisListType.X, ALU.add), 1)
                s.wait_ge(vv.h, vv.n)
                ss.inc(emit_rstd(s, consts, rcol[:, ci * 4:ci * 4 + 1],
                                 rcol[:, ci * 4 + 1:ci * 4 + 2], 1.0 / 128.0,
                                 rcol[:, ci * 4 + 2:ci * 4 + 3]), 1)
                v.wait_ge(ss.h, ss.n)
                v.scalar_tensor_tensor(sqt[:], raw, rcol[:, ci * 4 + 1:ci * 4 + 2], wbc[:],
                                       ALU.mult, ALU.mult)
                ev, od = sqt[:, 0:128:2], sqt[:, 1:128:2]
                v.tensor_tensor(rot[:, 0:128:2], cossl, ev, ALU.mult)
                v.tensor_tensor(wkT[0][:, 0:64], sinsl, od, ALU.mult)
                v.tensor_tensor(rot[:, 0:128:2], rot[:, 0:128:2], wkT[0][:, 0:64], ALU.subtract)
                v.tensor_tensor(rot[:, 1:128:2], sinsl, ev, ALU.mult)
                v.tensor_tensor(wkT[0][:, 0:64], cossl, od, ALU.mult)
                last_rope = v.tensor_tensor(rot[:, 1:128:2], rot[:, 1:128:2], wkT[0][:, 0:64], ALU.add)
            vv.inc(last_rope, 1)
            bc_done = vv.n
            v.tensor_tensor(sqt[:], Brot[:], Crot[:], ALU.mult)
            vv.inc(v.tensor_reduce(rcol[:, 0:1], sqt[:], mybir.AxisListType.X, ALU.add), 1)
            v.wait_ge(vv.h, vv.n)
            v.tensor_tensor(ccoef[:], g1_f[:, c * 64:(c + 1) * 64],
                            rcol[:, 0:1].broadcast_to([128, 64]), ALU.mult)
            v.tensor_tensor(ccoef[:], D_bc[:], ccoef[:], ALU.add)
            xsl = zb[:, 4096:8192]
            x3 = xsl.rearrange("p (h q) -> p h q", h=64)
            v.tensor_tensor(cxb[:].rearrange("p (h q) -> p h q", h=64), x3,
                            rep_inner(ccoef[:], 64, 64), ALU.mult)
            vv.inc(v.tensor_tensor(xg[:].rearrange("p (h q) -> p h q", h=64), x3,
                                   rep_inner(gsl, 64, 64), ALU.mult), 1)
            xg_done = vv.n

            sy.wait_ge(vv.h, bc_done)
            if prev_groups_t:
                sy.wait_ge(tT.h, prev_groups_t)
            sq.inc(sy.dma_start_transpose(BT[:], Brot[:]), 16)
            sq.inc(sy.dma_start_transpose(CT[:], Crot[:]), 16)

            t.wait_ge(vv.h, rhsM_done)
            for mt in range(16):
                pm = ps_m[mt % 2]
                if psm_busy[mt % 2]:
                    t.wait_ge(ss.h, psm_busy[mt % 2])
                tT.inc(t.matmul(pm[:], consts["slower_bf"][:], mbufA[:, mt * 512:(mt + 1) * 512],
                                start=True, stop=True), 1)
                s.wait_ge(tT.h, tT.n)
                ss.inc(s.activation(mbufB[:, mt * 512:(mt + 1) * 512], pm[:], AF.Exp), 1)
                psm_busy[mt % 2] = ss.n
            mA_free_at = tT.n

            t.wait_ge(sq.h, sq.n)
            tT.inc(t.matmul(ps_S[:], BT[:], CT[:], start=True, stop=True), 1)
            S_done_t = tT.n
            v.wait_ge(tT.h, tT.n)
            vv.inc(v.tensor_tensor(Ssb[:], ps_S[:], consts["trius_bf"][:], ALU.mult), 1)

            vv.inc(v.tensor_scalar(negA[:], Asl, -1.0, None, ALU.mult), 1)
            t.wait_ge(vv.h, vv.n)
            if psa_sbusy:
                t.wait_ge(ss.h, psa_sbusy)
            tT.inc(t.matmul(ps_a[:], consts["utri_f32"][:], Asl, start=True, stop=True), 1)
            v.wait_ge(tT.h, tT.n)
            vv.inc(v.tensor_copy(acs[:], ps_a[:]), 1)
            tT.inc(t.matmul(ps_b[0:1, 0:64], consts["ones_col_f32"][:], Asl, start=True, stop=True), 1)
            v.wait_ge(tT.h, tT.n)
            vv.inc(v.tensor_copy(asum[:], ps_b[0:1, 0:64]), 1)
            t.wait_ge(vv.h, vv.n)
            t.matmul(ps_a[:], consts["utri_f32"][:], negA[:], start=True, stop=False,
                     skip_group_check=True)
            tT.inc(t.matmul(ps_a[:], consts["ones_row_f32"][:], asum[:], start=False, stop=True,
                            skip_group_check=True), 1)
            s.wait_ge(tT.h, tT.n)
            s.wait_ge(vv.h, vv.n)
            ss.inc(s.activation(wexpE[:], ps_a[:], AF.Exp), 1)
            psa_sbusy = ss.n
            ss.inc(s.activation(expacs[:], acs[:], AF.Exp), 1)
            ss.inc(s.activation(gamr[:], asum[:], AF.Exp), 1)
            t.wait_ge(ss.h, ss.n)
            tT.inc(t.matmul(ps_b[:, 0:64], consts["ones_row_f32"][:], gamr[:], start=True, stop=True), 1)
            v.wait_ge(tT.h, tT.n)
            vv.inc(v.tensor_copy(gamb[:], ps_b[:, 0:64]), 1)
            v.wait_ge(ss.h, ss.n)
            vv.inc(v.tensor_tensor(xw[:].rearrange("p (h q) -> p h q", h=64),
                                   xg[:].rearrange("p (h q) -> p h q", h=64),
                                   rep_inner(wexpE[:], 64, 64), ALU.mult), 1)
            xw_done = vv.n

            v.wait_ge(ss.h, psm_busy[0])
            v.wait_ge(ss.h, psm_busy[1])
            v.wait_ge(tT.h, mA_free_at)
            vv.inc(v.tensor_tensor(mbufA[:].rearrange("p (h q) -> p h q", h=64),
                                   mbufB[:].rearrange("p (h q) -> p h q", h=64),
                                   rep_free(Ssb[:], 64, 128), ALU.mult), 1)
            mt_done = vv.n

            t.wait_ge(vv.h, mt_done)
            for hg in range(HG):
                if ydyo_busy:
                    t.wait_ge(vv.h, ydyo_busy)
                for h8 in range(8):
                    h = hg * 8 + h8
                    tT.inc(t.matmul(ps_yd[:, h8 * 64:(h8 + 1) * 64],
                                    mbufA[:, h * 128:(h + 1) * 128],
                                    xg[:, h * 64:(h + 1) * 64],
                                    start=(h8 == 0), stop=(h8 == 7), skip_group_check=True), 1)
                if pst_busy:
                    t.wait_ge(vv.h, pst_busy)
                for h8 in range(8):
                    h = hg * 8 + h8
                    tT.inc(t.matmul(ps_st[:, h8 * 64:(h8 + 1) * 64], Brot[:],
                                    xw[:, h * 64:(h + 1) * 64],
                                    start=(h8 == 0), stop=(h8 == 7), skip_group_check=True), 1)
                if c > 0:
                    if hg == 0:
                        t.wait_ge(vv.h, stbf_ready)
                    for h8 in range(8):
                        h = hg * 8 + h8
                        tT.inc(t.matmul(ps_yo[:, h8 * 64:(h8 + 1) * 64], CT[:],
                                        state_b[:, h * 64:(h + 1) * 64],
                                        start=(h8 == 0), stop=(h8 == 7), skip_group_check=True), 1)
                mm_done = tT.n
                v.wait_ge(tT.h, mm_done)
                if hg == 0 and ybuf_busy:
                    v.wait_ge(st.h, ybuf_busy)
                wt = wkT[hg % 2]
                if c > 0:
                    v.tensor_tensor(wt[:].rearrange("p (h q) -> p h q", h=8), ps_yo[:].rearrange("p (h q) -> p h q", h=8),
                                    rep_inner(expacs[:, hg * 8:(hg + 1) * 8], 8, 64), ALU.mult)
                    vv.inc(v.tensor_tensor(ybuf[:, hg * 512:(hg + 1) * 512], wt[:], ps_yd[:], ALU.add), 1)
                else:
                    vv.inc(v.tensor_copy(ybuf[:, hg * 512:(hg + 1) * 512], ps_yd[:]), 1)
                ydyo_busy = vv.n
                ssl = state_f[:, hg * 512:(hg + 1) * 512]
                if c > 0:
                    v.tensor_tensor(ssl.rearrange("p (h q) -> p h q", h=8), ssl.rearrange("p (h q) -> p h q", h=8),
                                    rep_inner(gamb[:, hg * 8:(hg + 1) * 8], 8, 64), ALU.mult)
                    v.tensor_tensor(ssl, ssl, ps_st[:], ALU.add)
                else:
                    v.tensor_copy(ssl, ps_st[:])
                vv.inc(v.tensor_copy(state_b[:, hg * 512:(hg + 1) * 512], ssl), 1)
                pst_busy = vv.n
            stbf_ready = vv.n
            prev_groups_t = tT.n
            if dump_c0 and c == 0:
                g.wait_ge(vv.h, vv.n)
                st.inc(g.dma_start(out=io["dbgM"], in_=mbufA[:]), 16)
                st.inc(g.dma_start(out=io["dbgE"], in_=mbufB[:]), 16)
                st.inc(g.dma_start(out=io["dbgS"], in_=Ssb[:]), 16)
                st.inc(g.dma_start(out=io["dbgxg"], in_=xg[:]), 16)
                st.inc(g.dma_start(out=io["dbgyp"], in_=ybuf[:]), 16)
                st.inc(g.dma_start(out=io["dbgcx"], in_=cxb[:]), 16)
                v.wait_ge(st.h, st.n)
                t.wait_ge(st.h, st.n)
                s.wait_ge(st.h, st.n)
            vv.inc(v.tensor_tensor(ybuf[:], ybuf[:], cxb[:], ALU.add), 1)
            zx_busy[c % 2] = vv.n
            g.wait_ge(vv.h, vv.n)
            st.inc(g.dma_start(out=io["ydram"][c * 128:(c + 1) * 128, :], in_=ybuf[:]), 16)
            ybuf_busy = st.n
        g.wait_ge(st.h, st.n)
        return {"vv": (vv, vv.n), "ss": (ss, ss.n), "tT": (tT, tT.n), "st": (st, st.n)}


def phase2c(ctx, io, consts, eng, entry=None):
    nc, t, v, s, g, sy = ctx.nc, eng["t"], eng["v"], eng["s"], eng["g"], eng["sy"]
    ident = consts["ident"]
    with ExitStack() as ph:
        E = ph.enter_context
        ld = ctx.sem("p2c_ld")
        trd = ctx.sem("p2c_trd")
        tev = ctx.sem("p2c_tev")
        mmd = ctx.sem("p2c_mmd")
        gev = ctx.sem("p2c_gev")
        ssc = ctx.sem("p2c_ssc")
        st = ctx.sem("p2c_st")

        woT = E(nc.sbuf_tensor("p2c_woT", [128, 32 * D], BF16))
        wnat = E(nc.sbuf_tensor("p2c_wnat", [128, D_INNER], BF16))
        onw_g = E(nc.sbuf_tensor("p2c_onw", [128, 32], F32))
        ytile = [E(nc.sbuf_tensor(f"p2c_y{i}", [128, D_INNER], BF16)) for i in range(2)]
        ztile = [E(nc.sbuf_tensor("p2c_z", [128, D_INNER], BF16))]
        sq = E(nc.sbuf_tensor("p2c_sq", [128, D_INNER], BF16))
        vgrid = E(nc.sbuf_tensor("p2c_vg", [128, 40], F32))
        ylT = E(nc.sbuf_tensor("p2c_ylT", [128, 32 * 128], BF16))
        utile = E(nc.sbuf_tensor("p2c_u", [128, D], F32))
        xpre = E(nc.sbuf_tensor("p2c_xpre", [128, D], F32))
        x1t = E(nc.sbuf_tensor("p2c_x1", [128, D], F32))
        cols = E(nc.sbuf_tensor("p2c_cols", [128, 8], F32))
        tps = [E(nc.psum_tensor(f"p2c_tp{i}", [128, 128], BF16)) for i in range(4)]
        mps = [E(nc.psum_tensor(f"p2c_mp{i}", [128, 512], F32)) for i in range(4)]

        tps_busy = [0] * 4
        mps_busy = [0] * 4

        if entry:
            for e in (t, v, s, g, sy):
                for key, (sem, val) in entry.items():
                    e.wait_ge(sem.h, val)
        for kt in range(32):
            ld.inc(g.dma_start(out=onw_g[:, kt:kt + 1],
                               in_=io["onw"][0:1, kt * 128:(kt + 1) * 128].rearrange("o c -> c o")), 16)
        ybusy = [0, 0]
        zbusy = [0, 0]
        for ti in range(NT):
            par = ti % 2
            yt, zt = ytile[par], ztile[0]
            if ybusy[par]:
                g.wait_ge(st.h, ybusy[par])
            ld.inc(g.dma_start(out=yt[:], in_=io["ydram"][ti * 128:(ti + 1) * 128, :]), 16)
            if zbusy[0]:
                g.wait_ge(tev.h, zbusy[0])
            ld.inc(g.dma_start(out=zt[:], in_=io["zx"][ti * 128:(ti + 1) * 128, 0:D_INNER]), 16)
            s.wait_ge(ld.h, ld.n)
            ssc.inc(s.activation(zt[:], zt[:], AF.Silu), 1)
            v.wait_ge(ssc.h, ssc.n)
            tev.inc(v.tensor_tensor(yt[:], yt[:], zt[:], ALU.mult), 1)
            zbusy[0] = tev.n
            v.tensor_tensor(sq[:], yt[:], yt[:], ALU.mult)
            tev.inc(v.tensor_reduce(vgrid[:, ti:ti + 1], sq[:], mybir.AxisListType.X, ALU.add), 1)
            g.wait_ge(tev.h, tev.n)
            st.inc(g.dma_start(out=io["yzdram"][ti * 128:(ti + 1) * 128, :], in_=yt[:]), 16)
            ybusy[par] = st.n
        s.wait_ge(tev.h, tev.n)
        ssc.inc(emit_rstd(s, consts, vgrid[:, 0:32], vgrid[:, 0:32], 1.0 / float(D_INNER),
                          vgrid[:, 0:32]), 1)

        v.wait_ge(ld.h, ld.n)
        for oc in range(16):
            ld.inc(g.dma_start(out=wnat[:], in_=io["w_out"][oc * 128:(oc + 1) * 128, :]), 16)
            t.wait_ge(ld.h, ld.n)
            for kt in range(32):
                pi = kt % 4
                if tps_busy[pi]:
                    t.wait_ge(tev.h, tps_busy[pi])
                trd.inc(t.transpose(tps[pi][:], wnat[:, kt * 128:(kt + 1) * 128], ident[:]), 1)
                v.wait_ge(trd.h, trd.n)
                tev.inc(v.tensor_scalar(woT[:, kt * D + oc * 128: kt * D + (oc + 1) * 128],
                                        tps[pi][:], onw_g[:, kt:kt + 1], None, ALU.mult), 1)
                tps_busy[pi] = tev.n
            g.wait_ge(trd.h, trd.n)

        yz_busy = [0, 0]
        u_busy = 0
        x1_busy = 0
        for ti in range(NT):
            par = ti % 2
            yt = ytile[par]
            if yz_busy[par]:
                g.wait_ge(trd.h, yz_busy[par])
            ld.inc(g.dma_start(out=yt[:], in_=io["yzdram"][ti * 128:(ti + 1) * 128, :]), 16)
            if u_busy:
                g.wait_ge(gev.h, u_busy)
            ld.inc(g.dma_start(out=utile[:], in_=io["u"][ti * 128:(ti + 1) * 128, :]), 16)
            v.wait_ge(ld.h, ld.n)
            v.wait_ge(ssc.h, ssc.n)
            tev.inc(v.tensor_scalar(yt[:], yt[:], vgrid[:, ti:ti + 1], None, ALU.mult), 1)
            t.wait_ge(tev.h, tev.n)
            for kt in range(32):
                pi = kt % 4
                if tps_busy[pi]:
                    t.wait_ge(tev.h, tps_busy[pi])
                trd.inc(t.transpose(tps[pi][:], yt[:, kt * 128:(kt + 1) * 128], ident[:]), 1)
                v.wait_ge(trd.h, trd.n)
                tev.inc(v.tensor_copy(ylT[:, kt * 128:(kt + 1) * 128], tps[pi][:]), 1)
                tps_busy[pi] = tev.n
            yz_busy[par] = trd.n
            t.wait_ge(tev.h, tev.n)
            for oc in range(4):
                pi = oc % 4
                if mps_busy[pi]:
                    t.wait_ge(gev.h, mps_busy[pi])
                ins = None
                for kt in range(32):
                    ins = t.matmul(mps[pi][:], ylT[:, kt * 128:(kt + 1) * 128],
                                   woT[:, kt * D + oc * 512:(kt) * D + (oc + 1) * 512],
                                   start=(kt == 0), stop=(kt == 31))
                mmd.inc(ins, 1)
                v.wait_ge(mmd.h, mmd.n)
                gev.inc(v.tensor_tensor(xpre[:, oc * 512:(oc + 1) * 512], mps[pi][:],
                                        utile[:, oc * 512:(oc + 1) * 512], ALU.add), 1)
                mps_busy[pi] = gev.n
                u_busy = gev.n
            if x1_busy:
                v.wait_ge(st.h, x1_busy)
            tev.inc(v.tensor_reduce(cols[:, 0:1], xpre[:], mybir.AxisListType.X, ALU.add), 1)
            v.wait_ge(tev.h, tev.n)
            v.scalar_tensor_tensor(x1t[:], cols[:, 0:1].broadcast_to([128, D]), -1.0 / D,
                                   xpre[:], ALU.mult, ALU.add)
            v.tensor_tensor(xpre[:], x1t[:], x1t[:], ALU.mult)
            vv_ = tev
            vv_.inc(v.tensor_reduce(cols[:, 2:3], xpre[:], mybir.AxisListType.X, ALU.add), 1)
            s.wait_ge(vv_.h, vv_.n)
            ssc.inc(emit_rstd(s, consts, cols[:, 2:3], cols[:, 3:4], 1.0 / D, cols[:, 4:5]), 1)
            v.wait_ge(ssc.h, ssc.n)
            gev.inc(v.tensor_scalar(x1t[:], x1t[:], cols[:, 3:4], None, ALU.mult), 1)
            g.wait_ge(gev.h, gev.n)
            st.inc(g.dma_start(out=io["x1"][ti * 128:(ti + 1) * 128, :], in_=x1t[:]), 16)
            x1_busy = st.n
        g.wait_ge(st.h, st.n)
        sy.wait_ge(st.h, st.n)
        return {"tev": (tev, tev.n), "ssc": (ssc, ssc.n), "gev": (gev, gev.n), "st": (st, st.n)}


def build_p2(upto=None, dump_c0=False):
    nc = bass.Bass("TRN2", target_bir_lowering=False)
    io = {}
    io["zx"] = nc.dram_tensor("zx", [S_TOK, D_IN_PAD], BF16, kind="ExternalInput").ap()
    io["dlt"] = nc.dram_tensor("dlt", [S_TOK, 192], F32, kind="ExternalInput").ap()
    io["alog"] = nc.dram_tensor("alog", [1, 64], F32, kind="ExternalInput").ap()
    io["dtb"] = nc.dram_tensor("dtb", [1, 64], F32, kind="ExternalInput").ap()
    io["dcoef"] = nc.dram_tensor("dcoef", [1, 64], F32, kind="ExternalInput").ap()
    io["bnw"] = nc.dram_tensor("bnw", [1, 128], F32, kind="ExternalInput").ap()
    io["cnw"] = nc.dram_tensor("cnw", [1, 128], F32, kind="ExternalInput").ap()
    io["u"] = nc.dram_tensor("u", [S_TOK, D], F32, kind="ExternalInput").ap()
    io["w_out"] = nc.dram_tensor("w_out", [D, D_INNER], F32, kind="ExternalInput").ap()
    io["onw"] = nc.dram_tensor("onw", [1, D_INNER], F32, kind="ExternalInput").ap()
    io["x1"] = nc.dram_tensor("x1", [S_TOK, D], F32, kind="ExternalOutput").ap()
    io["dbg"] = nc.dram_tensor("dbg", [S_TOK, 320], F32, kind="ExternalOutput").ap()
    if dump_c0:
        io["dbgM"] = nc.dram_tensor("dbgM", [128, 8192], BF16, kind="ExternalOutput").ap()
        io["dbgE"] = nc.dram_tensor("dbgE", [128, 8192], BF16, kind="ExternalOutput").ap()
        io["dbgS"] = nc.dram_tensor("dbgS", [128, 128], BF16, kind="ExternalOutput").ap()
        io["dbgxg"] = nc.dram_tensor("dbgxg", [128, 4096], BF16, kind="ExternalOutput").ap()
        io["dbgyp"] = nc.dram_tensor("dbgyp", [128, 4096], BF16, kind="ExternalOutput").ap()
        io["dbgcx"] = nc.dram_tensor("dbgcx", [128, 4096], BF16, kind="ExternalOutput").ap()
    io["dbg2"] = nc.dram_tensor("dbg2", [1, 4224], F32, kind="ExternalOutput").ap()
    io["ydram"] = nc.dram_tensor("ydram", [S_TOK, D_INNER], BF16,
                                 kind=("ExternalOutput" if upto == "ssd" else "Internal")).ap()
    io["yzdram"] = nc.dram_tensor("yzdram", [S_TOK, D_INNER], BF16, kind="Internal").ap()
    io["adram"] = nc.dram_tensor("adram", [S_TOK, 64], F32, kind="Internal").ap()
    io["ashdram"] = nc.dram_tensor("ashdram", [S_TOK, 64], F32, kind="Internal").ap()
    with ExitStack() as stack:
        ctx = Ctx(nc, stack)
        eng = {"t": nc.tensor, "v": nc.vector, "s": nc.scalar, "g": nc.gpsimd, "sy": nc.sync}
        ld0 = ctx.sem("c_ld")
        consts = load_consts(ctx, nc.gpsimd, ld0, keys=("ident", "utri_f32", "ones_col_f32", "ones_row_f32", "triu_bf", "trius_bf", "slower_bf", "csts"))
        for e in (nc.tensor, nc.vector, nc.scalar, nc.sync):
            e.wait_ge(ld0.h, ld0.n)
        ph2 = phase2(ctx, io, consts, eng, upto=(None if upto == "ssd" else upto), dump_c0=dump_c0)
        if upto is None:
            phase2c(ctx, io, consts, eng, entry=ph2)
    return nc


def phase3(ctx, io, consts, eng, entry=None):
    nc, t, v, s, g, sy = ctx.nc, eng["t"], eng["v"], eng["s"], eng["g"], eng["sy"]
    ident = consts["ident"]
    with ExitStack() as ph:
        E = ph.enter_context
        ld = ctx.sem("p3_ld")
        tT = ctx.sem("p3_tT")
        vv = ctx.sem("p3_vv")
        ss = ctx.sem("p3_ss")
        st = ctx.sem("p3_st")

        if entry:
            for e in (t, v, s, g, sy):
                for key, (sem, val) in entry.items():
                    e.wait_ge(sem.h, val)

        sug = E(nc.sbuf_tensor("p3_sug", [128, 16], F32))
        sdp = E(nc.sbuf_tensor("p3_sdp", [128, 64], F32))
        wsu = E(nc.sbuf_tensor("p3_wsu", [128, 128], F32))
        wsd = E(nc.sbuf_tensor("p3_wsd", [128, 16], F32))
        for j in range(16):
            ld.inc(g.dma_start(out=sug[:, j:j + 1],
                               in_=io["su_ug"][0:1, j * 128:(j + 1) * 128].rearrange("o c -> c o")), 16)
        for j in range(64):
            ld.inc(g.dma_start(out=sdp[:, j:j + 1],
                               in_=io["su_dp"][0:1, j * 128:(j + 1) * 128].rearrange("o c -> c o")), 16)
        for icol in range(128):
            ld.inc(g.dma_start(out=wsu[:, icol:icol + 1],
                               in_=io["wsc_ug"][0:1, icol * 128:(icol + 1) * 128].rearrange("o c -> c o")), 16)
        for icol in range(16):
            ld.inc(g.dma_start(out=wsd[:, icol:icol + 1],
                               in_=io["wsc_dp"][0:1, icol * 128:(icol + 1) * 128].rearrange("o c -> c o")), 16)
        v.wait_ge(ld.h, ld.n)
        v.tensor_scalar(wsu[:], wsu[:], 64.0, None, ALU.mult)
        vv.inc(v.tensor_scalar(wsd[:], wsd[:], 64.0, None, ALU.mult), 1)

        def gather2048(dram, kt, cols):
            return dram[8 * kt:, cols].rearrange("(bl j r) c -> (bl j) r c", bl=8, j=16)[:, 0, :] \
                if False else dram.rearrange("(j blk bl) c -> (blk bl) j c", j=16, bl=8)[:, :, cols] \
                if False else None

        with ExitStack() as sA:
            EA = sA.enter_context
            Sjh = EA(nc.sbuf_tensor("p3_Sjh", [128, 16 * 128], BF16))
            xch = EA(nc.sbuf_tensor("p3_xch", [128, DK * S_TOK], BF16))
            xt = EA(nc.sbuf_tensor("p3_xt", [128, D], BF16))
            zev = [EA(nc.sbuf_tensor(f"p3_zev{i}", [128, 512], BF16)) for i in range(2)]
            tps = [EA(nc.psum_tensor(f"p3_tpA{i}", [128, 128], BF16)) for i in range(2)]
            mpsA = [EA(nc.psum_tensor(f"p3_mpA{i}", [128, 512], F32)) for i in range(2)]

            v.wait_ge(ld.h, ld.n)
            ins = None
            for j in range(16):
                ins = v.tensor_scalar(Sjh[:, j * 128:(j + 1) * 128], consts["h128_2048"][:],
                                      sug[:, j:j + 1], None, ALU.mult)
            vv.inc(ins, 1)
            t.wait_ge(vv.h, vv.n)

            tps_busy = [0, 0]
            mps_busy = [0, 0]
            zev_busy = [0, 0]
            xt_busy = 0
            xch_view = lambda kt, ti: xch[:, kt * S_TOK + ti * 128:(kt) * S_TOK + (ti + 1) * 128]

            for ti in range(NT):
                if xt_busy:
                    g.wait_ge(tT.h, xt_busy)
                ld.inc(g.dma_start(out=xt[:], in_=io["x1"][ti * 128:(ti + 1) * 128, :]), 16)
                t.wait_ge(ld.h, ld.n)
                for kt in range(DK):
                    pi = kt % 2
                    if tps_busy[pi]:
                        t.wait_ge(vv.h, tps_busy[pi])
                    tT.inc(t.transpose(tps[pi][:], xt[:, kt * 128:(kt + 1) * 128], ident[:]), 1)
                    v.wait_ge(tT.h, tT.n)
                    vv.inc(v.tensor_copy(xch_view(kt, ti), tps[pi][:]), 1)
                    tps_busy[pi] = vv.n
                xt_busy = tT.n
            t.wait_ge(vv.h, vv.n)
            for j in range(16):
                for tc in range(8):
                    pi = tc % 2
                    if mps_busy[pi]:
                        t.wait_ge(vv.h, mps_busy[pi])
                    tT.inc(t.matmul(mpsA[pi][:], Sjh[:, j * 128:(j + 1) * 128],
                                    xch[:, j * S_TOK + tc * 512:(j) * S_TOK + (tc + 1) * 512],
                                    start=True, stop=True), 1)
                    v.wait_ge(tT.h, tT.n)
                    if zev_busy[pi]:
                        v.wait_ge(st.h, zev_busy[pi])
                    vv.inc(v.tensor_copy(zev[pi][:], mpsA[pi][:]), 1)
                    mps_busy[pi] = vv.n
                    g.wait_ge(vv.h, vv.n)
                    st.inc(g.dma_start(out=io["z2dram"][j::16, tc * 512:(tc + 1) * 512],
                                       in_=zev[pi][:]), 16)
                    zev_busy[pi] = st.n

        with ExitStack() as sB:
            EB = sB.enter_context
            wnat = EB(nc.sbuf_tensor("p3_wnat", [128, 2 * D], BF16))
            wstg = EB(nc.sbuf_tensor("p3_wstg", [128, 16 * 256], BF16))
            gbuf = [EB(nc.sbuf_tensor(f"p3_gbuf{i}", [128, 512], BF16)) for i in range(2)]
            vev = [EB(nc.sbuf_tensor(f"p3_vev{i}", [128, 512], BF16)) for i in range(2)]
            tpsB = [EB(nc.psum_tensor(f"p3_tpB{i}", [128, 128], BF16)) for i in range(2)]
            mpsB = [EB(nc.psum_tensor(f"p3_mpB{i}", [128, 512], F32)) for i in range(2)]
            t.wait_ge(vv.h, vv.n)
            for (wdram, wdraft, vfin, scol_off) in ((io["w_up"], io["wdraft"], io["vupF"], 0),
                                                    (io["w_gate"], io["wdraft"], io["vgtF"], 64)):
                tps_busy = [0, 0]
                wnat_busy = 0
                wstg_st = 0
                for ic in range(32):
                    if wnat_busy:
                        g.wait_ge(tT.h, wnat_busy)
                    ld.inc(g.dma_start(
                        out=wnat[:].rearrange("p (a d) -> p a d", a=2),
                        in_=wdram[ic * 256:(ic + 1) * 256, :].rearrange("(a p) d -> p a d", p=128)), 16)
                    v.wait_ge(ld.h, ld.n)
                    ins = None
                    for a in range(2):
                        ins = v.tensor_scalar(wnat[:, a * D:(a + 1) * D], wnat[:, a * D:(a + 1) * D],
                                              wsu[:, scol_off + ic * 2 + a:scol_off + ic * 2 + a + 1],
                                              None, ALU.mult)
                    vv.inc(ins, 1)
                    t.wait_ge(vv.h, vv.n)
                    if wstg_st:
                        v.wait_ge(st.h, wstg_st)
                    for a in range(2):
                        for kt in range(DK):
                            pi = kt % 2
                            if tps_busy[pi]:
                                t.wait_ge(vv.h, tps_busy[pi])
                            tT.inc(t.transpose(tpsB[pi][:], wnat[:, a * D + kt * 128: a * D + (kt + 1) * 128],
                                               ident[:]), 1)
                            v.wait_ge(tT.h, tT.n)
                            vv.inc(v.tensor_copy(wstg[:, kt * 256 + a * 128: kt * 256 + (a + 1) * 128],
                                                 tpsB[pi][:]), 1)
                            tps_busy[pi] = vv.n
                    wnat_busy = tT.n
                    g.wait_ge(vv.h, vv.n)
                    st.inc(g.dma_start(
                        out=wdraft.rearrange("(p kt) c -> p kt c", kt=16)[:, :, ic * 256:(ic + 1) * 256],
                        in_=wstg[:].rearrange("p (kt c) -> p kt c", c=256)), 16)
                    wstg_st = st.n
                g.wait_ge(st.h, st.n)
                gb_busy = [0, 0]
                mps_busy2 = [0, 0]
                vev_busy = [0, 0]
                for kt in range(DK):
                    for cchunk in range(16):
                        pi = cchunk % 2
                        if gb_busy[pi]:
                            g.wait_ge(tT.h, gb_busy[pi])
                        ld.inc(g.dma_start(
                            out=gbuf[pi][:],
                            in_=wdraft[kt * 128:(kt + 1) * 128, cchunk * 512:(cchunk + 1) * 512]), 16)
                        t.wait_ge(ld.h, ld.n)
                        if mps_busy2[pi]:
                            t.wait_ge(vv.h, mps_busy2[pi])
                        tT.inc(t.matmul(mpsB[pi][:], consts["i8h16"][:], gbuf[pi][:],
                                        start=True, stop=True), 1)
                        gb_busy[pi] = tT.n
                        v.wait_ge(tT.h, tT.n)
                        if vev_busy[pi]:
                            v.wait_ge(st.h, vev_busy[pi])
                        vv.inc(v.tensor_copy(vev[pi][:], mpsB[pi][:]), 1)
                        mps_busy2[pi] = vv.n
                        g.wait_ge(vv.h, vv.n)
                        st.inc(g.dma_start(out=vfin[kt * 128:(kt + 1) * 128,
                                                    cchunk * 512:(cchunk + 1) * 512],
                                           in_=vev[pi][:]), 16)
                        vev_busy[pi] = st.n

        with ExitStack() as sC:
            EC = sC.enter_context
            Sdh = EC(nc.sbuf_tensor("p3_Sdh", [128, 64 * 128], BF16))
            xpp = EC(nc.sbuf_tensor("p3_xpp", [128, 16 * S_TOK], BF16))
            Vu = [EC(nc.sbuf_tensor(f"p3_Vu{i}", [128, 16 * 128], BF16)) for i in range(2)]
            Vg = [EC(nc.sbuf_tensor(f"p3_Vg{i}", [128, 16 * 128], BF16)) for i in range(2)]
            sg2 = EC(nc.sbuf_tensor("p3_sg2", [128, 2 * 512], BF16))
            xtl = EC(nc.sbuf_tensor("p3_xtl", [128, 2 * 512], BF16))
            zev2 = [EC(nc.sbuf_tensor(f"p3_zev2{i}", [128, 512], BF16)) for i in range(2)]
            ps_g = [EC(nc.psum_tensor(f"p3_psg{i}", [128, 512], F32)) for i in range(2)]
            ps_u = [EC(nc.psum_tensor(f"p3_psu{i}", [128, 512], F32)) for i in range(2)]
            tpsC = [EC(nc.psum_tensor(f"p3_tpC{i}", [128, 128], BF16)) for i in range(2)]
            mpsC = [EC(nc.psum_tensor(f"p3_mpC{i}", [128, 512], F32)) for i in range(2)]
            t.wait_ge(vv.h, vv.n)
            v.wait_ge(ld.h, ld.n)
            ins = None
            for j in range(64):
                ins = v.tensor_scalar(Sdh[:, j * 128:(j + 1) * 128], consts["h128_8192"][:],
                                      sdp[:, j:j + 1], None, ALU.mult)
            vv.inc(ins, 1)
            t.wait_ge(vv.h, vv.n)
            xppv = lambda kt, tc: xpp[:, kt * S_TOK + tc * 512:(kt) * S_TOK + (tc + 1) * 512]
            for kt in range(DK):
                ld.inc(g.dma_start(
                    out=xpp[:, kt * S_TOK:(kt + 1) * S_TOK],
                    in_=io["z2dram"][kt * 128:(kt + 1) * 128, :]), 16)
            Vu_busy = [0, 0]
            Vg_busy = [0, 0]
            psg_busy = [0, 0]
            psu_busy = [0, 0]
            sg_busy = [0, 0]
            tps_busy = [0, 0]
            mps_busy = [0, 0]
            zev_busy = [0, 0]
            pending = []
            xtl_busy = [0, 0]
            t.wait_ge(ld.h, ld.n)

            def flush_pass1():
                while pending:
                    (pjg, ptc, ppar, pvv) = pending.pop(0)
                    piz = ptc % 2
                    t.wait_ge(vv.h, pvv)
                    if mps_busy[piz]:
                        t.wait_ge(vv.h, mps_busy[piz])
                    tT.inc(t.matmul(mpsC[piz][:], Sdh[:, pjg * 128:(pjg + 1) * 128],
                                    xtl[:, ppar * 512:(ppar + 1) * 512],
                                    start=True, stop=True), 1)
                    xtl_busy[ppar] = tT.n
                    v.wait_ge(tT.h, tT.n)
                    if zev_busy[piz]:
                        v.wait_ge(st.h, zev_busy[piz])
                    vv.inc(v.tensor_copy(zev2[piz][:], mpsC[piz][:]), 1)
                    mps_busy[piz] = vv.n
                    g.wait_ge(vv.h, vv.n)
                    st.inc(g.dma_start(out=io["zdram"][pjg::64,
                                                       ptc * 512:(ptc + 1) * 512],
                                       in_=zev2[piz][:]), 16)
                    zev_busy[piz] = st.n

            for jg in range(64):
                par = jg % 2
                if Vu_busy[par]:
                    g.wait_ge(tT.h, Vu_busy[par])
                ld.inc(g.dma_start(out=Vu[par][:].rearrange("p (kt m) -> p kt m", m=128),
                                   in_=io["vupF"].rearrange("(kt p) m -> p kt m", p=128)
                                   [:, :, jg * 128:(jg + 1) * 128]), 16)
                ld.inc(g.dma_start(out=Vg[par][:].rearrange("p (kt m) -> p kt m", m=128),
                                   in_=io["vgtF"].rearrange("(kt p) m -> p kt m", p=128)
                                   [:, :, jg * 128:(jg + 1) * 128]), 16)
                t.wait_ge(ld.h, ld.n)
                for tc in range(8):
                    pig = tc % 2
                    xpar = tc % 2
                    if psg_busy[pig]:
                        t.wait_ge(ss.h, psg_busy[pig])
                    ins = None
                    for kt in range(DK):
                        ins = t.matmul(ps_g[pig][:], Vg[par][:, kt * 128:(kt + 1) * 128],
                                       xppv(kt, tc), start=(kt == 0), stop=(kt == DK - 1))
                    tT.inc(ins, 1)
                    s.wait_ge(tT.h, tT.n)
                    if sg_busy[xpar]:
                        s.wait_ge(vv.h, sg_busy[xpar])
                    ss.inc(s.activation(sg2[:, xpar * 512:(xpar + 1) * 512], ps_g[pig][:], AF.Silu), 1)
                    psg_busy[pig] = ss.n
                    if psu_busy[pig]:
                        t.wait_ge(vv.h, psu_busy[pig])
                    ins = None
                    for kt in range(DK):
                        ins = t.matmul(ps_u[pig][:], Vu[par][:, kt * 128:(kt + 1) * 128],
                                       xppv(kt, tc), start=(kt == 0), stop=(kt == DK - 1))
                    tT.inc(ins, 1)
                    v.wait_ge(tT.h, tT.n)
                    v.wait_ge(ss.h, ss.n)
                    if xtl_busy[xpar]:
                        v.wait_ge(tT.h, xtl_busy[xpar])
                    vv.inc(v.tensor_tensor(xtl[:, xpar * 512:(xpar + 1) * 512],
                                           sg2[:, xpar * 512:(xpar + 1) * 512],
                                           ps_u[pig][:], ALU.mult), 1)
                    psu_busy[pig] = vv.n
                    sg_busy[xpar] = vv.n
                    pending.append((jg, tc, xpar, vv.n))
                    if len(pending) >= 2:
                        (pjg, ptc, ppar, pvv) = pending.pop(0)
                        piz = ptc % 2
                        t.wait_ge(vv.h, pvv)
                        if mps_busy[piz]:
                            t.wait_ge(vv.h, mps_busy[piz])
                        tT.inc(t.matmul(mpsC[piz][:], Sdh[:, pjg * 128:(pjg + 1) * 128],
                                        xtl[:, ppar * 512:(ppar + 1) * 512],
                                        start=True, stop=True), 1)
                        v.wait_ge(tT.h, tT.n)
                        if zev_busy[piz]:
                            v.wait_ge(st.h, zev_busy[piz])
                        vv.inc(v.tensor_copy(zev2[piz][:], mpsC[piz][:]), 1)
                        mps_busy[piz] = vv.n
                        xtl_busy[ppar] = tT.n
                        g.wait_ge(vv.h, vv.n)
                        st.inc(g.dma_start(out=io["zdram"][pjg::64,
                                                           ptc * 512:(ptc + 1) * 512],
                                           in_=zev2[piz][:]), 16)
                        zev_busy[piz] = st.n
                flush_pass1()
                Vu_busy[par] = tT.n
                Vg_busy[par] = tT.n

        with ExitStack() as sD:
            ED = sD.enter_context
            wnat3 = ED(nc.sbuf_tensor("p3_wnat3", [128, INTER], BF16))
            vstg = ED(nc.sbuf_tensor("p3_vstg", [128, 64 * 128], BF16))
            gbuf3 = [ED(nc.sbuf_tensor(f"p3_gbuf3{i}", [128, 512], BF16)) for i in range(2)]
            vev3 = [ED(nc.sbuf_tensor(f"p3_vev3{i}", [128, 512], BF16)) for i in range(2)]
            tpsD = [ED(nc.psum_tensor(f"p3_tpD{i}", [128, 128], BF16)) for i in range(2)]
            mpsD = [ED(nc.psum_tensor(f"p3_mpD{i}", [128, 512], F32)) for i in range(2)]
            t.wait_ge(vv.h, vv.n)
            tps_busy = [0, 0]
            for ot in range(16):
                g.wait_ge(tT.h, tT.n)
                g.wait_ge(st.h, st.n)
                ld.inc(g.dma_start(out=wnat3[:], in_=io["w_down"][ot * 128:(ot + 1) * 128, :]), 16)
                v.wait_ge(ld.h, ld.n)
                vv.inc(v.tensor_scalar(wnat3[:], wnat3[:], wsd[:, ot:ot + 1], None, ALU.mult), 1)
                t.wait_ge(vv.h, vv.n)
                for kt in range(64):
                    pi = kt % 2
                    if tps_busy[pi]:
                        t.wait_ge(vv.h, tps_busy[pi])
                    tT.inc(t.transpose(tpsD[pi][:], wnat3[:, kt * 128:(kt + 1) * 128], ident[:]), 1)
                    v.wait_ge(tT.h, tT.n)
                    vv.inc(v.tensor_copy(vstg[:, kt * 128:(kt + 1) * 128], tpsD[pi][:]), 1)
                    tps_busy[pi] = vv.n
                g.wait_ge(vv.h, vv.n)
                st.inc(g.dma_start(
                    out=io["wdraft2"].rearrange("(p kt) o -> p kt o", kt=64)[:, :, ot * 128:(ot + 1) * 128],
                    in_=vstg[:].rearrange("p (kt o) -> p kt o", o=128)), 16)
            g.wait_ge(st.h, st.n)
            gb_busy = [0, 0]
            mps_busy2 = [0, 0]
            vev_busy = [0, 0]
            for kt in range(64):
                for cchunk in range(4):
                    pi = cchunk % 2
                    if gb_busy[pi]:
                        g.wait_ge(tT.h, gb_busy[pi])
                    ld.inc(g.dma_start(
                        out=gbuf3[pi][:],
                        in_=io["wdraft2"][kt * 128:(kt + 1) * 128, cchunk * 512:(cchunk + 1) * 512]), 16)
                    t.wait_ge(ld.h, ld.n)
                    if mps_busy2[pi]:
                        t.wait_ge(vv.h, mps_busy2[pi])
                    tT.inc(t.matmul(mpsD[pi][:], consts["i2h64"][:], gbuf3[pi][:],
                                    start=True, stop=True), 1)
                    gb_busy[pi] = tT.n
                    v.wait_ge(tT.h, tT.n)
                    if vev_busy[pi]:
                        v.wait_ge(st.h, vev_busy[pi])
                    vv.inc(v.tensor_copy(vev3[pi][:], mpsD[pi][:]), 1)
                    mps_busy2[pi] = vv.n
                    g.wait_ge(vv.h, vv.n)
                    st.inc(g.dma_start(out=io["vdown"][kt * 128:(kt + 1) * 128,
                                                       cchunk * 512:(cchunk + 1) * 512],
                                       in_=vev3[pi][:]), 16)
                    vev_busy[pi] = st.n

        with ExitStack() as sE:
            EE = sE.enter_context
            xdd = EE(nc.sbuf_tensor("p3_xdd", [128, 64 * 512], BF16))
            voc = [EE(nc.sbuf_tensor(f"p3_voc{i}", [128, 64 * 256], BF16)) for i in range(2)]
            x1tl = EE(nc.sbuf_tensor("p3_x1tl", [128, 4 * D], BF16))
            xpr = EE(nc.sbuf_tensor("p3_xpr", [128, 4 * D], BF16))
            x2tl = EE(nc.sbuf_tensor("p3_x2t", [128, D], F32))
            cols = EE(nc.sbuf_tensor("p3_cols", [128, 8], F32))
            ps_d = [EE(nc.psum_tensor(f"p3_psd{i}", [128, 256], F32)) for i in range(4)]

            psd_busy = [0] * 4
            voc_busy = [0, 0]
            x1_busy = 0
            x2_busy = 0
            xdd_busy = 0

            for tb in range(8):
                if xdd_busy:
                    g.wait_ge(tT.h, xdd_busy)
                for kt in range(64):
                    ld.inc(g.dma_start(
                        out=xdd[:, kt * 512:(kt + 1) * 512],
                        in_=io["zdram"][kt * 128:(kt + 1) * 128, tb * 512:(tb + 1) * 512]), 16)
                if x1_busy:
                    g.wait_ge(vv.h, x1_busy)
                ld.inc(g.dma_start(out=x1tl[:].rearrange("p (a d) -> p a d", a=4),
                                   in_=io["x1"][tb * 512:(tb + 1) * 512, :].rearrange("(a p) d -> p a d", p=128)), 16)
                t.wait_ge(ld.h, ld.n)
                for oc in range(8):
                    vpar = oc % 2
                    if voc_busy[vpar]:
                        g.wait_ge(tT.h, voc_busy[vpar])
                    ld.inc(g.dma_start(
                        out=voc[vpar][:].rearrange("p (k o) -> p k o", o=256),
                        in_=io["vdown"][:, oc * 256:(oc + 1) * 256].rearrange("(k p) o -> p k o", p=128)), 16)
                    t.wait_ge(ld.h, ld.n)
                    for ts_ in range(4):
                        pid = (oc * 4 + ts_) % 4
                        if psd_busy[pid]:
                            t.wait_ge(vv.h, psd_busy[pid])
                        ins = None
                        for kt in range(64):
                            ins = t.matmul(ps_d[pid][:],
                                           xdd[:, kt * 512 + ts_ * 128:(kt) * 512 + (ts_ + 1) * 128],
                                           voc[vpar][:, kt * 256:(kt + 1) * 256],
                                           start=(kt == 0), stop=(kt == 63))
                        tT.inc(ins, 1)
                        v.wait_ge(tT.h, tT.n)
                        v.wait_ge(ld.h, ld.n)
                        vv.inc(v.tensor_tensor(xpr[:, ts_ * D + oc * 256: ts_ * D + (oc + 1) * 256],
                                               ps_d[pid][:],
                                               x1tl[:, ts_ * D + oc * 256: ts_ * D + (oc + 1) * 256],
                                               ALU.add), 1)
                        psd_busy[pid] = vv.n
                    voc_busy[vpar] = tT.n
                x1_busy = vv.n
                xdd_busy = tT.n
                for ts_ in range(4):
                    ti = tb * 4 + ts_
                    if x2_busy:
                        v.wait_ge(st.h, x2_busy)
                    xp = xpr[:, ts_ * D:(ts_ + 1) * D]
                    vv.inc(v.tensor_reduce(cols[:, 0:1], xp, mybir.AxisListType.X, ALU.add), 1)
                    v.wait_ge(vv.h, vv.n)
                    v.scalar_tensor_tensor(x2tl[:], cols[:, 0:1].broadcast_to([128, D]), -1.0 / D,
                                           xp, ALU.mult, ALU.add)
                    v.tensor_tensor(xp, x2tl[:], x2tl[:], ALU.mult)
                    vv.inc(v.tensor_reduce(cols[:, 2:3], xp, mybir.AxisListType.X, ALU.add), 1)
                    s.wait_ge(vv.h, vv.n)
                    ss.inc(emit_rstd(s, consts, cols[:, 2:3], cols[:, 3:4], 1.0 / D, cols[:, 4:5]), 1)
                    v.wait_ge(ss.h, ss.n)
                    vv.inc(v.tensor_scalar(x2tl[:], x2tl[:], cols[:, 3:4], None, ALU.mult), 1)
                    g.wait_ge(vv.h, vv.n)
                    st.inc(g.dma_start(out=io["x2"][ti * 128:(ti + 1) * 128, :], in_=x2tl[:]), 16)
                    x2_busy = st.n
        g.wait_ge(st.h, st.n)
        sy.wait_ge(st.h, st.n)
        return {"vv": (vv, vv.n), "ss": (ss, ss.n), "tT": (tT, tT.n), "st": (st, st.n)}


def build_p3():
    nc = bass.Bass("TRN2", target_bir_lowering=False)
    io = {}
    io["x1"] = nc.dram_tensor("x1", [S_TOK, D], F32, kind="ExternalInput").ap()
    io["su_ug"] = nc.dram_tensor("su_ug", [1, D], F32, kind="ExternalInput").ap()
    io["su_dp"] = nc.dram_tensor("su_dp", [1, INTER], F32, kind="ExternalInput").ap()
    io["wsc_ug"] = nc.dram_tensor("wsc_ug", [1, 2 * INTER], F32, kind="ExternalInput").ap()
    io["wsc_dp"] = nc.dram_tensor("wsc_dp", [1, D], F32, kind="ExternalInput").ap()
    io["w_up"] = nc.dram_tensor("w_up", [INTER, D], F32, kind="ExternalInput").ap()
    io["w_gate"] = nc.dram_tensor("w_gate", [INTER, D], F32, kind="ExternalInput").ap()
    io["w_down"] = nc.dram_tensor("w_down", [D, INTER], F32, kind="ExternalInput").ap()
    io["x2"] = nc.dram_tensor("x2", [S_TOK, D], F32, kind="ExternalOutput").ap()
    io["zdram"] = nc.dram_tensor("zdram", [INTER, S_TOK], BF16, kind="Internal").ap()
    io["vdown"] = nc.dram_tensor("vdown", [INTER, D], BF16, kind="Internal").ap()
    io["z2dram"] = nc.dram_tensor("z2dram", [D, S_TOK], BF16, kind="Internal").ap()
    io["wdraft"] = nc.dram_tensor("wdraft", [D, INTER], BF16, kind="Internal").ap()
    io["wdraft2"] = nc.dram_tensor("wdraft2", [INTER, D], BF16, kind="Internal").ap()
    io["vupF"] = nc.dram_tensor("vupF", [D, INTER], BF16, kind="Internal").ap()
    io["vgtF"] = nc.dram_tensor("vgtF", [D, INTER], BF16, kind="Internal").ap()
    with ExitStack() as stack:
        ctx = Ctx(nc, stack)
        eng = {"t": nc.tensor, "v": nc.vector, "s": nc.scalar, "g": nc.gpsimd, "sy": nc.sync}
        ld0 = ctx.sem("c_ld")
        consts = load_consts(ctx, nc.gpsimd, ld0, keys=("ident", "h128_2048", "h128_8192", "i8h16", "i2h64", "csts"))
        for e in (nc.tensor, nc.vector, nc.scalar, nc.sync):
            e.wait_ge(ld0.h, ld0.n)
        phase3(ctx, io, consts, eng)
    return nc



def phase4(ctx, io, consts, eng, entry=None):
    nc, t, v, s, g, sy = ctx.nc, eng["t"], eng["v"], eng["s"], eng["g"], eng["sy"]
    ident = consts["ident"]
    NQC = 8
    with ExitStack() as ph:
        E = ph.enter_context
        ld = ctx.sem("p4_ld")
        tT = ctx.sem("p4_tT")
        vv = ctx.sem("p4_vv")
        ss = ctx.sem("p4_ss")
        st = ctx.sem("p4_st")
        if entry:
            for e in (t, v, s, g, sy):
                for key, (sem, val) in entry.items():
                    e.wait_ge(sem.h, val)

        with ExitStack() as sA:
            EA = sA.enter_context
            x2ch = EA(nc.sbuf_tensor("p4_x2ch", [128, DK * S_TOK], BF16))
            xt = EA(nc.sbuf_tensor("p4_xt", [128, D], BF16))
            wqm = [EA(nc.sbuf_tensor(f"p4_wqm{i}", [128, 16 * 128], BF16)) for i in range(2)]
            wvcb = [EA(nc.sbuf_tensor(f"p4_wvcb{i}", [128, 16 * 512], BF16)) for i in range(2)]
            ev = [EA(nc.sbuf_tensor(f"p4_ev{i}", [128, 512], BF16)) for i in range(2)]
            tps = [EA(nc.psum_tensor(f"p4_tp{i}", [128, 128], BF16)) for i in range(2)]
            mps = [EA(nc.psum_tensor(f"p4_mp{i}", [128, 512], F32)) for i in range(4)]

            tps_busy = [0, 0]
            mps_busy = [0] * 4
            ev_busy = [0, 0]
            xt_busy = 0
            xch_view = lambda kt, c0, w: x2ch[:, kt * S_TOK + c0: kt * S_TOK + c0 + w]

            for ti in range(NT):
                if xt_busy:
                    g.wait_ge(tT.h, xt_busy)
                ld.inc(g.dma_start(out=xt[:], in_=io["x2"][ti * 128:(ti + 1) * 128, :]), 16)
                t.wait_ge(ld.h, ld.n)
                for kt in range(DK):
                    pi = kt % 2
                    if tps_busy[pi]:
                        t.wait_ge(vv.h, tps_busy[pi])
                    tT.inc(t.transpose(tps[pi][:], xt[:, kt * 128:(kt + 1) * 128], ident[:]), 1)
                    v.wait_ge(tT.h, tT.n)
                    vv.inc(v.tensor_copy(xch_view(kt, ti * 128, 128), tps[pi][:]), 1)
                    tps_busy[pi] = vv.n
                xt_busy = tT.n
            t.wait_ge(vv.h, vv.n)
            for (wdram, outdram) in ((io["wq"], io["qT"]), (io["wk"], io["kT"])):
                wq_busy = [0, 0]
                for m in range(16):
                    par = m % 2
                    if wq_busy[par]:
                        g.wait_ge(tT.h, wq_busy[par])
                    ld.inc(g.dma_start(out=wqm[par][:].rearrange("p (k c) -> p k c", c=128),
                                       in_=wdram[:, m * 128:(m + 1) * 128]
                                       .rearrange("(k p) c -> p k c", p=128)), 16)
                    t.wait_ge(ld.h, ld.n)
                    for tc in range(NQC):
                        pi = tc % 4
                        if mps_busy[pi]:
                            t.wait_ge(vv.h, mps_busy[pi])
                        ins = None
                        for kt in range(DK):
                            ins = t.matmul(mps[pi][:], wqm[par][:, kt * 128:(kt + 1) * 128],
                                           xch_view(kt, tc * 512, 512),
                                           start=(kt == 0), stop=(kt == DK - 1))
                        tT.inc(ins, 1)
                        pe = tc % 2
                        v.wait_ge(tT.h, tT.n)
                        if ev_busy[pe]:
                            v.wait_ge(st.h, ev_busy[pe])
                        vv.inc(v.tensor_copy(ev[pe][:], mps[pi][:]), 1)
                        mps_busy[pi] = vv.n
                        g.wait_ge(vv.h, vv.n)
                        st.inc(g.dma_start(out=outdram[m * 128:(m + 1) * 128, tc * 512:(tc + 1) * 512],
                                           in_=ev[pe][:]), 16)
                        ev_busy[pe] = st.n
                    wq_busy[par] = tT.n
            wv_busy = [0, 0]
            for oc in range(4):
                par = oc % 2
                if wv_busy[par]:
                    g.wait_ge(tT.h, wv_busy[par])
                ld.inc(g.dma_start(out=wvcb[par][:].rearrange("p (k c) -> p k c", c=512),
                                   in_=io["wv"][:, oc * 512:(oc + 1) * 512]
                                   .rearrange("(k p) c -> p k c", p=128)), 16)
                t.wait_ge(ld.h, ld.n)
                for ti in range(NT):
                    pi = ti % 4
                    if mps_busy[pi]:
                        t.wait_ge(vv.h, mps_busy[pi])
                    ins = None
                    for kt in range(DK):
                        ins = t.matmul(mps[pi][:], xch_view(kt, ti * 128, 128),
                                       wvcb[par][:, kt * 512:(kt + 1) * 512],
                                       start=(kt == 0), stop=(kt == DK - 1))
                    tT.inc(ins, 1)
                    pe = ti % 2
                    v.wait_ge(tT.h, tT.n)
                    if ev_busy[pe]:
                        v.wait_ge(st.h, ev_busy[pe])
                    vv.inc(v.tensor_copy(ev[pe][:], mps[pi][:]), 1)
                    mps_busy[pi] = vv.n
                    g.wait_ge(vv.h, vv.n)
                    st.inc(g.dma_start(out=io["vtok"][ti * 128:(ti + 1) * 128, oc * 512:(oc + 1) * 512],
                                       in_=ev[pe][:]), 16)
                    ev_busy[pe] = st.n
                wv_busy[par] = tT.n

        with ExitStack() as sB:
            EB = sB.enter_context
            qTh = EB(nc.sbuf_tensor("p4_qTh", [128, S_TOK], BF16))
            kTh = EB(nc.sbuf_tensor("p4_kTh", [128, S_TOK], BF16))
            vh = EB(nc.sbuf_tensor("p4_vh", [128, 32 * 128], BF16))
            Ebuf = EB(nc.sbuf_tensor("p4_E", [128, 36 * 512], BF16))
            vsufT = EB(nc.sbuf_tensor("p4_vsufT", [128, S_TOK], BF16))
            bsums = EB(nc.sbuf_tensor("p4_bs", [1, 32 * 128], BF16))
            carr4 = EB(nc.sbuf_tensor("p4_carr", [1, 32 * 128], BF16))
            denr = EB(nc.sbuf_tensor("p4_denr", [1, 1024], F32))
            recb = EB(nc.sbuf_tensor("p4_recb", [1, 512], BF16))
            oev = [EB(nc.sbuf_tensor(f"p4_oev{i}", [128, 512], BF16)) for i in range(2)]
            wrk = EB(nc.sbuf_tensor("p4_wrk", [128, 512], F32))
            cnt_sb = EB(nc.sbuf_tensor("p4_cnt", [1, S_TOK], F32))
            ps_sc = [EB(nc.psum_tensor(f"p4_psc{i}", [128, 512], F32)) for i in range(2)]
            ps_av = EB(nc.psum_tensor("p4_psav", [128, 512], F32))
            ps_dn = EB(nc.psum_tensor("p4_psdn", [1, 512], F32))
            ps_vs = EB(nc.psum_tensor("p4_psvs", [128, 128], F32))
            ps_bc = EB(nc.psum_tensor("p4_psbc", [128, 512], F32))
            ps_b1 = EB(nc.psum_tensor("p4_psb1", [1, 128], F32))

            t.wait_ge(vv.h, vv.n)
            ld.inc(g.dma_start(out=cnt_sb[:], in_=io["cntrow"]), 16)
            qk_busy = 0
            Efree = [0] * 2
            E_busy = 0
            av_busy = 0
            vs_busy = 0
            bc_busy = 0
            oev_busy = [0, 0]
            dn_busy = 0
            b1_busy = 0

            for h in range(16):
                if qk_busy:
                    g.wait_ge(tT.h, qk_busy)
                ld.inc(g.dma_start(out=qTh[:], in_=io["qT"][h * 128:(h + 1) * 128, :]), 16)
                ld.inc(g.dma_start(out=kTh[:], in_=io["kT"][h * 128:(h + 1) * 128, :]), 16)
                ld.inc(g.dma_start(out=vh[:].rearrange("p (k c) -> p k c", c=128),
                                   in_=io["vtok"][:, h * 128:(h + 1) * 128]
                                   .rearrange("(k p) c -> p k c", p=128)), 16)
                t.wait_ge(ld.h, ld.n)
                for kb in range(NT):
                    if b1_busy:
                        t.wait_ge(vv.h, b1_busy)
                    tT.inc(t.matmul(ps_b1[:], consts["ones_col_bf"][:], vh[:, kb * 128:(kb + 1) * 128],
                                    start=True, stop=True), 1)
                    v.wait_ge(tT.h, tT.n)
                    vv.inc(v.tensor_copy(bsums[:, kb * 128:(kb + 1) * 128], ps_b1[:]), 1)
                    b1_busy = vv.n
                v.tensor_copy(carr4[:, 30 * 128:31 * 128], bsums[:, 31 * 128:32 * 128])
                ins = None
                for kb in range(29, -1, -1):
                    ins = v.tensor_tensor(carr4[:, kb * 128:(kb + 1) * 128],
                                          carr4[:, (kb + 1) * 128:(kb + 2) * 128],
                                          bsums[:, (kb + 1) * 128:(kb + 2) * 128], ALU.add)
                vv.inc(ins, 1)
                t.wait_ge(vv.h, vv.n)
                for kb in range(NT):
                    if vs_busy:
                        t.wait_ge(vv.h, vs_busy)
                    if kb < 31:
                        t.matmul(ps_vs[:], vh[:, kb * 128:(kb + 1) * 128], consts["slower_bf"][:],
                                 start=True, stop=False, skip_group_check=True)
                        ins = t.matmul(ps_vs[:], carr4[0:1, kb * 128:(kb + 1) * 128],
                                       consts["ones_row_bf"][:], start=False, stop=True,
                                       skip_group_check=True)
                    else:
                        ins = t.matmul(ps_vs[:], vh[:, kb * 128:(kb + 1) * 128], consts["slower_bf"][:],
                                       start=True, stop=True)
                    tT.inc(ins, 1)
                    v.wait_ge(tT.h, tT.n)
                    vv.inc(v.tensor_copy(vsufT[:, kb * 128:(kb + 1) * 128], ps_vs[:]), 1)
                    vs_busy = vv.n
                for qc in range(NQC):
                    nkb = 4 * (qc + 1)
                    if E_busy:
                        s.wait_ge(tT.h, E_busy)
                    for kb in range(nkb):
                        pi = kb % 2
                        if Efree[pi]:
                            t.wait_ge(ss.h, Efree[pi])
                        tT.inc(t.matmul(ps_sc[pi][:], kTh[:, kb * 128:(kb + 1) * 128],
                                        qTh[:, qc * 512:(qc + 1) * 512], start=True, stop=True), 1)
                        s.wait_ge(tT.h, tT.n)
                        ss.inc(s.activation(Ebuf[:, kb * 512:(kb + 1) * 512], ps_sc[pi][:],
                                            AF.Exp, scale=1.0 / 64.0), 1)
                        Efree[pi] = ss.n
                    v.wait_ge(ss.h, ss.n)
                    ins = None
                    for o in range(4):
                        kb = qc * 4 + o
                        ins = v.tensor_tensor(Ebuf[:, kb * 512:(kb + 1) * 512],
                                              Ebuf[:, kb * 512:(kb + 1) * 512],
                                              consts["qsmask"][:, o * 512:(o + 1) * 512], ALU.mult)
                    vv.inc(ins, 1)
                    maskdone = vv.n
                    t.wait_ge(vv.h, maskdone)
                    if av_busy:
                        t.wait_ge(vv.h, av_busy)
                    for kb in range(nkb):
                        tT.inc(t.matmul(ps_av[:], vh[:, kb * 128:(kb + 1) * 128],
                                        Ebuf[:, kb * 512:(kb + 1) * 512],
                                        start=(kb == 0), stop=(kb == nkb - 1)), 1)
                    if dn_busy:
                        t.wait_ge(vv.h, dn_busy)
                    for kb in range(nkb):
                        tT.inc(t.matmul(ps_dn[:], consts["ones_col_bf"][:],
                                        Ebuf[:, kb * 512:(kb + 1) * 512],
                                        start=(kb == 0), stop=(kb == nkb - 1)), 1)
                    E_busy = tT.n
                    v.wait_ge(tT.h, tT.n)
                    v.tensor_tensor(denr[:, 0:512], ps_dn[:], cnt_sb[:, qc * 512:(qc + 1) * 512], ALU.add)
                    dn_busy = vv.n + 2
                    vv.inc(v.reciprocal(denr[:, 512:1024], denr[:, 0:512]), 1)
                    v.wait_ge(vv.h, vv.n)
                    vv.inc(v.tensor_copy(recb[:], denr[:, 512:1024]), 1)
                    t.wait_ge(vv.h, vv.n)
                    if bc_busy:
                        t.wait_ge(vv.h, bc_busy)
                    tT.inc(t.matmul(ps_bc[:], consts["ones_row_bf"][:], recb[:],
                                    start=True, stop=True), 1)
                    v.wait_ge(tT.h, tT.n)
                    v.tensor_tensor(wrk[:], ps_av[:], vsufT[:, qc * 512:(qc + 1) * 512], ALU.add)
                    av_busy = vv.n + 1
                    pe = qc % 2
                    if oev_busy[pe]:
                        v.wait_ge(st.h, oev_busy[pe])
                    vv.inc(v.tensor_tensor(oev[pe][:], wrk[:], ps_bc[:], ALU.mult), 1)
                    bc_busy = vv.n
                    g.wait_ge(vv.h, vv.n)
                    st.inc(g.dma_start(out=io["attnT"][h * 128:(h + 1) * 128, qc * 512:(qc + 1) * 512],
                                       in_=oev[pe][:]), 16)
                    oev_busy[pe] = st.n
                qk_busy = tT.n

        with ExitStack() as sC:
            EC = sC.enter_context
            aT = EC(nc.sbuf_tensor("p4_aT", [128, 16 * 128], BF16))
            wof = EC(nc.sbuf_tensor("p4_wof", [128, 16 * D], BF16))
            x2tl = EC(nc.sbuf_tensor("p4_x2tl", [128, D], F32))
            xpr = EC(nc.sbuf_tensor("p4_xpr", [128, D], F32))
            x3tl = EC(nc.sbuf_tensor("p4_x3t", [128, D], F32))
            cols = EC(nc.sbuf_tensor("p4_cols", [128, 8], F32))
            mps4 = [EC(nc.psum_tensor(f"p4_mp4{i}", [128, 512], F32)) for i in range(4)]
            t.wait_ge(vv.h, vv.n)
            ld.inc(g.dma_start(out=wof[:].rearrange("p (k c) -> p k c", c=D),
                               in_=io["wo"].rearrange("(k p) c -> p k c", p=128)), 16)
            mps_busy = [0] * 4
            aT_busy = 0
            x2_busy = 0
            x3_busy = 0
            for ti in range(NT):
                if aT_busy:
                    g.wait_ge(tT.h, aT_busy)
                ld.inc(g.dma_start(out=aT[:].rearrange("p (k c) -> p k c", c=128),
                                   in_=io["attnT"][:, ti * 128:(ti + 1) * 128]
                                   .rearrange("(k p) c -> p k c", p=128)), 16)
                if x2_busy:
                    g.wait_ge(vv.h, x2_busy)
                ld.inc(g.dma_start(out=x2tl[:], in_=io["x2"][ti * 128:(ti + 1) * 128, :]), 16)
                t.wait_ge(ld.h, ld.n)
                for oc in range(4):
                    pi = oc % 4
                    if mps_busy[pi]:
                        t.wait_ge(vv.h, mps_busy[pi])
                    ins = None
                    for kt in range(DK):
                        ins = t.matmul(mps4[pi][:], aT[:, kt * 128:(kt + 1) * 128],
                                       wof[:, kt * D + oc * 512: kt * D + (oc + 1) * 512],
                                       start=(kt == 0), stop=(kt == DK - 1))
                    tT.inc(ins, 1)
                    v.wait_ge(tT.h, tT.n)
                    vv.inc(v.tensor_tensor(xpr[:, oc * 512:(oc + 1) * 512], mps4[pi][:],
                                           x2tl[:, oc * 512:(oc + 1) * 512], ALU.add), 1)
                    mps_busy[pi] = vv.n
                aT_busy = tT.n
                x2_busy = vv.n
                if x3_busy:
                    v.wait_ge(st.h, x3_busy)
                vv.inc(v.tensor_reduce(cols[:, 0:1], xpr[:], mybir.AxisListType.X, ALU.add), 1)
                v.wait_ge(vv.h, vv.n)
                v.scalar_tensor_tensor(x3tl[:], cols[:, 0:1].broadcast_to([128, D]), -1.0 / D,
                                       xpr[:], ALU.mult, ALU.add)
                v.tensor_tensor(xpr[:], x3tl[:], x3tl[:], ALU.mult)
                vv.inc(v.tensor_reduce(cols[:, 2:3], xpr[:], mybir.AxisListType.X, ALU.add), 1)
                s.wait_ge(vv.h, vv.n)
                ss.inc(emit_rstd(s, consts, cols[:, 2:3], cols[:, 3:4], 1.0 / D, cols[:, 4:5]), 1)
                v.wait_ge(ss.h, ss.n)
                vv.inc(v.tensor_scalar(x3tl[:], x3tl[:], cols[:, 3:4], None, ALU.mult), 1)
                g.wait_ge(vv.h, vv.n)
                st.inc(g.dma_start(out=io["x3"][ti * 128:(ti + 1) * 128, :], in_=x3tl[:]), 16)
                x3_busy = st.n
        g.wait_ge(st.h, st.n)
        sy.wait_ge(st.h, st.n)
        return {"vv": (vv, vv.n), "ss": (ss, ss.n), "tT": (tT, tT.n), "st": (st, st.n)}


def build_p4():
    nc = bass.Bass("TRN2", target_bir_lowering=False)
    io = {}
    io["x2"] = nc.dram_tensor("x2", [S_TOK, D], F32, kind="ExternalInput").ap()
    io["wq"] = nc.dram_tensor("wq", [D, D], F32, kind="ExternalInput").ap()
    io["wk"] = nc.dram_tensor("wk", [D, D], F32, kind="ExternalInput").ap()
    io["wv"] = nc.dram_tensor("wv", [D, D], F32, kind="ExternalInput").ap()
    io["wo"] = nc.dram_tensor("wo", [D, D], F32, kind="ExternalInput").ap()
    io["cntrow"] = nc.dram_tensor("cntrow", [1, S_TOK], F32, kind="ExternalInput").ap()
    io["x3"] = nc.dram_tensor("x3", [S_TOK, D], F32, kind="ExternalOutput").ap()
    io["qT"] = nc.dram_tensor("qT", [D, S_TOK], BF16, kind="Internal").ap()
    io["kT"] = nc.dram_tensor("kT", [D, S_TOK], BF16, kind="Internal").ap()
    io["vtok"] = nc.dram_tensor("vtok", [S_TOK, D], BF16, kind="Internal").ap()
    io["attnT"] = nc.dram_tensor("attnT", [D, S_TOK], BF16, kind="Internal").ap()
    with ExitStack() as stack:
        ctx = Ctx(nc, stack)
        eng = {"t": nc.tensor, "v": nc.vector, "s": nc.scalar, "g": nc.gpsimd, "sy": nc.sync}
        ld0 = ctx.sem("c_ld")
        consts = load_consts(ctx, nc.gpsimd, ld0, keys=("ident", "ones_col_bf", "ones_row_bf", "slower_bf", "qsmask", "csts"))
        for e in (nc.tensor, nc.vector, nc.scalar, nc.sync):
            e.wait_ge(ld0.h, ld0.n)
        phase4(ctx, io, consts, eng)
    return nc

def phase5(ctx, io, consts, eng, entry=None):
    nc, t, v, s, g, sy = ctx.nc, eng["t"], eng["v"], eng["s"], eng["g"], eng["sy"]
    ident = consts["ident"]
    with ExitStack() as ph:
        E = ph.enter_context
        ld = ctx.sem("p5_ld")
        tT = ctx.sem("p5_tT")
        vv = ctx.sem("p5_vv")
        ss = ctx.sem("p5_ss")
        st = ctx.sem("p5_st")
        if entry:
            for e in (t, v, s, g, sy):
                for key, (sem, val) in entry.items():
                    e.wait_ge(sem.h, val)

        with ExitStack() as sA:
            EA = sA.enter_context
            atb = EA(nc.sbuf_tensor("p5_atb", [16, D], BF16))
            bub = EA(nc.sbuf_tensor("p5_bub", [16, INTER], BF16))
            adb = EA(nc.sbuf_tensor("p5_adb", [16, INTER], BF16))
            bdb = EA(nc.sbuf_tensor("p5_bdb", [16, D], BF16))
            aload = EA(nc.sbuf_tensor("p5_aload", [128, 80 * 16], BF16))
            qi = EA(nc.sbuf_tensor("p5_qi", [128, INTER], I32))
            dq = EA(nc.sbuf_tensor("p5_dq", [128, 4096], F32))
            dqb = EA(nc.sbuf_tensor("p5_dqb", [128, INTER], BF16))
            ssb = EA(nc.sbuf_tensor("p5_ssb", [128, 128], F32))
            stg = EA(nc.sbuf_tensor("p5_stg", [128, 64 * 128], BF16))
            tps = [EA(nc.psum_tensor(f"p5_tp{i}", [128, 128], BF16)) for i in range(2)]
            abp = [EA(nc.psum_tensor(f"p5_ab{i}", [128, 128], F32)) for i in range(2)]

            ld.inc(g.dma_start(out=aload[:, 0:16 * 16].rearrange("p (k r) -> p k r", r=16),
                               in_=io["alup"].rearrange("(k p) r -> p k r", p=128)), 16)
            ld.inc(g.dma_start(out=bub[:], in_=io["blup"]), 16)
            ld.inc(g.dma_start(out=aload[:, 256:256 + 64 * 16].rearrange("p (k r) -> p k r", r=16),
                               in_=io["aldn"].rearrange("(k p) r -> p k r", p=128)), 16)
            ld.inc(g.dma_start(out=bdb[:], in_=io["bldn"]), 16)
            t.wait_ge(ld.h, ld.n)
            tps_busy = [0, 0]
            for kt in range(16):
                pi = kt % 2
                if tps_busy[pi]:
                    t.wait_ge(vv.h, tps_busy[pi])
                tT.inc(t.transpose(tps[pi][0:16, :], aload[:, kt * 16:(kt + 1) * 16], ident[:]), 1)
                v.wait_ge(tT.h, tT.n)
                vv.inc(v.tensor_copy(atb[:, kt * 128:(kt + 1) * 128], tps[pi][0:16, :]), 1)
                tps_busy[pi] = vv.n
            for kt in range(64):
                pi = kt % 2
                if tps_busy[pi]:
                    t.wait_ge(vv.h, tps_busy[pi])
                tT.inc(t.transpose(tps[pi][0:16, :], aload[:, 256 + kt * 16:256 + (kt + 1) * 16],
                                   ident[:]), 1)
                v.wait_ge(tT.h, tT.n)
                vv.inc(v.tensor_copy(adb[:, kt * 128:(kt + 1) * 128], tps[pi][0:16, :]), 1)
                tps_busy[pi] = vv.n

            ab_busy = [0, 0]
            stg_st = 0
            qi_busy = 0
            dqb_busy = 0
            for ht in range(64):
                if qi_busy:
                    g.wait_ge(vv.h, qi_busy)
                ld.inc(g.dma_start(out=qi[:, 0:D], in_=io["wupq"][ht * 128:(ht + 1) * 128, :]), 16)
                ld.inc(g.dma_start(out=ssb[:, 0:32], in_=io["wups"][ht * 128:(ht + 1) * 128, :]), 16)
                v.wait_ge(ld.h, ld.n)
                if dqb_busy:
                    v.wait_ge(tT.h, dqb_busy)
                v.tensor_scalar(dq[:, 0:D], qi[:, 0:D], -7.5, None, ALU.add)
                vv.inc(v.tensor_tensor(dqb[:, 0:D].rearrange("p (a c) -> p a c", c=64),
                                       dq[:, 0:D].rearrange("p (a c) -> p a c", c=64),
                                       rep_inner(ssb[:, 0:32], 32, 64), ALU.mult), 1)
                qi_busy = vv.n
                t.wait_ge(vv.h, vv.n)
                if stg_st:
                    v.wait_ge(st.h, stg_st)
                for dt in range(16):
                    pi = dt % 2
                    if tps_busy[pi]:
                        t.wait_ge(vv.h, tps_busy[pi])
                    tT.inc(t.transpose(tps[pi][:], dqb[:, dt * 128:(dt + 1) * 128], ident[:]), 1)
                    v.wait_ge(tT.h, tT.n)
                    vv.inc(v.tensor_copy(stg[:, dt * 128:(dt + 1) * 128], tps[pi][:]), 1)
                    tps_busy[pi] = vv.n
                    if ab_busy[pi]:
                        t.wait_ge(vv.h, ab_busy[pi])
                    tT.inc(t.matmul(abp[pi][:], atb[:, dt * 128:(dt + 1) * 128],
                                    bub[:, ht * 128:(ht + 1) * 128], start=True, stop=True), 1)
                    v.wait_ge(tT.h, tT.n)
                    v.wait_ge(vv.h, vv.n)
                    vv.inc(v.tensor_tensor(stg[:, dt * 128:(dt + 1) * 128], abp[pi][:],
                                           stg[:, dt * 128:(dt + 1) * 128], ALU.add), 1)
                    ab_busy[pi] = vv.n
                dqb_busy = tT.n
                g.wait_ge(vv.h, vv.n)
                st.inc(g.dma_start(
                    out=io["weffu"].rearrange("(kt p) c -> p kt c", p=128)[:, 0:16, ht * 128:(ht + 1) * 128],
                    in_=stg[:, 0:16 * 128].rearrange("p (kt c) -> p kt c", c=128)), 16)
                stg_st = st.n

            for dt in range(16):
                for hh in range(2):
                    ld.inc(g.dma_start(out=qi[:, hh * 4096:(hh + 1) * 4096],
                                       in_=io["wdnq"][dt * 128:(dt + 1) * 128,
                                                      hh * 4096:(hh + 1) * 4096]), 16)
                ld.inc(g.dma_start(out=ssb[:], in_=io["wdns"][dt * 128:(dt + 1) * 128, :]), 16)
                v.wait_ge(ld.h, ld.n)
                if stg_st:
                    v.wait_ge(st.h, stg_st)
                if dqb_busy:
                    v.wait_ge(tT.h, dqb_busy)
                for hh in range(2):
                    v.tensor_scalar(dq[:], qi[:, hh * 4096:(hh + 1) * 4096], -7.5, None, ALU.add)
                    vv.inc(v.tensor_tensor(
                        dqb[:, hh * 4096:(hh + 1) * 4096].rearrange("p (a c) -> p a c", c=64),
                        dq[:].rearrange("p (a c) -> p a c", c=64),
                        rep_inner(ssb[:, hh * 64:(hh + 1) * 64], 64, 64), ALU.mult), 1)
                t.wait_ge(vv.h, vv.n)
                for ht in range(64):
                    pi = ht % 2
                    if tps_busy[pi]:
                        t.wait_ge(vv.h, tps_busy[pi])
                    tT.inc(t.transpose(tps[pi][:], dqb[:, ht * 128:(ht + 1) * 128], ident[:]), 1)
                    v.wait_ge(tT.h, tT.n)
                    vv.inc(v.tensor_copy(stg[:, ht * 128:(ht + 1) * 128], tps[pi][:]), 1)
                    tps_busy[pi] = vv.n
                    if ab_busy[pi]:
                        t.wait_ge(vv.h, ab_busy[pi])
                    tT.inc(t.matmul(abp[pi][:], adb[:, ht * 128:(ht + 1) * 128],
                                    bdb[:, dt * 128:(dt + 1) * 128], start=True, stop=True), 1)
                    v.wait_ge(tT.h, tT.n)
                    v.wait_ge(vv.h, vv.n)
                    vv.inc(v.tensor_tensor(stg[:, ht * 128:(ht + 1) * 128], abp[pi][:],
                                           stg[:, ht * 128:(ht + 1) * 128], ALU.add), 1)
                    ab_busy[pi] = vv.n
                dqb_busy = tT.n
                g.wait_ge(vv.h, vv.n)
                st.inc(g.dma_start(
                    out=io["weffd"].rearrange("(kt p) c -> p kt c", p=128)[:, :, dt * 128:(dt + 1) * 128],
                    in_=stg[:].rearrange("p (kt c) -> p kt c", c=128)), 16)
                stg_st = st.n

        with ExitStack() as sB:
            EB = sB.enter_context
            x3ch = EB(nc.sbuf_tensor("p5_x3ch", [128, DK * S_TOK], BF16))
            xt = EB(nc.sbuf_tensor("p5_xt", [128, D], BF16))
            bupg = EB(nc.sbuf_tensor("p5_bupg", [128, 64], F32))
            wut = [EB(nc.sbuf_tensor(f"p5_wut{i}", [128, 16 * 128], BF16)) for i in range(2)]
            ev = [EB(nc.sbuf_tensor(f"p5_ev{i}", [128, 512], BF16)) for i in range(2)]
            tpsB = [EB(nc.psum_tensor(f"p5_tpB{i}", [128, 128], BF16)) for i in range(2)]
            mpsB = [EB(nc.psum_tensor(f"p5_mpB{i}", [128, 512], F32)) for i in range(4)]
            t.wait_ge(vv.h, vv.n)
            for icol in range(64):
                ld.inc(g.dma_start(out=bupg[:, icol:icol + 1],
                                   in_=io["bup"][0:1, icol * 128:(icol + 1) * 128]
                                   .rearrange("o c -> c o")), 16)
            tps_busy = [0, 0]
            mps_busy = [0] * 4
            ev_busy = [0, 0]
            xt_busy = 0
            for ti in range(NT):
                if xt_busy:
                    g.wait_ge(tT.h, xt_busy)
                ld.inc(g.dma_start(out=xt[:], in_=io["x3"][ti * 128:(ti + 1) * 128, :]), 16)
                t.wait_ge(ld.h, ld.n)
                for kt in range(DK):
                    pi = kt % 2
                    if tps_busy[pi]:
                        t.wait_ge(vv.h, tps_busy[pi])
                    tT.inc(t.transpose(tpsB[pi][:], xt[:, kt * 128:(kt + 1) * 128], ident[:]), 1)
                    v.wait_ge(tT.h, tT.n)
                    vv.inc(v.tensor_copy(x3ch[:, kt * S_TOK + ti * 128: kt * S_TOK + (ti + 1) * 128],
                                         tpsB[pi][:]), 1)
                    tps_busy[pi] = vv.n
                xt_busy = tT.n
            t.wait_ge(vv.h, vv.n)
            wu_busy = [0, 0]
            for ht in range(64):
                par = ht % 2
                if wu_busy[par]:
                    g.wait_ge(tT.h, wu_busy[par])
                ld.inc(g.dma_start(out=wut[par][:].rearrange("p (k c) -> p k c", c=128),
                                   in_=io["weffu"].rearrange("(k p) c -> p k c", p=128)
                                   [:, :, ht * 128:(ht + 1) * 128]), 16)
                t.wait_ge(ld.h, ld.n)
                for tc in range(8):
                    pi = tc % 4
                    if mps_busy[pi]:
                        t.wait_ge(ss.h, mps_busy[pi])
                    ins = None
                    for kt in range(DK):
                        ins = t.matmul(mpsB[pi][:], wut[par][:, kt * 128:(kt + 1) * 128],
                                       x3ch[:, kt * S_TOK + tc * 512:(kt) * S_TOK + (tc + 1) * 512],
                                       start=(kt == 0), stop=(kt == DK - 1))
                    tT.inc(ins, 1)
                    s.wait_ge(tT.h, tT.n)
                    pe = tc % 2
                    if ev_busy[pe]:
                        s.wait_ge(st.h, ev_busy[pe])
                    ss.inc(s.activation(ev[pe][:], mpsB[pi][:], AF.Relu, bias=bupg[:, ht:ht + 1]), 1)
                    mps_busy[pi] = ss.n
                    g.wait_ge(ss.h, ss.n)
                    st.inc(g.dma_start(out=io["x4dram"][ht * 128:(ht + 1) * 128, tc * 512:(tc + 1) * 512],
                                       in_=ev[pe][:]), 16)
                    ev_busy[pe] = st.n
                wu_busy[par] = tT.n

        with ExitStack() as sC:
            EC = sC.enter_context
            wdc = EC(nc.sbuf_tensor("p5_wdc", [128, 64 * 512], BF16))
            x4s = [EC(nc.sbuf_tensor(f"p5_x4s{i}", [128, 64 * 128], BF16)) for i in range(2)]
            bdr = EC(nc.sbuf_tensor("p5_bdr", [1, D], BF16))
            x3s = EC(nc.sbuf_tensor("p5_x3s", [128, 512], F32))
            evc = [EC(nc.sbuf_tensor(f"p5_evc{i}", [128, 512], F32)) for i in range(2)]
            mpsC = [EC(nc.psum_tensor(f"p5_mpC{i}", [128, 512], F32)) for i in range(4)]
            t.wait_ge(vv.h, vv.n)
            v.wait_ge(ld.h, ld.n)
            ld.inc(g.dma_start(out=bdr[:], in_=io["bdn"]), 16)
            v.wait_ge(ld.h, ld.n)
            mps_busy = [0] * 4
            x4_busy = [0, 0]
            ev_busy = [0, 0]
            x3_busy = 0
            for oc in range(4):
                ld.inc(g.dma_start(out=wdc[:].rearrange("p (k c) -> p k c", c=512),
                                   in_=io["weffd"].rearrange("(k p) c -> p k c", p=128)
                                   [:, :, oc * 512:(oc + 1) * 512]), 16)
                t.wait_ge(ld.h, ld.n)
                for ti in range(NT):
                    par = ti % 2
                    if x4_busy[par]:
                        g.wait_ge(tT.h, x4_busy[par])
                    ld.inc(g.dma_start(out=x4s[par][:].rearrange("p (k c) -> p k c", c=128),
                                       in_=io["x4dram"].rearrange("(k p) c -> p k c", p=128)
                                       [:, :, ti * 128:(ti + 1) * 128]), 16)
                    if x3_busy:
                        g.wait_ge(vv.h, x3_busy)
                    ld.inc(g.dma_start(out=x3s[:], in_=io["x3"][ti * 128:(ti + 1) * 128,
                                                                oc * 512:(oc + 1) * 512]), 16)
                    t.wait_ge(ld.h, ld.n)
                    pi = ti % 4
                    if mps_busy[pi]:
                        t.wait_ge(vv.h, mps_busy[pi])
                    for kt in range(64):
                        t.matmul(mpsC[pi][:], x4s[par][:, kt * 128:(kt + 1) * 128],
                                 wdc[:, kt * 512:(kt + 1) * 512],
                                 start=(kt == 0), stop=False, skip_group_check=True)
                    ins = t.matmul(mpsC[pi][:], consts["ones_row_bf"][:],
                                   bdr[:, oc * 512:(oc + 1) * 512],
                                   start=False, stop=True, skip_group_check=True)
                    tT.inc(ins, 1)
                    x4_busy[par] = tT.n
                    v.wait_ge(tT.h, tT.n)
                    pe = ti % 2
                    if ev_busy[pe]:
                        v.wait_ge(st.h, ev_busy[pe])
                    vv.inc(v.tensor_tensor(evc[pe][:], mpsC[pi][:], x3s[:], ALU.add), 1)
                    mps_busy[pi] = vv.n
                    x3_busy = vv.n
                    g.wait_ge(vv.h, vv.n)
                    st.inc(g.dma_start(out=io["x5pre"][ti * 128:(ti + 1) * 128,
                                                       oc * 512:(oc + 1) * 512],
                                       in_=evc[pe][:]), 16)
                    ev_busy[pe] = st.n
            g.wait_ge(st.h, st.n)
            xpr = EC(nc.sbuf_tensor("p5_xpr", [128, D], F32))
            xo = EC(nc.sbuf_tensor("p5_xo", [128, D], F32))
            cols = EC(nc.sbuf_tensor("p5_cols", [128, 8], F32))
            xo_busy = 0
            xp_busy = 0
            for ti in range(NT):
                if xp_busy:
                    g.wait_ge(vv.h, xp_busy)
                ld.inc(g.dma_start(out=xpr[:], in_=io["x5pre"][ti * 128:(ti + 1) * 128, :]), 16)
                v.wait_ge(ld.h, ld.n)
                if xo_busy:
                    v.wait_ge(st.h, xo_busy)
                vv.inc(v.tensor_reduce(cols[:, 0:1], xpr[:], mybir.AxisListType.X, ALU.add), 1)
                v.wait_ge(vv.h, vv.n)
                v.scalar_tensor_tensor(xo[:], cols[:, 0:1].broadcast_to([128, D]), -1.0 / D,
                                       xpr[:], ALU.mult, ALU.add)
                v.tensor_tensor(xpr[:], xo[:], xo[:], ALU.mult)
                vv.inc(v.tensor_reduce(cols[:, 2:3], xpr[:], mybir.AxisListType.X, ALU.add), 1)
                xp_busy = vv.n
                s.wait_ge(vv.h, vv.n)
                ss.inc(emit_rstd(s, consts, cols[:, 2:3], cols[:, 3:4], 1.0 / D, cols[:, 4:5]), 1)
                v.wait_ge(ss.h, ss.n)
                vv.inc(v.tensor_scalar(xo[:], xo[:], cols[:, 3:4], None, ALU.mult), 1)
                g.wait_ge(vv.h, vv.n)
                st.inc(g.dma_start(out=io["out"][ti * 128:(ti + 1) * 128, :], in_=xo[:]), 16)
                xo_busy = st.n
        g.wait_ge(st.h, st.n)
        sy.wait_ge(st.h, st.n)
        return {"vv": (vv, vv.n), "ss": (ss, ss.n), "tT": (tT, tT.n), "st": (st, st.n)}


def build_p5():
    nc = bass.Bass("TRN2", target_bir_lowering=False)
    io = {}
    io["x3"] = nc.dram_tensor("x3", [S_TOK, D], F32, kind="ExternalInput").ap()
    io["wupq"] = nc.dram_tensor("wupq", [INTER, D], I32, kind="ExternalInput").ap()
    io["wups"] = nc.dram_tensor("wups", [INTER, 32], F32, kind="ExternalInput").ap()
    io["bup"] = nc.dram_tensor("bup", [1, INTER], F32, kind="ExternalInput").ap()
    io["alup"] = nc.dram_tensor("alup", [D, 16], F32, kind="ExternalInput").ap()
    io["blup"] = nc.dram_tensor("blup", [16, INTER], F32, kind="ExternalInput").ap()
    io["wdnq"] = nc.dram_tensor("wdnq", [D, INTER], I32, kind="ExternalInput").ap()
    io["wdns"] = nc.dram_tensor("wdns", [D, 128], F32, kind="ExternalInput").ap()
    io["bdn"] = nc.dram_tensor("bdn", [1, D], F32, kind="ExternalInput").ap()
    io["aldn"] = nc.dram_tensor("aldn", [INTER, 16], F32, kind="ExternalInput").ap()
    io["bldn"] = nc.dram_tensor("bldn", [16, D], F32, kind="ExternalInput").ap()
    io["out"] = nc.dram_tensor("out", [S_TOK, D], F32, kind="ExternalOutput").ap()
    io["weffu"] = nc.dram_tensor("weffu", [D, INTER], BF16, kind="Internal").ap()
    io["weffd"] = nc.dram_tensor("weffd", [INTER, D], BF16, kind="Internal").ap()
    io["x4dram"] = nc.dram_tensor("x4dram", [INTER, S_TOK], BF16, kind="Internal").ap()
    io["x5pre"] = nc.dram_tensor("x5pre", [S_TOK, D], F32, kind="Internal").ap()
    with ExitStack() as stack:
        ctx = Ctx(nc, stack)
        eng = {"t": nc.tensor, "v": nc.vector, "s": nc.scalar, "g": nc.gpsimd, "sy": nc.sync}
        ld0 = ctx.sem("c_ld")
        consts = load_consts(ctx, nc.gpsimd, ld0, keys=("ident", "ones_row_bf", "csts"))
        for e in (nc.tensor, nc.vector, nc.scalar, nc.sync):
            e.wait_ge(ld0.h, ld0.n)
        phase5(ctx, io, consts, eng)
    return nc


def kernel_multilaunch(**inputs):
    n = 8
    call = host_consts()
    def csel(keys):
        return {("c_" + k): call[k] for k in keys}
    c1 = csel(("ident", "csts"))
    c2 = csel(("ident", "utri_f32", "ones_col_f32", "ones_row_f32", "triu_bf", "trius_bf", "slower_bf", "csts"))
    c3 = csel(("ident", "h128_2048", "h128_8192", "i8h16", "i2h64", "csts"))
    c4 = csel(("ident", "ones_col_bf", "ones_row_bf", "slower_bf", "qsmask", "csts"))
    c5 = csel(("ident", "ones_row_bf", "csts"))
    u_full = np.ascontiguousarray(inputs["L0_attn_u"])
    cnt = (4095.0 - np.arange(S_TOK, dtype=np.float32)).reshape(1, S_TOK)

    nc1 = build_p1(None)
    maps = [dict(u=u_full[i], w_in=inputs["L0_attn_W_in"], **c1) for i in range(n)]
    r1 = run_bass_kernel_spmd(nc1, maps, core_ids=list(range(n))).results

    nc2 = build_p2()
    maps = [dict(zx=r1[i]["zx"], dlt=r1[i]["dlt"],
                 alog=inputs["L0_attn_A_log"].reshape(1, 64),
                 dtb=inputs["L0_attn_dt_bias"].reshape(1, 64),
                 dcoef=inputs["L0_attn_D"].reshape(1, 64),
                 bnw=inputs["L0_attn_B_norm_w"].reshape(1, 128),
                 cnw=inputs["L0_attn_C_norm_w"].reshape(1, 128),
                 u=u_full[i], w_out=inputs["L0_attn_W_out"],
                 onw=inputs["L0_attn_out_norm_w"].reshape(1, D_INNER), **c2)
            for i in range(n)]
    r2 = run_bass_kernel_spmd(nc2, maps, core_ids=list(range(n))).results

    nc3 = build_p3()
    maps = [dict(x1=r2[i]["x1"],
                 su_ug=inputs["L0_ffn_SU_ug"].reshape(1, D),
                 su_dp=inputs["L0_ffn_SU_dp"].reshape(1, INTER),
                 wsc_ug=inputs["L0_ffn_Wscale_ug"].reshape(1, 2 * INTER),
                 wsc_dp=inputs["L0_ffn_Wscale_dp"].reshape(1, D),
                 w_up=inputs["L0_ffn_W_up"], w_gate=inputs["L0_ffn_W_gate"],
                 w_down=inputs["L0_ffn_W_down"], **c3) for i in range(n)]
    r3 = run_bass_kernel_spmd(nc3, maps, core_ids=list(range(n))).results

    nc4 = build_p4()
    maps = [dict(x2=r3[i]["x2"], wq=inputs["L1_attn_Wq"], wk=inputs["L1_attn_Wk"],
                 wv=inputs["L1_attn_Wv"], wo=inputs["L1_attn_Wo"], cntrow=cnt, **c4)
            for i in range(n)]
    r4 = run_bass_kernel_spmd(nc4, maps, core_ids=list(range(n))).results

    nc5 = build_p5()
    maps = [dict(x3=r4[i]["x3"],
                 wupq=inputs["L1_ffn_w_up_q"], wups=inputs["L1_ffn_w_up_scale"],
                 bup=inputs["L1_ffn_b_up"].reshape(1, INTER),
                 alup=inputs["L1_ffn_w_up_lora_a"], blup=inputs["L1_ffn_w_up_lora_b"],
                 wdnq=inputs["L1_ffn_w_down_q"], wdns=inputs["L1_ffn_w_down_scale"],
                 bdn=inputs["L1_ffn_b_down"].reshape(1, D),
                 aldn=inputs["L1_ffn_w_down_lora_a"], bldn=inputs["L1_ffn_w_down_lora_b"],
                 **c5) for i in range(n)]
    r5 = run_bass_kernel_spmd(nc5, maps, core_ids=list(range(n))).results

    return np.stack([r5[i]["out"] for i in range(n)], axis=0)


P1_KEYS = ("ident", "csts")
P2_KEYS = ("ident", "utri_f32", "ones_col_f32", "ones_row_f32", "triu_bf", "trius_bf",
           "slower_bf", "csts")
P3_KEYS = ("ident", "h128_2048", "h128_8192", "i8h16", "i2h64", "csts")
P4_KEYS = ("ident", "ones_col_bf", "ones_row_bf", "slower_bf", "qsmask", "csts")
P5_KEYS = ("ident", "ones_row_bf", "csts")


def build_fused():
    nc = bass.Bass("TRN2", target_bir_lowering=False)
    io = {}
    De = lambda name, shape, dt, kind: nc.dram_tensor(name, shape, dt, kind=kind).ap()
    io["u"] = De("u", [S_TOK, D], F32, "ExternalInput")
    io["w_in"] = De("w_in", [D_IN, D], F32, "ExternalInput")
    io["alog"] = De("alog", [1, 64], F32, "ExternalInput")
    io["dtb"] = De("dtb", [1, 64], F32, "ExternalInput")
    io["dcoef"] = De("dcoef", [1, 64], F32, "ExternalInput")
    io["bnw"] = De("bnw", [1, 128], F32, "ExternalInput")
    io["cnw"] = De("cnw", [1, 128], F32, "ExternalInput")
    io["w_out"] = De("w_out", [D, D_INNER], F32, "ExternalInput")
    io["onw"] = De("onw", [1, D_INNER], F32, "ExternalInput")
    io["su_ug"] = De("su_ug", [1, D], F32, "ExternalInput")
    io["su_dp"] = De("su_dp", [1, INTER], F32, "ExternalInput")
    io["wsc_ug"] = De("wsc_ug", [1, 2 * INTER], F32, "ExternalInput")
    io["wsc_dp"] = De("wsc_dp", [1, D], F32, "ExternalInput")
    io["w_up"] = De("w_up", [INTER, D], F32, "ExternalInput")
    io["w_gate"] = De("w_gate", [INTER, D], F32, "ExternalInput")
    io["w_down"] = De("w_down", [D, INTER], F32, "ExternalInput")
    io["wq"] = De("wq", [D, D], F32, "ExternalInput")
    io["wk"] = De("wk", [D, D], F32, "ExternalInput")
    io["wv"] = De("wv", [D, D], F32, "ExternalInput")
    io["wo"] = De("wo", [D, D], F32, "ExternalInput")
    io["cntrow"] = De("cntrow", [1, S_TOK], F32, "ExternalInput")
    io["wupq"] = De("wupq", [INTER, D], I32, "ExternalInput")
    io["wups"] = De("wups", [INTER, 32], F32, "ExternalInput")
    io["bup"] = De("bup", [1, INTER], F32, "ExternalInput")
    io["alup"] = De("alup", [D, 16], F32, "ExternalInput")
    io["blup"] = De("blup", [16, INTER], F32, "ExternalInput")
    io["wdnq"] = De("wdnq", [D, INTER], I32, "ExternalInput")
    io["wdns"] = De("wdns", [D, 128], F32, "ExternalInput")
    io["bdn"] = De("bdn", [1, D], F32, "ExternalInput")
    io["aldn"] = De("aldn", [INTER, 16], F32, "ExternalInput")
    io["bldn"] = De("bldn", [16, D], F32, "ExternalInput")
    io["out"] = De("out", [S_TOK, D], F32, "ExternalOutput")
    io["zx"] = De("zx", [S_TOK, D_IN_PAD], BF16, "Internal")
    io["dlt"] = De("dlt", [S_TOK, 192], F32, "Internal")
    io["dbg"] = De("dbg", [S_TOK, 320], F32, "Internal")
    io["dbg2"] = De("dbg2", [1, 4224], F32, "Internal")
    io["ydbg"] = None
    io["ydram"] = De("ydram", [S_TOK, D_INNER], BF16, "Internal")
    io["yzdram"] = De("yzdram", [S_TOK, D_INNER], BF16, "Internal")
    io["adram"] = De("adram", [S_TOK, 64], F32, "Internal")
    io["ashdram"] = De("ashdram", [S_TOK, 64], F32, "Internal")
    io["x1"] = De("x1", [S_TOK, D], F32, "Internal")
    io["zdram"] = De("zdram", [INTER, S_TOK], BF16, "Internal")
    io["vdown"] = De("vdown", [INTER, D], BF16, "Internal")
    io["z2dram"] = De("z2dram", [D, S_TOK], BF16, "Internal")
    io["wdraft"] = De("wdraft", [D, INTER], BF16, "Internal")
    io["wdraft2"] = De("wdraft2", [INTER, D], BF16, "Internal")
    io["vupF"] = De("vupF", [D, INTER], BF16, "Internal")
    io["vgtF"] = De("vgtF", [D, INTER], BF16, "Internal")
    io["x2"] = De("x2", [S_TOK, D], F32, "Internal")
    io["qT"] = De("qT", [D, S_TOK], BF16, "Internal")
    io["kT"] = De("kT", [D, S_TOK], BF16, "Internal")
    io["vtok"] = De("vtok", [S_TOK, D], BF16, "Internal")
    io["attnT"] = De("attnT", [D, S_TOK], BF16, "Internal")
    io["x3"] = De("x3", [S_TOK, D], F32, "Internal")
    io["weffu"] = De("weffu", [D, INTER], BF16, "Internal")
    io["weffd"] = De("weffd", [INTER, D], BF16, "Internal")
    io["x4dram"] = De("x4dram", [INTER, S_TOK], BF16, "Internal")
    io["x5pre"] = De("x5pre", [S_TOK, D], F32, "Internal")
    cdr = declare_const_drams(nc)
    with ExitStack() as stack:
        ctx = Ctx(nc, stack)
        eng = {"t": nc.tensor, "v": nc.vector, "s": nc.scalar, "g": nc.gpsimd, "sy": nc.sync}

        def consts_for(keys, lsem):
            return load_consts_scoped(stack, nc, nc.gpsimd, lsem, cdr, keys)

        def run_phase(keys, fn, name, entry):
            cs = ExitStack()
            lds = ctx.sem("f_ld_" + name)
            cc = load_consts_scoped(cs, nc, nc.gpsimd, lds, cdr, keys)
            for e in (nc.tensor, nc.vector, nc.scalar, nc.sync):
                e.wait_ge(lds.h, lds.n)
            ex = fn(ctx, io, cc, eng, entry=entry)
            cs.close()
            return ex

        e1 = run_phase(P1_KEYS, phase1, "p1", None)
        cs2 = ExitStack()
        lds2 = ctx.sem("f_ld_p2")
        c2 = load_consts_scoped(cs2, nc, nc.gpsimd, lds2, cdr, P2_KEYS)
        for e in (nc.tensor, nc.vector, nc.scalar, nc.sync):
            e.wait_ge(lds2.h, lds2.n)
        e2 = phase2(ctx, io, c2, eng, entry=e1)
        e2c = phase2c(ctx, io, c2, eng, entry=e2)
        cs2.close()
        e3 = run_phase(P3_KEYS, phase3, "p3", e2c)
        e4 = run_phase(P4_KEYS, phase4, "p4", e3)
        run_phase(P5_KEYS, phase5, "p5", e4)
    return nc


def kernel(**inputs):
    n = 8
    call = host_consts()
    consts = {("c_" + k): v for k, v in call.items()}
    u_full = np.ascontiguousarray(inputs["L0_attn_u"])
    cnt = (4095.0 - np.arange(S_TOK, dtype=np.float32)).reshape(1, S_TOK)
    nc = build_fused()
    maps = [dict(
        u=u_full[i], w_in=inputs["L0_attn_W_in"],
        alog=inputs["L0_attn_A_log"].reshape(1, 64),
        dtb=inputs["L0_attn_dt_bias"].reshape(1, 64),
        dcoef=inputs["L0_attn_D"].reshape(1, 64),
        bnw=inputs["L0_attn_B_norm_w"].reshape(1, 128),
        cnw=inputs["L0_attn_C_norm_w"].reshape(1, 128),
        w_out=inputs["L0_attn_W_out"],
        onw=inputs["L0_attn_out_norm_w"].reshape(1, D_INNER),
        su_ug=inputs["L0_ffn_SU_ug"].reshape(1, D),
        su_dp=inputs["L0_ffn_SU_dp"].reshape(1, INTER),
        wsc_ug=inputs["L0_ffn_Wscale_ug"].reshape(1, 2 * INTER),
        wsc_dp=inputs["L0_ffn_Wscale_dp"].reshape(1, D),
        w_up=inputs["L0_ffn_W_up"], w_gate=inputs["L0_ffn_W_gate"],
        w_down=inputs["L0_ffn_W_down"],
        wq=inputs["L1_attn_Wq"], wk=inputs["L1_attn_Wk"],
        wv=inputs["L1_attn_Wv"], wo=inputs["L1_attn_Wo"], cntrow=cnt,
        wupq=inputs["L1_ffn_w_up_q"], wups=inputs["L1_ffn_w_up_scale"],
        bup=inputs["L1_ffn_b_up"].reshape(1, INTER),
        alup=inputs["L1_ffn_w_up_lora_a"], blup=inputs["L1_ffn_w_up_lora_b"],
        wdnq=inputs["L1_ffn_w_down_q"], wdns=inputs["L1_ffn_w_down_scale"],
        bdn=inputs["L1_ffn_b_down"].reshape(1, D),
        aldn=inputs["L1_ffn_w_down_lora_a"], bldn=inputs["L1_ffn_w_down_lora_b"],
        **consts) for i in range(n)]
    res = run_bass_kernel_spmd(nc, maps, core_ids=list(range(n)))
    return np.stack([res.results[i]["out"] for i in range(n)], axis=0)
